# version 15
# baseline (speedup 1.0000x reference)
import contextlib
import numpy as np
import ml_dtypes
import concourse.bass as bass
import concourse.mybir as mybir
from concourse.bass_utils import run_bass_kernel_spmd

F32 = mybir.dt.float32
BF16 = mybir.dt.bfloat16
I8 = mybir.dt.int8
AF = mybir.ActivationFunctionType
OP = mybir.AluOpType
AX = mybir.AxisListType
BF = ml_dtypes.bfloat16

D = 2048
DI = 4096
NST = 16
T = 1024
TH = 1028
NCORE = 8
DC = 16
DIC = 32
HC = 64
H = 8192
EPS_LN = 1e-5
EPS_RMS = 1e-8

ENGINES = ("te", "v", "s", "gp", "sy")
ENG_ATTR = {"te": "tensor", "v": "vector", "s": "scalar", "gp": "gpsimd", "sy": "sync"}
L1V = ["ln1w", "ln1b", "ln2w", "ln2b", "tmu", "tmw", "mk", "mv", "mr", "cmk", "cmr", "bo"]


class Emit:
    def __init__(self, nc, ctx):
        self.nc = nc
        self.ops = {e: [] for e in ENGINES}
        self.sems = {e: ctx.enter_context(nc.semaphore("sem_" + e)) for e in ENGINES}
        self.cnt = {e: 0 for e in ENGINES}
        self.lastw = {e: {o: 0 for o in ENGINES} for e in ENGINES}

    def op(self, eng, fn, dma=False):
        n = 16 if dma else 1
        self.cnt[eng] += n
        tok = (eng, self.cnt[eng])
        sem = self.sems[eng]

        def wrapped(e, fn=fn, sem=sem, n=n):
            fn(e).then_inc(sem, n)
        self.ops[eng].append(wrapped)
        if eng == "v" and not dma:
            self.wait(eng, tok)
        return tok

    def dma(self, eng, out, in_, slow=False):
        kw = {"allow_slow_non_contiguous": True} if slow else {}
        tok = self.op(eng, lambda e, o=out, i=in_, kw=kw: e.dma_start(
            out=o, in_=i, **kw), dma=True)
        self.wait(eng, tok)
        return tok

    def wait(self, eng, tok):
        if tok is None:
            return
        s, v = tok
        if self.lastw[eng][s] >= v:
            return
        self.lastw[eng][s] = v
        sem = self.sems[s]
        self.ops[eng].append(lambda e, sem=sem, v=v: e.wait_ge(sem, v))

    def waits(self, eng, toks):
        best = {}
        for t in toks:
            if t is None:
                continue
            s, v = t
            best[s] = max(best.get(s, 0), v)
        for s, v in best.items():
            self.wait(eng, (s, v))

    def barrier(self):
        snap = dict(self.cnt)
        for e in ENGINES:
            for o in ENGINES:
                if o != e and snap[o] > 0:
                    self.wait(e, (o, snap[o]))

    def flush(self):
        with self.nc.Block() as block:
            for eng in ENGINES:
                ops = self.ops[eng]
                if not ops:
                    continue

                def run(e, ops=tuple(ops)):
                    for f in ops:
                        f(e)
                getattr(block, ENG_ATTR[eng])(run)
            self.ops = {e: [] for e in ENGINES}


def build_nc(upto=99, dbg=False):
    nc = bass.Bass("TRN2", target_bir_lowering=False, debug=False)
    dtr = nc.dram_tensor

    def inp(name, shape, dtype=F32):
        return dtr(name, shape, dtype, kind="ExternalInput").ap()

    x0c = inp("x0c", [128, DC, TH])
    win_t = inp("win_t", [73, 16, 128, 128])
    wdt_t = inp("wdt_t", [DIC, 128, 128])
    alog = inp("alog", [128, DIC, NST])
    bdt = inp("bdt", [128, DIC])
    convw = inp("convw", [128, DIC, 4])
    convb = inp("convb", [128, DIC])
    dpar = inp("dpar", [128, DIC])
    wout_t = inp("wout_t", [DC, DIC, 128, 128])
    wgu_t = inp("wgu_t", [HC, 2, DC, 128, 128], I8)
    wdn_t = inp("wdn_t", [HC, DC, 128, 128], I8)
    sgate = inp("sgate", [128, HC])
    sup = inp("sup", [128, HC])
    sdown = inp("sdown", [128, DC])
    l1v = inp("l1v", [128, DC, 12])
    wkvr_t = inp("wkvr_t", [DC, 3, DC, 128, 128])
    wo_t = inp("wo_t", [DC, DC, 128, 128])
    wcm_t = inp("wcm_t", [DC, 3, DC, 128, 128])
    w12_t = inp("w12_t", [HC, 2, DC, 128, 128])
    w3_t = inp("w3_t", [HC, DC, 128, 128])
    g1i = inp("g1i", [128, DC])
    g2i = inp("g2i", [128, DC])
    g3i = inp("g3i", [128, HC])
    ident = inp("ident", [128, 128], BF16)
    sel16 = inp("sel16", [16, 128], BF16)
    oh16 = inp("oh16", [16, NST, 128], BF16)
    selc = inp("selc", [128, NST, 128], BF16)
    onesc_b = inp("onesc_b", [128, 1], BF16)
    onesc_f = inp("onesc_f", [128, 1])
    ones1_b = inp("ones1_b", [1, 128])
    selp = inp("selp", [128, 3, 8])
    empt = inp("empt", [128, 3])
    selb = inp("selb", [128, 8])
    mhalo = inp("mhalo", [128, 1])

    outT = dtr("outT", [128, DC, T], F32, kind="ExternalOutput").ap()
    kd = {"kind": "ExternalOutput"} if dbg else {}
    dbg_sini = dtr("dbg_sini", [128, 512], F32, **kd).ap() if dbg else None
    dbg_cc1 = dtr("dbg_cc1", [NCORE * 128, 1024], F32, **kd).ap() if dbg else None
    dbg_cc1i = dtr("dbg_cc1i", [128, 1024], F32, **kd).ap() if dbg else None
    dbg_wl = dtr("dbg_wl", [128, DIC, NST], F32, **kd).ap() if dbg else None
    dbg_wg = dtr("dbg_wg", [128, DIC, NST], F32, **kd).ap() if dbg else None

    sp_z = dtr("sp_z", [DIC, 128, T], BF16, **kd).ap()
    sp_u = dtr("sp_u", [DIC, 128, T], BF16, **kd).ap()
    sp_cm = dtr("sp_cm", [DIC, 128, T], BF16, **kd).ap()
    sp_y = dtr("sp_y", [DIC, 128, T], BF16, **kd).ap()
    sp_k = dtr("sp_k", [DC, 128, T], BF16).ap()
    sp_v = dtr("sp_v", [DC, 128, T], BF16).ap()
    sp_r = dtr("sp_r", [DC, 128, T], BF16).ap()
    sp_mr = dtr("sp_mr", [DC, 128, T], BF16).ap()
    sp_x2 = dtr("sp_x2", [DC, 128, T], F32).ap()
    sp_h = dtr("sp_h", [HC, 128, T], BF16).ap()
    sp_s1 = dtr("sp_s1", [HC, 128, T], BF16).ap()
    sp_tm = dtr("sp_tm", [128, 8], F32).ap()
    cc1i = dtr("cc1i", [128, 1024], F32).ap()
    cc1o = dtr("cc1o", [NCORE * 128, 1024], F32).ap()
    cc2i = dtr("cc2i", [128, DC], F32).ap()
    cc2o = dtr("cc2o", [NCORE * 128, DC], F32).ap()
    cc3i = dtr("cc3i", [128, 5 * DC], F32).ap()
    cc3o = dtr("cc3o", [NCORE * 128, 5 * DC], F32).ap()

    octx = contextlib.ExitStack()
    with octx:
        def sbuf(name, shape, dtp=F32, ctx=octx):
            return ctx.enter_context(nc.sbuf_tensor(name, shape, dtp))
        em = Emit(nc, octx)

        x_res = sbuf("x_res", [128, DC, TH])
        c_l1 = sbuf("c_l1", [128, DC, 12])
        c_nw = sbuf("c_nw", [128, DC])
        c_nwT = sbuf("c_nwT", [128, DC])
        c_sg = sbuf("c_sg", [128, HC]); c_su = sbuf("c_su", [128, HC])
        c_sd = sbuf("c_sd", [128, DC])
        c_g1 = sbuf("c_g1", [128, DC]); c_g2 = sbuf("c_g2", [128, DC])
        c_g3 = sbuf("c_g3", [128, HC])
        c_id = sbuf("c_id", [128, 128], BF16)
        c_ocb = sbuf("c_ocb", [128, 1], BF16)
        c_ocf = sbuf("c_ocf", [128, 1])
        c_o1b = sbuf("c_o1b", [1, 128])
        c_selp = sbuf("c_selp", [128, 3, 8])
        c_empt = sbuf("c_empt", [128, 3])
        c_selb = sbuf("c_selb", [128, 8])
        c_mhalo = sbuf("c_mhalo", [128, 1])
        sc = [sbuf(f"sc{i}", [128, T]) for i in range(8)]
        scb = [sbuf(f"scb{i}", [128, T], BF16) for i in range(4)]
        lnr = sbuf("lnr", [1, 512, 5])
        tiny = sbuf("tiny", [128, 64])
        trow = sbuf("trow", [1, 64])
        abuf = sbuf("abuf", [128, DC, 4])

        pb = [octx.enter_context(nc.psum_tensor(f"pb{i}", [128, 512], F32))
              for i in range(7)]
        pbt = octx.enter_context(nc.psum_tensor("pbt", [128, 128], BF16))

        def ts(out, in0, s1, s2, o0, o1=None, eng="v"):
            kw = {}
            if o1 is not None:
                kw["op1"] = o1
            return em.op(eng, lambda e, out=out, in0=in0, s1=s1, s2=s2, o0=o0, kw=kw:
                         e.tensor_scalar(out=out, in0=in0, scalar1=s1, scalar2=s2,
                                         op0=o0, **kw))

        def tt(out, in0, in1, op, eng="v"):
            return em.op(eng, lambda e, out=out, in0=in0, in1=in1, op=op:
                         e.tensor_tensor(out=out, in0=in0, in1=in1, op=op))

        def stt(out, in0, scalar, in1, op0, op1, eng="v"):
            return em.op(eng, lambda e, out=out, in0=in0, scalar=scalar, in1=in1,
                         op0=op0, op1=op1: e.scalar_tensor_tensor(
                             out=out, in0=in0, scalar=scalar, in1=in1, op0=op0,
                             op1=op1))

        def act(out, in_, func, bias=0.0, scale=1.0, accum=None, eng="s"):
            kw = {}
            if accum is not None:
                kw["accum_out"] = accum
            if eng == "s":
                em.wait("s", ("v", em.cnt["v"]))
            tok = em.op(eng, lambda e, out=out, in_=in_, func=func, bias=bias,
                        scale=scale, kw=kw: e.activation(
                            out=out, in_=in_, func=func, bias=bias, scale=scale,
                            **kw))
            if accum is not None:
                em.wait(eng, tok)
            return tok

        def cp(out, in_, eng="v"):
            if eng == "s":
                em.wait("s", ("v", em.cnt["v"]))
                return em.op(eng, lambda e, out=out, in_=in_: e.copy(out=out,
                                                                     in_=in_))
            return em.op(eng, lambda e, out=out, in_=in_: e.tensor_copy(out=out,
                                                                        in_=in_))

        def mm(out, lhsT, rhs, start, stop):
            return em.op("te", lambda e, out=out, l=lhsT, r=rhs, st=start, sp=stop:
                         e.matmul(out=out, lhsT=l, rhs=r, start=st, stop=sp))

        def tpose(out, in_):
            return em.op("te", lambda e, out=out, in_=in_:
                         e.transpose(out, in_, c_id[:]))

        def scan(out, d0, d1, initial):
            tok = em.op("v", lambda e, out=out, d0=d0, d1=d1, init=initial:
                        e.tensor_tensor_scan(out=out, data0=d0, data1=d1,
                                             initial=init, op0=OP.mult,
                                             op1=OP.add))
            em.wait("v", tok)
            return tok

        def red(out, in_, op, absval=False):
            tok = em.op("v", lambda e, out=out, in_=in_, op=op, av=absval:
                        e.tensor_reduce(out=out, in_=in_, axis=AX.X, op=op,
                                        apply_absolute_value=av))
            em.wait("v", tok)
            return tok

        def memset(ap_, val):
            tok = em.op("v", lambda e, ap_=ap_, val=val: e.memset(ap_, val))
            em.wait("v", tok)
            return tok

        def recip(out, in_):
            tok = em.op("v", lambda e, out=out, in_=in_: e.reciprocal(out=out,
                                                                      in_=in_))
            em.wait("v", tok)
            return tok

        pbusy = {}

        def pwait(i):
            em.waits("te", [pbusy.get(i)])

        def collective(in_ap, out_ap):
            def fn(e, i=in_ap, o=out_ap):
                return e.collective_compute(
                    "AllGather", OP.bypass, replica_groups=[list(range(NCORE))],
                    ins=[i.opt()], outs=[o.opt()])
            return em.op("gp", fn)

        def l1c(name):
            return L1V.index(name)

        def lw(name, ch):
            return c_l1[:, ch, l1c(name):l1c(name) + 1]

        def emit_ln(src, n, out_fn, affine=None):
            for th0 in range(0, n, 512):
                tn = min(512, n - th0)
                memset(lnr[0:1, 0:tn, 0], 0.0)
                memset(lnr[0:1, 0:tn, 1], 0.0)
                mmprev = [None, None]
                for ch in range(DC):
                    em.wait("s", mmprev[ch % 2])
                    ctok = cp(scb[ch % 2][:, 0:tn], src(ch)[:, th0:th0 + tn], eng="s")
                    sqtok = act(scb[2 + ch % 2][:, 0:tn], scb[ch % 2][:, 0:tn],
                                AF.Square)
                    em.wait("te", ctok)
                    pwait(4)
                    t1 = mm(pb[4][0:1, 0:tn], c_ocb[:, 0:1], scb[ch % 2][:, 0:tn],
                            True, True)
                    em.wait("te", sqtok)
                    pwait(5)
                    t2 = mm(pb[5][0:1, 0:tn], c_ocb[:, 0:1], scb[2 + ch % 2][:, 0:tn],
                            True, True)
                    em.waits("v", [t1, t2])
                    mmprev[ch % 2] = t2
                    pbusy[4] = tt(lnr[0:1, 0:tn, 0], lnr[0:1, 0:tn, 0],
                                  pb[4][0:1, 0:tn], OP.add)
                    pbusy[5] = tt(lnr[0:1, 0:tn, 1], lnr[0:1, 0:tn, 1],
                                  pb[5][0:1, 0:tn], OP.add)
                ts(lnr[0:1, 0:tn, 0], lnr[0:1, 0:tn, 0], 1.0 / D, None,
                   OP.mult)
                ts(lnr[0:1, 0:tn, 1], lnr[0:1, 0:tn, 1], 1.0 / D, None,
                   OP.mult)
                tt(lnr[0:1, 0:tn, 2], lnr[0:1, 0:tn, 0],
                   lnr[0:1, 0:tn, 0], OP.mult)
                tt(lnr[0:1, 0:tn, 2], lnr[0:1, 0:tn, 1],
                   lnr[0:1, 0:tn, 2], OP.subtract)
                vtok = ts(lnr[0:1, 0:tn, 3], lnr[0:1, 0:tn, 2],
                          float(EPS_LN), None, OP.add)
                em.wait("s", vtok)
                sq = act(lnr[0:1, 0:tn, 3], lnr[0:1, 0:tn, 3],
                         AF.Sqrt)
                em.wait("v", sq)
                rtok = recip(lnr[0:1, 0:tn, 4], lnr[0:1, 0:tn, 3])
                em.waits("te", [rtok])
                bmu = mm(pb[4][:, 0:tn], c_o1b[0:1, :], lnr[0:1, 0:tn, 0],
                         True, True)
                brv = mm(pb[5][:, 0:tn], c_o1b[0:1, :], lnr[0:1, 0:tn, 4],
                         True, True)
                em.waits("v", [bmu, brv])
                for ch in range(DC):
                    tt(sc[7][:, 0:tn], src(ch)[:, th0:th0 + tn], pb[4][:, 0:tn],
                       OP.subtract)
                    tt(sc[7][:, 0:tn], sc[7][:, 0:tn], pb[5][:, 0:tn], OP.mult)
                    if affine is not None:
                        wcol, bcol = affine
                        ts(sc[7][:, 0:tn], sc[7][:, 0:tn], wcol(ch), bcol(ch),
                           OP.mult, OP.add)
                    out_fn(ch, th0, tn)
                em.wait("te", ("v", em.cnt["v"]))

        def emit_tok_ln(src_col, dst_col, wname, bname, slot=0):
            t1 = mm(pb[4][0:1, 0:DC], c_ocf[:, 0:1], src_col, True, True)
            sqt = act(tiny[:, 32:32 + DC], src_col, AF.Square)
            em.wait("te", sqt)
            t2 = mm(pb[5][0:1, 0:DC], c_ocf[:, 0:1], tiny[:, 32:32 + DC], True, True)
            em.waits("v", [t1, t2])
            rd1 = red(trow[0:1, slot + 8:slot + 9],
                      pb[4][0:1, 0:DC].rearrange("x (a c) -> x a c", a=1), OP.add)
            rd2 = red(trow[0:1, slot + 9:slot + 10],
                      pb[5][0:1, 0:DC].rearrange("x (a c) -> x a c", a=1), OP.add)
            em.waits("gp", [rd2])
            hop = cp(trow[0:1, slot:slot + 2], trow[0:1, slot + 8:slot + 10],
                     eng="v")
            em.wait("v", hop)
            ts(trow[0:1, slot:slot + 1], trow[0:1, slot:slot + 1], 1.0 / D, None,
               OP.mult)
            ts(trow[0:1, slot + 1:slot + 2], trow[0:1, slot + 1:slot + 2], 1.0 / D,
               None, OP.mult)
            tt(trow[0:1, slot + 2:slot + 3], trow[0:1, slot:slot + 1],
               trow[0:1, slot:slot + 1], OP.mult)
            tt(trow[0:1, slot + 2:slot + 3], trow[0:1, slot + 1:slot + 2],
               trow[0:1, slot + 2:slot + 3], OP.subtract)
            vt = ts(trow[0:1, slot + 3:slot + 4], trow[0:1, slot + 2:slot + 3],
                    float(EPS_LN), None, OP.add)
            em.wait("s", vt)
            sq2 = act(trow[0:1, slot + 3:slot + 4], trow[0:1, slot + 3:slot + 4],
                      AF.Sqrt)
            em.wait("v", sq2)
            rct = recip(trow[0:1, slot + 4:slot + 5], trow[0:1, slot + 3:slot + 4])
            em.waits("te", [rct])
            bmu = mm(pb[4][:, 0:1], c_o1b[0:1, :], trow[0:1, slot:slot + 1], True,
                     True)
            brv = mm(pb[5][:, 0:1], c_o1b[0:1, :], trow[0:1, slot + 4:slot + 5],
                     True, True)
            em.waits("v", [bmu, brv])
            for ch in range(DC):
                ts(dst_col[:, ch:ch + 1], src_col[:, ch:ch + 1], pb[4][:, 0:1], None,
                   OP.subtract)
                tt(dst_col[:, ch:ch + 1], dst_col[:, ch:ch + 1], pb[5][:, 0:1],
                   OP.mult)
                tt(dst_col[:, ch:ch + 1], dst_col[:, ch:ch + 1], lw(wname, ch),
                   OP.mult)
                tt(dst_col[:, ch:ch + 1], dst_col[:, ch:ch + 1], lw(bname, ch),
                   OP.add)
                ts(dst_col[:, ch:ch + 1], dst_col[:, ch:ch + 1], c_mhalo[:, 0:1],
                   None, OP.mult)
            em.wait("te", ("v", em.cnt["v"]))

        lt = []
        for dst, src_ in ((c_l1, l1v), (c_sg, sgate), (c_su, sup),
                          (c_sd, sdown), (c_g1, g1i), (c_g2, g2i), (c_g3, g3i),
                          (c_ocf, onesc_f), (c_selp, selp), (c_empt, empt),
                          (c_selb, selb), (c_mhalo, mhalo)):
            lt.append(em.dma("sy", dst[:], src_))
        for dst, src_ in ((c_id, ident), (c_ocb, onesc_b)):
            lt.append(em.dma("gp", dst[:], src_))
        lt.append(em.dma("sy", c_o1b[:], ones1_b))
        t_x0 = em.dma("sy", x_res[:], x0c)
        em.waits("v", lt)
        ts(c_nw[:], c_l1[:, :, 5], -1.0, None, OP.mult)
        ts(c_nwT[:], c_l1[:, :, 5], -float(T), None, OP.mult)
        em.barrier()

        with contextlib.ExitStack() as ctx1:
            xp = sbuf("xp", [128, 4, TH], BF16, ctx=ctx1)
            dtlb = sbuf("dtlb", [128, T], BF16, ctx=ctx1)
            bmb = sbuf("bmb", [128, 4, T], BF16, ctx=ctx1)
            c_wdt = sbuf("c_wdt", [128, DIC, 128], BF16, ctx=ctx1)
            wgg = sbuf("wgg", [128, DIC, NST], F32, ctx=ctx1)
            wloc = sbuf("wloc", [128, DIC, NST], F32, ctx=ctx1)
            sini = sbuf("sini", [128, DIC, NST], F32, ctx=ctx1)
            c_a = sbuf("c_a", [128, DIC, NST], ctx=ctx1)
            c_bdt2 = sbuf("c_bdt2", [128, DIC], ctx=ctx1)
            c_convw = sbuf("c_convw", [128, DIC, 4], ctx=ctx1)
            c_convb = sbuf("c_convb", [128, DIC], ctx=ctx1)
            c_dpar = sbuf("c_dpar", [128, DIC], ctx=ctx1)
            c_s16 = sbuf("c_s16", [16, 128], BF16, ctx=ctx1)
            c_oh16 = sbuf("c_oh16", [16, NST, 128], BF16, ctx=ctx1)
            c_selc = sbuf("c_selc", [128, NST, 128], BF16, ctx=ctx1)
            st16b = sbuf("st16b", [16, TH], BF16, ctx=ctx1)
            st16c = sbuf("st16c", [16, T], BF16, ctx=ctx1)
            xrep = sbuf("xrep", [128, TH], BF16, ctx=ctx1)

            lt1 = []
            for dst, src_ in ((c_bdt2, bdt), (c_convw, convw),
                              (c_convb, convb), (c_dpar, dpar)):
                lt1.append(em.dma("sy", dst[:], src_))
            for dst, src_ in ((c_s16, sel16), (c_oh16, oh16), (c_selc, selc)):
                lt1.append(em.dma("gp", dst[:], src_))
            t_a = em.dma("sy", c_a[:], alog)
            em.wait("s", t_a)
            ae = act(c_a[:], c_a[:], AF.Exp)
            em.waits("v", lt1 + [ae])
            ts(c_a[:], c_a[:], -1.0, None, OP.mult)
            ts(c_bdt2[:], c_bdt2[:], 2.0, None, OP.mult)
            em.barrier()

            t_wdt = em.dma("gp", c_wdt[:], wdt_t.rearrange("c p m -> p c m"))

            with contextlib.ExitStack() as ctx1a:
                xln0 = sbuf("xln0", [128, DC, TH], BF16, ctx=ctx1a)
                wbuf = [sbuf(f"wbuf{i}", [128, 16, 128], BF16, ctx=ctx1a)
                        for i in range(2)]

                def out_ln0(ch, th0, tn):
                    cp(xln0[:, ch, th0:th0 + tn], sc[7][:, 0:tn])
                em.wait("s", t_x0)
                em.wait("v", t_x0)
                emit_ln(lambda ch: x_res[:, ch, :], TH, out_ln0)
                em.barrier()

                wpar = [None, None]
                for m in range(73):
                    par = m % 2
                    em.waits("gp", [wpar[par]])
                    twl = em.dma("gp", wbuf[par][:],
                                 win_t[m].rearrange("k p c -> p k c"))
                    if 32 <= m < 36:
                        tts_ = [(0, 512), (512, 512), (1024, 4)]
                    else:
                        tts_ = [(4, 512), (516, 512)]
                    for fi, (t0, tn) in enumerate(tts_):
                        bank = pb[fi % 2]
                        em.waits("te", [twl])
                        pwait(fi % 2)
                        for kc in range(DC):
                            last = mm(bank[:, 0:tn], wbuf[par][:, kc, :],
                                      xln0[:, kc, t0:t0 + tn], kc == 0, kc == DC - 1)
                        rt = t0 - 4
                        if m < 32:
                            em.waits("s", [last, wpar[par]])
                            etok = act(scb[m % 2][:, 0:tn], bank[:, 0:tn], AF.Silu)
                            em.wait("sy", etok)
                            wpar[par] = em.dma("sy", sp_z[m, :, rt:rt + tn],
                                               scb[m % 2][:, 0:tn])
                            pbusy[fi % 2] = etok
                        elif m < 36:
                            em.waits("v", [last])
                            wpar[par] = cp(xp[:, m - 32, t0:t0 + tn], bank[:, 0:tn])
                            pbusy[fi % 2] = wpar[par]
                        elif m < 40:
                            em.waits("v", [last])
                            wpar[par] = cp(bmb[:, m - 36, rt:rt + tn], bank[:, 0:tn])
                            pbusy[fi % 2] = wpar[par]
                        elif m < 72:
                            em.waits("v", [last, wpar[par]])
                            etok = cp(scb[2 + m % 2][:, 0:tn], bank[:, 0:tn])
                            em.wait("sy", etok)
                            wpar[par] = em.dma("sy", sp_cm[m - 40, :, rt:rt + tn],
                                               scb[2 + m % 2][:, 0:tn])
                            pbusy[fi % 2] = etok
                        else:
                            em.waits("v", [last])
                            wpar[par] = cp(dtlb[:, rt:rt + tn], bank[:, 0:tn])
                            pbusy[fi % 2] = wpar[par]
                em.barrier()

            def emit_delta(c, accum=None):
                em.wait("te", t_wdt)
                for t0 in range(0, T, 512):
                    pwait(5)
                    tk = mm(pb[5][:, 0:512], c_wdt[:, c, :], dtlb[:, t0:t0 + 512],
                            True, True)
                    em.wait("s", tk)
                    e1 = act(sc[0][:, t0:t0 + 512], pb[5][:, 0:512], AF.Exp,
                             bias=c_bdt2[:, c:c + 1])
                    pbusy[5] = e1
                return act(sc[0][:], sc[0][:], AF.Ln, bias=1.0, accum=accum)

            def emit_u(c, prev_te):
                em.waits("gp", [prev_te])
                tg = em.dma("gp", st16b[:, 0:TH],
                            xp[(c % 8) * 16:(c % 8) * 16 + 16, c // 8, :])
                xtoks = []
                for fi, (t0, tn) in enumerate(((0, 512), (512, 512), (1024, 4))):
                    em.waits("te", [tg])
                    pwait(4)
                    tk = mm(pb[4][:, 0:tn], c_s16[:], st16b[0:16, t0:t0 + tn],
                            True, True)
                    em.waits("v", [tk])
                    xtoks.append(cp(xrep[:, t0:t0 + tn], pb[4][:, 0:tn]))
                    pbusy[4] = xtoks[-1]
                ts(sc[1][:], xrep[:, 1:1025], c_convw[:, c, 0:1], None, OP.mult)
                for k in range(1, 4):
                    stt(sc[1][:], xrep[:, 1 + k:1025 + k], c_convw[:, c, k:k + 1],
                        sc[1][:], OP.mult, OP.add)
                em.wait("s", ("v", em.cnt["v"]))
                return act(sc[1][:], sc[1][:], AF.Silu, bias=c_convb[:, c:c + 1])

            def bcast_B(n, stok):
                out = []
                for fi, t0 in enumerate((0, 512)):
                    em.waits("te", [stok])
                    pwait(2 + fi)
                    out.append(mm(pb[2 + fi][:, 0:512], c_oh16[:, n, :],
                                  st16c[0:16, t0:t0 + 512], True, True))
                return out

            prev_te = None
            uspill = None
            for c in range(DIC):
                em.barrier()
                dtok = emit_delta(c, accum=tiny[:, 0:1])
                utok = emit_u(c, prev_te)
                em.waits("v", [utok, uspill])
                ub = cp(scb[1][:], sc[1][:])
                em.wait("sy", ub)
                uspill = em.dma("sy", sp_u[c], scb[1][:])
                em.waits("v", [dtok])
                dut = tt(sc[2][:], sc[0][:], sc[1][:], OP.mult)
                gt = tt(wgg[:, c, :], c_a[:, c, :],
                        tiny[:, 0:1].broadcast_to([128, NST]), OP.mult)
                em.wait("s", gt)
                act(wgg[:, c, :], wgg[:, c, :], AF.Exp)
                em.waits("gp", [prev_te])
                bstage = em.dma("gp", st16c[:, 0:T],
                                bmb[(c % 8) * 16:(c % 8) * 16 + 16, c // 8, :])
                for n in range(NST):
                    em.wait("s", dut)
                    datok = act(sc[3][:], sc[0][:], AF.Exp, scale=c_a[:, c, n:n + 1])
                    btoks = bcast_B(n, bstage)
                    em.waits("v", [datok] + btoks)
                    pbusy[2] = tt(sc[4][:, 0:512], sc[2][:, 0:512], pb[2][:, 0:512],
                                  OP.mult)
                    pbusy[3] = tt(sc[4][:, 512:1024], sc[2][:, 512:1024],
                                  pb[3][:, 0:512], OP.mult)
                    scan(sc[5][:], sc[3][:], sc[4][:], 0.0)
                    cp(wloc[:, c, n:n + 1], sc[5][:, T - 1:T])
                    em.barrier()
                prev_te = ("te", em.cnt["te"])
            em.barrier()

            if dbg:
                em.dma("sy", dbg_wl, wloc[:])
                em.dma("sy", dbg_wg, wgg[:])
            pw = em.dma("gp", cc1i[:, 0:512],
                        wloc[:].rearrange("p a b -> p (a b)"))
            pg = em.dma("gp", cc1i[:, 512:1024],
                        wgg[:].rearrange("p a b -> p (a b)"))
            em.waits("gp", [pw, pg])
            if dbg:
                em.waits("gp", [pw, pg])
                em.dma("gp", dbg_cc1i, cc1i)
            cct = collective(cc1i, cc1o)
            if dbg:
                em.waits("gp", [cct])
                em.dma("gp", dbg_cc1, cc1o)
            with contextlib.ExitStack() as ctx1b:
                accW = [sbuf(f"accW{j}", [128, 512], F32, ctx=ctx1b)
                        for j in range(3)]
                accG = [sbuf(f"accG{j}", [128, 512], F32, ctx=ctx1b)
                        for j in range(3)]
                rb1 = [sbuf(f"rb1{i}", [128, 1024], F32, ctx=ctx1b)
                       for i in range(2)]
                for j in range(3):
                    memset(accW[j][:], 0.0)
                    memset(accG[j][:], 0.0)
                em.wait("gp", ("v", em.cnt["v"]))
                rtp = [None, None]
                for cp_ in range(NCORE):
                    em.waits("gp", [cct, rtp[cp_ % 2]])
                    rt = em.dma("gp", rb1[cp_ % 2][:],
                                cc1o[cp_ * 128:(cp_ + 1) * 128, :])
                    em.wait("v", rt)
                    for j in range(3):
                        stt(accW[j][:], rb1[cp_ % 2][:, 0:512],
                            c_selp[:, j, cp_:cp_ + 1], accW[j][:], OP.mult, OP.add)
                        rtp[cp_ % 2] = stt(accG[j][:], rb1[cp_ % 2][:, 512:1024],
                                           c_selp[:, j, cp_:cp_ + 1], accG[j][:],
                                           OP.mult, OP.add)
                sv = sini[:].rearrange("p a b -> p (a b)")
                memset(sv, 0.0)
                for j in range(3):
                    ts(accG[j][:], accG[j][:], c_empt[:, j:j + 1], None, OP.add)
                    tt(sv, sv, accG[j][:], OP.mult)
                    tt(sv, sv, accW[j][:], OP.add)
                if dbg:
                    em.wait("sy", ("v", em.cnt["v"]))
                    em.dma("sy", dbg_sini, sv)
                em.barrier()

            prev_te = None
            yspill = None
            for c in range(DIC):
                em.barrier()
                dtok = emit_delta(c)
                em.wait("sy", ("v", em.cnt["v"]))
                ur = em.dma("sy", scb[1][:], sp_u[c])
                em.wait("v", ur)
                cp(sc[1][:], scb[1][:])
                em.waits("v", [dtok])
                dut = tt(sc[2][:], sc[0][:], sc[1][:], OP.mult)
                em.waits("sy", [prev_te])
                cmr = em.dma("sy", scb[2][:], sp_cm[c])
                em.waits("gp", [prev_te])
                bstage = em.dma("gp", st16c[:, 0:T],
                                bmb[(c % 8) * 16:(c % 8) * 16 + 16, c // 8, :])
                memset(sc[6][:], 0.0)
                for n in range(NST):
                    em.wait("s", dut)
                    datok = act(sc[3][:], sc[0][:], AF.Exp, scale=c_a[:, c, n:n + 1])
                    btoks = bcast_B(n, bstage)
                    em.waits("v", [datok] + btoks)
                    pbusy[2] = tt(sc[4][:, 0:512], sc[2][:, 0:512], pb[2][:, 0:512],
                                  OP.mult)
                    pbusy[3] = tt(sc[4][:, 512:1024], sc[2][:, 512:1024],
                                  pb[3][:, 0:512], OP.mult)
                    scan(sc[5][:], sc[3][:], sc[4][:], sini[:, c, n:n + 1])
                    em.wait("te", cmr)
                    pwait(0)
                    pwait(1)
                    ctoks = [mm(pb[0][:, 0:512], c_selc[:, n, :], scb[2][:, 0:512],
                                True, True),
                             mm(pb[1][:, 0:512], c_selc[:, n, :], scb[2][:, 512:1024],
                                True, True)]
                    em.waits("v", ctoks)
                    pbusy[0] = tt(sc[7][:, 0:512], sc[5][:, 0:512], pb[0][:, 0:512],
                                  OP.mult)
                    pbusy[1] = tt(sc[7][:, 512:1024], sc[5][:, 512:1024],
                                  pb[1][:, 0:512], OP.mult)
                    tt(sc[6][:], sc[6][:], sc[7][:], OP.add)
                    em.barrier()
                zr = em.dma("sy", scb[3][:], sp_z[c])
                stt(sc[6][:], sc[1][:], c_dpar[:, c:c + 1], sc[6][:], OP.mult,
                    OP.add)
                em.waits("v", [zr, yspill])
                yb = tt(scb[1][:], sc[6][:], scb[3][:], OP.mult)
                em.wait("sy", yb)
                yspill = em.dma("sy", sp_y[c], scb[1][:])
                prev_te = ("te", em.cnt["te"])
            em.barrier()

            with contextlib.ExitStack() as ctx1c:
                ygrp = sbuf("ygrp", [128, 8, T], BF16, ctx=ctx1c)
                wob = [sbuf(f"wob{i}", [128, 8, 128], BF16, ctx=ctx1c)
                       for i in range(2)]
                for g in range(4):
                    gr = []
                    em.wait("sy", ("te", em.cnt["te"]))
                    for ki in range(8):
                        gr.append(em.dma("sy", ygrp[:, ki, :], sp_y[g * 8 + ki]))
                    wpar = [None, None]
                    for m in range(DC):
                        par = m % 2
                        em.waits("gp", [wpar[par]])
                        twl = em.dma("gp", wob[par][:],
                                     wout_t[m, g * 8:(g + 1) * 8].rearrange(
                                         "k p c -> p k c"))
                        for fi, t0 in enumerate((0, 512)):
                            bank = pb[fi]
                            em.waits("te", [twl] + gr)
                            pwait(fi)
                            for ki in range(8):
                                last = mm(bank[:, 0:512], wob[par][:, ki, :],
                                          ygrp[:, ki, t0:t0 + 512], ki == 0,
                                          ki == 7)
                            em.waits("v", [last])
                            wpar[par] = tt(x_res[:, m, 4 + t0:4 + t0 + 512],
                                           x_res[:, m, 4 + t0:4 + t0 + 512],
                                           bank[:, 0:512], OP.add)
                            pbusy[fi] = wpar[par]
                    em.barrier()

        if upto <= 1:
            em.barrier()
            fin = em.dma("sy", outT, x_res[:, :, 4:TH])
            em.waits("gp", [fin])
            em.flush()
            return nc

        with contextlib.ExitStack() as ctx2:
            xlnb = sbuf("xlnb", [128, DC, T], BF16, ctx=ctx2)
            fuse = [sbuf(f"fuse{i}", [128, T], BF16, ctx=ctx2) for i in range(8)]
            wg8 = [sbuf(f"wg8{i}", [128, 2, DC, 128], BF16, ctx=ctx2)
                   for i in range(2)]
            wd8 = [sbuf(f"wd8{i}", [128, 8, 128], BF16, ctx=ctx2)
                   for i in range(2)]

            def out_lnf(ch, th0, tn):
                cp(xlnb[:, ch, th0:th0 + tn], sc[7][:, 0:tn])
            emit_ln(lambda ch: x_res[:, ch, 4:TH], T, out_lnf)
            em.barrier()

            wpar = [None, None]
            dpar2 = [None, None]
            for g in range(8):
                ftoks = []
                for ki in range(8):
                    hm = g * 8 + ki
                    par = hm % 2
                    em.waits("gp", [wpar[par]])
                    twl = em.dma("gp", wg8[par][:],
                                 wgu_t[hm].rearrange("two k p c -> p two k c"))
                    for fi, t0 in enumerate((0, 512)):
                        em.waits("te", [twl])
                        pwait(fi)
                        for kc in range(DC):
                            gm = mm(pb[fi][:, 0:512], wg8[par][:, 0, kc, :],
                                    xlnb[:, kc, t0:t0 + 512], kc == 0, kc == DC - 1)
                        pwait(2 + fi)
                        for kc in range(DC):
                            um = mm(pb[2 + fi][:, 0:512], wg8[par][:, 1, kc, :],
                                    xlnb[:, kc, t0:t0 + 512], kc == 0, kc == DC - 1)
                        em.waits("s", [gm])
                        gt = act(sc[0][:, t0:t0 + 512], pb[fi][:, 0:512], AF.Silu,
                                 scale=c_sg[:, hm:hm + 1])
                        pbusy[fi] = gt
                        em.waits("v", [um, gt])
                        ft = stt(fuse[ki][:, t0:t0 + 512], pb[2 + fi][:, 0:512],
                                 c_su[:, hm:hm + 1], sc[0][:, t0:t0 + 512],
                                 OP.mult, OP.mult)
                        pbusy[2 + fi] = ft
                        ftoks.append(ft)
                    wpar[par] = ftoks[-1]
                for mp in range(DC):
                    par = mp % 2
                    em.waits("gp", [dpar2[par]])
                    twl = em.dma("gp", wd8[par][:],
                                 wdn_t[g * 8:(g + 1) * 8, mp].rearrange(
                                     "k p c -> p k c"))
                    for fi, t0 in enumerate((0, 512)):
                        bank = pb[4 + fi]
                        em.waits("te", [twl] + ftoks)
                        pwait(4 + fi)
                        for ki in range(8):
                            dm = mm(bank[:, 0:512], wd8[par][:, ki, :],
                                    fuse[ki][:, t0:t0 + 512], ki == 0, ki == 7)
                        em.waits("v", [dm])
                        dpar2[par] = stt(x_res[:, mp, 4 + t0:4 + t0 + 512],
                                         bank[:, 0:512], c_sd[:, mp:mp + 1],
                                         x_res[:, mp, 4 + t0:4 + t0 + 512],
                                         OP.mult, OP.add)
                        pbusy[4 + fi] = dpar2[par]
                em.barrier()
            em.barrier()

        if upto <= 2:
            fin = em.dma("sy", outT, x_res[:, :, 4:TH])
            em.waits("gp", [fin])
            em.flush()
            return nc

        pbnd = em.dma("sy", cc2i, x_res[:, :, TH - 1:TH].rearrange(
            "p c t -> p (c t)"), slow=True)
        em.waits("gp", [pbnd])
        cc2t = collective(cc2i, cc2o)
        memset(abuf[:, :, 0], 0.0)
        em.wait("gp", ("v", em.cnt["v"]))
        hs = None
        for cp_ in range(NCORE):
            em.waits("gp", [cc2t, hs])
            rt = em.dma("gp", abuf[:, :, 1], cc2o[cp_ * 128:(cp_ + 1) * 128, :],
                        slow=True)
            em.wait("v", rt)
            hs = stt(abuf[:, :, 0], abuf[:, :, 1], c_selb[:, cp_:cp_ + 1],
                     abuf[:, :, 0], OP.mult, OP.add)
        em.barrier()

        with contextlib.ExitStack() as ctx3:
            slA = sbuf("slA", [128, DC, T + 1], BF16, ctx=ctx3)
            slB = sbuf("slB", [128, DC, T], BF16, ctx=ctx3)
            wv3 = [sbuf(f"wv3{i}", [128, DC, 128], BF16, ctx=ctx3)
                   for i in range(2)]
            aag = sbuf("aag", [128, 5, DC], F32, ctx=ctx3)

            sx = []
            for ch in range(DC):
                sx.append(em.dma("sy", sp_x2[ch], x_res[:, ch, 4:TH]))
            em.waits("v", sx)
            em.waits("s", sx)

            def out_xb(ch, th0, tn):
                cp(x_res[:, ch, 4 + th0:4 + th0 + tn], sc[7][:, 0:tn])
            emit_ln(lambda ch: x_res[:, ch, 4:TH], T, out_xb)
            em.barrier()

            t1 = mm(pb[4][0:1, 0:DC], c_ocf[:, 0:1], abuf[:, :, 0], True, True)
            sqt = act(abuf[:, :, 1], abuf[:, :, 0], AF.Square)
            em.wait("te", sqt)
            t2 = mm(pb[5][0:1, 0:DC], c_ocf[:, 0:1], abuf[:, :, 1], True, True)
            em.waits("v", [t1, t2])
            rd1 = red(trow[0:1, 30:31],
                      pb[4][0:1, 0:DC].rearrange("x (a c) -> x a c", a=1), OP.add)
            rd2 = red(trow[0:1, 31:32],
                      pb[5][0:1, 0:DC].rearrange("x (a c) -> x a c", a=1), OP.add)
            em.waits("gp", [rd2])
            hop = cp(trow[0:1, 0:2], trow[0:1, 30:32])
            em.wait("v", hop)
            ts(trow[0:1, 0:1], trow[0:1, 0:1], 1.0 / D, None, OP.mult)
            ts(trow[0:1, 1:2], trow[0:1, 1:2], 1.0 / D, None, OP.mult)
            tt(trow[0:1, 2:3], trow[0:1, 0:1], trow[0:1, 0:1], OP.mult)
            tt(trow[0:1, 2:3], trow[0:1, 1:2], trow[0:1, 2:3], OP.subtract)
            vt = ts(trow[0:1, 3:4], trow[0:1, 2:3], float(EPS_LN), None, OP.add)
            em.wait("s", vt)
            sq2 = act(trow[0:1, 3:4], trow[0:1, 3:4], AF.Sqrt)
            em.wait("v", sq2)
            rct = recip(trow[0:1, 4:5], trow[0:1, 3:4])
            em.waits("te", [rct])
            bmu = mm(pb[4][:, 0:1], c_o1b[0:1, :], trow[0:1, 0:1], True, True)
            brv = mm(pb[5][:, 0:1], c_o1b[0:1, :], trow[0:1, 4:5], True, True)
            em.waits("v", [bmu, brv])
            for ch in range(DC):
                ts(abuf[:, ch, 2:3], abuf[:, ch, 0:1], pb[4][:, 0:1], None,
                   OP.subtract)
                tt(abuf[:, ch, 2:3], abuf[:, ch, 2:3], pb[5][:, 0:1], OP.mult)
                tt(abuf[:, ch, 3:4], abuf[:, ch, 2:3], lw("ln1w", ch), OP.mult)
                tt(abuf[:, ch, 3:4], abuf[:, ch, 3:4], lw("ln1b", ch), OP.add)
                ts(abuf[:, ch, 3:4], abuf[:, ch, 3:4], c_mhalo[:, 0:1], None,
                   OP.mult)
            em.wait("te", ("v", em.cnt["v"]))

            def out_ln1(ch, th0, tn):
                cp(slA[:, ch, 1 + th0:1 + th0 + tn], sc[7][:, 0:tn])
            emit_ln(lambda ch: x_res[:, ch, 4:TH], T, out_ln1,
                    affine=(lambda ch: lw("ln1w", ch), lambda ch: lw("ln1b", ch)))
            for ch in range(DC):
                cp(slA[:, ch, 0:1], abuf[:, ch, 3:4])
            em.barrier()

            def emit_mix(src, mname, ch, out_ap):
                tt(sc[0][:], src[:, ch, 1:T + 1], src[:, ch, 0:T], OP.subtract)
                return stt(out_ap, sc[0][:], lw(mname, ch), src[:, ch, 0:T],
                           OP.mult, OP.add)

            def kvr_mm(widx, dst_sp, sigmoid_first):
                wpar = [None, None]
                prev_s = None
                for ch in range(DC):
                    if sigmoid_first:
                        em.waits("v", [prev_s])
                        mt = emit_mix(slA, "mr", ch, scb[0][:])
                        em.wait("s", mt)
                        prev_s = act(slB[:, ch, :], scb[0][:], AF.Sigmoid)
                    else:
                        emit_mix(slA, "mk" if widx == 0 else "mv", ch, slB[:, ch, :])
                em.barrier()
                for m in range(DC):
                    par = m % 2
                    em.waits("gp", [wpar[par]])
                    twl = em.dma("gp", wv3[par][:],
                                 wkvr_t[m, widx].rearrange("k p c -> p k c"))
                    for fi, t0 in enumerate((0, 512)):
                        bank = pb[fi]
                        em.waits("te", [twl])
                        pwait(fi)
                        for kc in range(DC):
                            km = mm(bank[:, 0:512], wv3[par][:, kc, :],
                                    slB[:, kc, t0:t0 + 512], kc == 0, kc == DC - 1)
                        em.waits("v", [km, wpar[par]])
                        et = cp(scb[fi][:, t0:t0 + 512], bank[:, 0:512])
                        pbusy[fi] = et
                        em.wait("sy", et)
                        wpar[par] = em.dma("sy", dst_sp[m, :, t0:t0 + 512],
                                           scb[fi][:, t0:t0 + 512])
                em.barrier()

            kvr_mm(0, sp_k, False)
            kvr_mm(1, sp_v, False)
            kvr_mm(2, sp_r, True)

            def emit_e12(ch):
                em.wait("sy", ("v", em.cnt["v"]))
                em.wait("v", ("s", em.cnt["s"]))
                kr = em.dma("sy", scb[0][:], sp_k[ch])
                vr = em.dma("sy", scb[1][:], sp_v[ch])
                em.wait("v", kr)
                ts(sc[0][:], scb[0][:], lw("tmu", ch), None, OP.add)
                qt = ts(sc[1][:], sc[0][:], lw("tmw", ch), None, OP.max)
                em.wait("s", qt)
                e1t = act(sc[2][:], sc[1][:], AF.Exp, bias=c_nw[:, ch:ch + 1],
                          scale=-1.0)
                em.wait("v", e1t)
                sbt = tt(sc[3][:], sc[0][:], sc[1][:], OP.subtract)
                em.wait("s", sbt)
                e2t = act(sc[3][:], sc[3][:], AF.Exp)
                em.waits("v", [e2t, vr])
                tt(sc[4][:], sc[3][:], scb[1][:], OP.mult)

            for ch in range(DC):
                emit_e12(ch)
                scan(sc[5][:], sc[2][:], sc[4][:], 0.0)
                cp(aag[:, 0, ch:ch + 1], sc[5][:, T - 1:T])
                scan(sc[5][:], sc[2][:], sc[3][:], 0.0)
                cp(aag[:, 1, ch:ch + 1], sc[5][:, T - 1:T])
                act(sc[6][:], sc[1][:], AF.Copy, accum=tiny[:, 0:1])
                act(aag[:, 2, ch:ch + 1], tiny[:, 0:1], AF.Exp, scale=-1.0,
                    bias=c_nwT[:, ch:ch + 1])
                cp(aag[:, 3, ch:ch + 1], x_res[:, ch, TH - 1:TH])
            rlr = em.dma("gp", tiny[:, 32:32 + DC],
                         sp_r[:, :, T - 1:T].rearrange("c p t -> p (c t)"),
                         slow=True)
            em.wait("v", rlr)
            cp(aag[:, 4, :], tiny[:, 32:32 + DC])
            em.barrier()

            pub = em.dma("sy", cc3i, aag[:].rearrange("p f c -> p (f c)"))
            em.waits("gp", [pub])
            cc3t = collective(cc3i, cc3o)
            a_in = abuf[:, :, 0]
            b_in = abuf[:, :, 1]
            ypv = abuf[:, :, 2]
            rpv = abuf[:, :, 3]
            accA = [sc[0][:, 0:DC], sc[0][:, 64:64 + DC], sc[0][:, 128:128 + DC]]
            accB = [sc[1][:, 0:DC], sc[1][:, 64:64 + DC], sc[1][:, 128:128 + DC]]
            accD = [sc[2][:, 0:DC], sc[2][:, 64:64 + DC], sc[2][:, 128:128 + DC]]
            for j in range(3):
                memset(accA[j], 0.0)
                memset(accB[j], 0.0)
                memset(accD[j], 0.0)
            memset(ypv, 0.0)
            memset(rpv, 0.0)
            em.wait("gp", ("v", em.cnt["v"]))
            rtp = None
            for cp_ in range(NCORE):
                em.waits("gp", [cc3t, rtp])
                rt = em.dma("gp", sc[3][:, 0:5 * DC],
                            cc3o[cp_ * 128:(cp_ + 1) * 128, :])
                em.wait("v", rt)
                for j in range(3):
                    stt(accA[j], sc[3][:, 0:DC], c_selp[:, j, cp_:cp_ + 1], accA[j],
                        OP.mult, OP.add)
                    stt(accB[j], sc[3][:, DC:2 * DC], c_selp[:, j, cp_:cp_ + 1],
                        accB[j], OP.mult, OP.add)
                    stt(accD[j], sc[3][:, 2 * DC:3 * DC], c_selp[:, j, cp_:cp_ + 1],
                        accD[j], OP.mult, OP.add)
                stt(ypv, sc[3][:, 3 * DC:4 * DC], c_selb[:, cp_:cp_ + 1], ypv,
                    OP.mult, OP.add)
                rtp = stt(rpv, sc[3][:, 4 * DC:5 * DC], c_selb[:, cp_:cp_ + 1], rpv,
                          OP.mult, OP.add)
            memset(a_in, 0.0)
            memset(b_in, 0.0)
            for j in range(3):
                ts(accD[j], accD[j], c_empt[:, j:j + 1], None, OP.add)
                tt(a_in, a_in, accD[j], OP.mult)
                tt(a_in, a_in, accA[j], OP.add)
                tt(b_in, b_in, accD[j], OP.mult)
                tt(b_in, b_in, accB[j], OP.add)
            em.barrier()

            wkvr = slA
            ts(tiny[:, 30:31], c_mhalo[:, 0:1], -1.0, 1.0, OP.mult, OP.add)
            ts(sc[0][:, 0:DC], b_in, tiny[:, 30:31], None, OP.add)
            recip(sc[0][:, 0:DC], sc[0][:, 0:DC])
            tt(sc[1][:, 0:DC], a_in, sc[0][:, 0:DC], OP.mult)
            tt(sc[1][:, 0:DC], sc[1][:, 0:DC], rpv, OP.mult)
            cp(wkvr[:, :, 0], sc[1][:, 0:DC])
            for ch in range(DC):
                emit_e12(ch)
                rr = em.dma("sy", scb[2][:], sp_r[ch])
                scan(sc[5][:], sc[2][:], sc[4][:], a_in[:, ch:ch + 1])
                scan(sc[6][:], sc[2][:], sc[3][:], b_in[:, ch:ch + 1])
                recip(sc[7][:], sc[6][:])
                tt(sc[5][:], sc[5][:], sc[7][:], OP.mult)
                em.wait("v", rr)
                tt(wkvr[:, ch, 1:T + 1], sc[5][:], scb[2][:], OP.mult)
            em.barrier()

            wpar = [None, None]
            for m in range(DC):
                par = m % 2
                em.waits("gp", [wpar[par]])
                twl = em.dma("gp", wv3[par][:],
                             wo_t[m].rearrange("k p c -> p k c"))
                for fi, (t0, tn) in enumerate(((1, 512), (513, 512), (0, 1))):
                    bi_ = fi % 2 if tn > 1 else 2
                    bank = pb[bi_]
                    em.waits("te", [twl])
                    pwait(bi_)
                    for kc in range(DC):
                        am = mm(bank[:, 0:tn], wv3[par][:, kc, :],
                                wkvr[:, kc, t0:t0 + tn], kc == 0, kc == DC - 1)
                    em.waits("v", [am])
                    if tn > 1:
                        wpar[par] = stt(x_res[:, m, 3 + t0:3 + t0 + tn],
                                        bank[:, 0:tn], lw("bo", m),
                                        x_res[:, m, 3 + t0:3 + t0 + tn],
                                        OP.add, OP.add)
                    else:
                        ts(abuf[:, m, 1:2], bank[:, 0:1], lw("bo", m), None, OP.add)
                        wpar[par] = tt(abuf[:, m, 1:2], abuf[:, m, 1:2],
                                       ypv[:, m:m + 1], OP.add)
                    pbusy[bi_] = wpar[par]
            em.barrier()

            emit_tok_ln(abuf[:, :, 1], abuf[:, :, 2], "ln2w", "ln2b", slot=8)

            ln2x = slA

            def out_ln2(ch, th0, tn):
                cp(ln2x[:, ch, 1 + th0:1 + th0 + tn], sc[7][:, 0:tn])
            emit_ln(lambda ch: x_res[:, ch, 4:TH], T, out_ln2,
                    affine=(lambda ch: lw("ln2w", ch), lambda ch: lw("ln2b", ch)))
            for ch in range(DC):
                cp(ln2x[:, ch, 0:1], abuf[:, ch, 2:3])
            em.barrier()

            for ch in range(DC):
                emit_mix(ln2x, "cmk", ch, slB[:, ch, :])
            prev = None
            for ch in range(DC):
                em.waits("v", [prev])
                mt = emit_mix(ln2x, "cmr", ch, scb[ch % 2][:])
                em.waits("sy", [mt])
                prev = em.dma("sy", sp_mr[ch], scb[ch % 2][:])
            em.barrier()

            k2b = slA
            wpar = [None, None]
            for m in range(DC):
                par = m % 2
                em.waits("gp", [wpar[par]])
                twl = em.dma("gp", wv3[par][:],
                             wcm_t[m, 0].rearrange("k p c -> p k c"))
                for fi, t0 in enumerate((0, 512)):
                    bank = pb[fi]
                    em.waits("te", [twl])
                    pwait(fi)
                    for kc in range(DC):
                        km = mm(bank[:, 0:512], wv3[par][:, kc, :],
                                slB[:, kc, t0:t0 + 512], kc == 0, kc == DC - 1)
                    em.waits("v", [km])
                    em.wait("v", ("s", em.cnt["s"]))
                    rl = ts(sc[0][:, 0:512], bank[:, 0:512], 0.0, None, OP.max)
                    pbusy[fi] = rl
                    em.wait("s", rl)
                    wpar[par] = act(k2b[:, m, t0:t0 + 512], sc[0][:, 0:512],
                                    AF.Square)
            em.barrier()
            rt = None
            for ch in range(DC):
                rt = em.dma("sy", slB[:, ch, :], sp_mr[ch])
            em.wait("v", rt)
            em.wait("te", rt)
            wpar = [None, None]
            for m in range(DC):
                par = m % 2
                em.waits("gp", [wpar[par]])
                twl = em.dma("gp", wv3[par][:],
                             wcm_t[m, 2].rearrange("k p c -> p k c"))
                crt = None
                for fi, t0 in enumerate((0, 512)):
                    em.waits("te", [twl])
                    pwait(2 + fi)
                    for kc in range(DC):
                        rm = mm(pb[2 + fi][:, 0:512], wv3[par][:, kc, :],
                                slB[:, kc, t0:t0 + 512], kc == 0, kc == DC - 1)
                    em.waits("s", [rm])
                    crt = act(sc[2][:, t0:t0 + 512], pb[2 + fi][:, 0:512],
                              AF.Sigmoid)
                    pbusy[2 + fi] = crt
                em.waits("gp", [wpar[par], crt])
                twl2 = em.dma("gp", wv3[par][:],
                              wcm_t[m, 1].rearrange("k p c -> p k c"))
                for fi, t0 in enumerate((0, 512)):
                    em.waits("te", [twl2, crt])
                    pwait(fi)
                    for kc in range(DC):
                        vm = mm(pb[fi][:, 0:512], wv3[par][:, kc, :],
                                k2b[:, kc, t0:t0 + 512], kc == 0, kc == DC - 1)
                    em.waits("v", [vm, crt])
                    pbusy[fi] = tt(sc[1][:, 0:512], sc[2][:, t0:t0 + 512],
                                   pb[fi][:, 0:512], OP.mult)
                    wpar[par] = tt(x_res[:, m, 4 + t0:4 + t0 + 512],
                                   x_res[:, m, 4 + t0:4 + t0 + 512],
                                   sc[1][:, 0:512], OP.add)
            em.barrier()
            prevs = [None, None]
            for ch in range(DC):
                em.waits("sy", [prevs[ch % 2]])
                rt = em.dma("sy", sc[ch % 2][:], sp_x2[ch])
                em.wait("v", rt)
                prevs[ch % 2] = tt(x_res[:, ch, 4:TH], x_res[:, ch, 4:TH],
                                   sc[ch % 2][:], OP.add)
            em.barrier()

        if upto <= 3:
            fin = em.dma("sy", outT, x_res[:, :, 4:TH])
            em.waits("gp", [fin])
            em.flush()
            return nc

        with contextlib.ExitStack() as ctx4:
            rows = sbuf("rows", [1, T, 6], F32, ctx=ctx4)
            qb = sbuf("q1b", [128, DC, T], BF16, ctx=ctx4)
            tmax = sbuf("tmax", [128, 8, 2], F32, ctx=ctx4)
            qi8 = sbuf("qi8", [128, T], I8, ctx=ctx4)

            with contextlib.ExitStack() as ctx4s:
                w12s = [sbuf("w12s0", [128, 2, DC, 128], F32, ctx=ctx4s)]
                memset(tiny[:, 8:11], 0.0)
                spar = [None]
                for hm in range(HC):
                    par = 0
                    tl = em.dma("s", w12s[par][:],
                                w12_t[hm].rearrange("two k p c -> p two k c"))
                    em.wait("s", tl)
                    for half in range(2):
                        for seg in range(2):
                            a1 = act(sc[0][:],
                                     w12s[par][:, half, seg * 8:(seg + 1) * 8]
                                     .rearrange("p k c -> p (k c)"),
                                     AF.Abs, accum=tiny[:, 12:13])
                            em.wait("v", a1)
                            spar[par] = tt(tiny[:, 8 + half:9 + half],
                                           tiny[:, 8 + half:9 + half],
                                           tiny[:, 12:13], OP.add)
                for kc in range(HC):
                    par = 0
                    tl = em.dma("s", w12s[par][:, 0],
                                w3_t[kc].rearrange("k p c -> p k c"))
                    em.wait("s", tl)
                    for seg in range(2):
                        a1 = act(sc[0][:],
                                 w12s[par][:, 0, seg * 8:(seg + 1) * 8]
                                 .rearrange("p k c -> p (k c)"),
                                 AF.Abs, accum=tiny[:, 14:15])
                        em.wait("v", a1)
                        spar[par] = tt(tiny[:, 10:11], tiny[:, 10:11],
                                       tiny[:, 14:15], OP.add)
                em.barrier()
            t1 = mm(pb[4][0:1, 0:3], c_ocf[:, 0:1], tiny[:, 8:11], True, True)
            em.wait("v", t1)
            ts(trow[0:1, 8:11], pb[4][0:1, 0:3], 1.0 / (D * H), None, OP.mult)
            ts(trow[0:1, 11:14], trow[0:1, 8:11], float(EPS_RMS), None, OP.add)
            ivt = recip(trow[0:1, 14:17], trow[0:1, 11:14])
            em.wait("s", ivt)
            act(trow[0:1, 17:20], trow[0:1, 8:11], AF.Ln, bias=1.0)
            nt = ts(trow[0:1, 20:23], trow[0:1, 8:11], -1.0, 1.0, OP.mult, OP.add)
            em.wait("s", nt)
            lt2 = act(trow[0:1, 23:26], trow[0:1, 20:23], AF.Ln)
            em.wait("v", lt2)
            tt(trow[0:1, 26:29], trow[0:1, 17:20], trow[0:1, 23:26], OP.subtract)
            al = ts(trow[0:1, 26:29], trow[0:1, 26:29], 0.5, None, OP.mult)
            em.waits("te", [ivt, al])
            bi_ = mm(pb[4][:, 0:9], c_o1b[0:1, :], trow[0:1, 8:17], True, True)
            em.wait("v", bi_)
            cp(tiny[:, 16:25], pb[4][:, 0:9])
            em.barrier()

            for th0 in range(0, T, 512):
                memset(rows[0:1, th0:th0 + 512, 0], 0.0)
                memset(lnr[0:1, 0:512, 1], 0.0)
                mmprev = [None, None]
                for ch in range(DC):
                    em.wait("s", mmprev[ch % 2])
                    ctok = cp(scb[ch % 2][:, 0:512],
                              x_res[:, ch, 4 + th0:4 + th0 + 512], eng="s")
                    sqtok = act(scb[2 + ch % 2][:, 0:512], scb[ch % 2][:, 0:512],
                                AF.Square)
                    em.wait("te", ctok)
                    pwait(4)
                    t1 = mm(pb[4][0:1, 0:512], c_ocb[:, 0:1], scb[ch % 2][:, 0:512],
                            True, True)
                    em.wait("te", sqtok)
                    pwait(5)
                    t2 = mm(pb[5][0:1, 0:512], c_ocb[:, 0:1],
                            scb[2 + ch % 2][:, 0:512], True, True)
                    em.waits("v", [t1, t2])
                    mmprev[ch % 2] = t2
                    pbusy[4] = tt(rows[0:1, th0:th0 + 512, 0],
                                  rows[0:1, th0:th0 + 512, 0], pb[4][0:1, 0:512],
                                  OP.add)
                    pbusy[5] = tt(lnr[0:1, 0:512, 1], lnr[0:1, 0:512, 1],
                                  pb[5][0:1, 0:512], OP.add)
                ts(rows[0:1, th0:th0 + 512, 0], rows[0:1, th0:th0 + 512, 0],
                   1.0 / D, None, OP.mult)
                ts(lnr[0:1, 0:512, 1], lnr[0:1, 0:512, 1], 1.0 / D, None,
                   OP.mult)
                tt(lnr[0:1, 0:512, 2], rows[0:1, th0:th0 + 512, 0],
                   rows[0:1, th0:th0 + 512, 0], OP.mult)
                tt(lnr[0:1, 0:512, 2], lnr[0:1, 0:512, 1],
                   lnr[0:1, 0:512, 2], OP.subtract)
                vtok = ts(lnr[0:1, 0:512, 3], lnr[0:1, 0:512, 2],
                          float(EPS_LN), None, OP.add)
                em.wait("s", vtok)
                sq = act(lnr[0:1, 0:512, 3], lnr[0:1, 0:512, 3],
                         AF.Sqrt)
                em.wait("v", sq)
                recip(rows[0:1, th0:th0 + 512, 1], lnr[0:1, 0:512, 3])
                tt(rows[0:1, th0:th0 + 512, 2], rows[0:1, th0:th0 + 512, 1],
                   rows[0:1, th0:th0 + 512, 1], OP.mult)
                tt(rows[0:1, th0:th0 + 512, 2], rows[0:1, th0:th0 + 512, 2],
                   lnr[0:1, 0:512, 2], OP.mult)
                vt2 = ts(lnr[0:1, 0:512, 4], rows[0:1, th0:th0 + 512, 2],
                         float(EPS_RMS), None, OP.add)
                em.wait("s", vt2)
                sq3 = act(lnr[0:1, 0:512, 4], lnr[0:1, 0:512, 4],
                          AF.Sqrt)
                em.wait("v", sq3)
                recip(rows[0:1, th0:th0 + 512, 2], lnr[0:1, 0:512, 4])
            em.barrier()

            memset(tmax[:], 0.0)
            rtk = mxk = None
            em.wait("gp", ("v", em.cnt["v"]))
            for th0 in range(0, T, 512):
                pwait(4)
                bmu = mm(pb[4][:, 0:512], c_o1b[0:1, :], rows[0:1, th0:th0 + 512, 0],
                         True, True)
                pwait(5)
                brv = mm(pb[5][:, 0:512], c_o1b[0:1, :], rows[0:1, th0:th0 + 512, 1],
                         True, True)
                em.waits("v", [bmu, brv])
                for ch in range(DC):
                    tt(sc[0][:, 0:512], x_res[:, ch, 4 + th0:4 + th0 + 512],
                       pb[4][:, 0:512], OP.subtract)
                    xh = tt(sc[0][:, 0:512], sc[0][:, 0:512], pb[5][:, 0:512],
                            OP.mult)
                    pbusy[4] = xh
                    pbusy[5] = xh
                    for br in range(2):
                        gcol = c_g1[:, ch:ch + 1] if br == 0 else c_g2[:, ch:ch + 1]
                        gx = ts(scb[br][:, 0:512], sc[0][:, 0:512], gcol, None,
                                OP.mult)
                        for tb in range(4):
                            tj = th0 // 128 + tb
                            em.waits("te", [gx, rtk, mxk])
                            trt = tpose(pbt[:], scb[br][:, tb * 128:(tb + 1) * 128])
                            em.wait("v", trt)
                            em.wait("v", mxk)
                            rtk = red(sc[5][:, 0:1], pbt[:], OP.max, absval=True)
                            em.wait("gp", rtk)
                            mxk = tt(tmax[:, tj, br:br + 1],
                                     tmax[:, tj, br:br + 1],
                                     sc[5][:, 0:1], OP.max)
            em.barrier()
            mt = None
            for br in range(2):
                for tb in range(8):
                    mt = em.dma("gp", rows[0:1, tb * 128:(tb + 1) * 128, 4 + br],
                                tmax[:, tb, br:br + 1], slow=True)
            em.wait("v", mt)
            for br in range(2):
                tt(rows[0:1, 0:T, 4 + br], rows[0:1, 0:T, 4 + br],
                   rows[0:1, 0:T, 2], OP.mult)
                ts(rows[0:1, 0:T, 4 + br], rows[0:1, 0:T, 4 + br], 1e-4, None,
                   OP.max)
            em.barrier()

            def emit_qbuild(br):
                gc = c_g1 if br == 0 else c_g2
                for th0 in range(0, T, 512):
                    pwait(4)
                    bmu = mm(pb[4][:, 0:512], c_o1b[0:1, :],
                             rows[0:1, th0:th0 + 512, 0], True, True)
                    pwait(5)
                    brv = mm(pb[5][:, 0:512], c_o1b[0:1, :],
                             rows[0:1, th0:th0 + 512, 1], True, True)
                    recip(lnr[0:1, 0:512, 0], rows[0:1, th0:th0 + 512, 4 + br])
                    tt(lnr[0:1, 0:512, 0], lnr[0:1, 0:512, 0],
                       rows[0:1, th0:th0 + 512, 2], OP.mult)
                    sf = ts(lnr[0:1, 0:512, 0], lnr[0:1, 0:512, 0],
                            127.0, None, OP.mult)
                    em.waits("te", [sf])
                    pwait(0)
                    bsf = mm(pb[0][:, 0:512], c_o1b[0:1, :],
                             lnr[0:1, 0:512, 0], True, True)
                    em.waits("v", [bmu, brv, bsf])
                    lastq = None
                    for ch in range(DC):
                        tt(sc[0][:, 0:512], x_res[:, ch, 4 + th0:4 + th0 + 512],
                           pb[4][:, 0:512], OP.subtract)
                        tt(sc[0][:, 0:512], sc[0][:, 0:512], pb[5][:, 0:512],
                           OP.mult)
                        ts(sc[0][:, 0:512], sc[0][:, 0:512], gc[:, ch:ch + 1],
                           None, OP.mult)
                        lastq = tt(sc[0][:, 0:512], sc[0][:, 0:512],
                                   pb[0][:, 0:512], OP.mult)
                        cp(qi8[:, 0:512], sc[0][:, 0:512])
                        cp(qb[:, ch, th0:th0 + 512], qi8[:, 0:512])
                    pbusy[0] = lastq
                    pbusy[4] = lastq
                    pbusy[5] = lastq
                em.barrier()
                ts(rows[0:1, 0:T, 4 + br], rows[0:1, 0:T, 4 + br], 1.0 / 127.0,
                   None, OP.mult)
                ts(rows[0:1, 0:T, 4 + br], rows[0:1, 0:T, 4 + br],
                   trow[0:1, 26 + br:27 + br], None, OP.mult)
                em.barrier()

            def emit_w12pass(br):
                with contextlib.ExitStack() as ctxw:
                    w12s = sbuf("w12sh%d" % br, [128, DC, 128], F32, ctx=ctxw)
                    w12b = sbuf("w12bh%d" % br, [128, DC, 128], BF16, ctx=ctxw)
                    dsr = sbuf("dsr%d" % br, [128, T], F32, ctx=ctxw)
                    for fi, t0 in enumerate((0, 512)):
                        pwait(4)
                        dd = mm(pb[4][:, 0:512], c_o1b[0:1, :],
                                rows[0:1, t0:t0 + 512, 4 + br], True, True)
                        em.waits("v", [dd])
                        pbusy[4] = cp(dsr[:, t0:t0 + 512], pb[4][:, 0:512])
                    hprev = None
                    mm_prev = None
                    for hm in range(HC):
                        tl = em.dma("s", w12s[:],
                                    w12_t[hm, br].rearrange("k p c -> p k c"))
                        em.wait("s", tl)
                        em.waits("v", [mm_prev])
                        bt = None
                        for seg in range(2):
                            th_ = act(sc[0][:],
                                      w12s[:, seg * 8:(seg + 1) * 8]
                                      .rearrange("p k c -> p (k c)"), AF.Tanh,
                                      scale=tiny[:, 22 + br:23 + br])
                            em.wait("v", th_)
                            cp(qi8[:], sc[0][:])
                            bt = cp(w12b[:, seg * 8:(seg + 1) * 8]
                                    .rearrange("p k c -> p (k c)"), qi8[:])
                        em.waits("te", [bt])
                        ht = None
                        for fi, t0 in enumerate((0, 512)):
                            pwait(fi)
                            m1 = None
                            for kc in range(DC):
                                m1 = mm(pb[fi][:, 0:512], w12b[:, kc, :],
                                        qb[:, kc, t0:t0 + 512], kc == 0,
                                        kc == DC - 1)
                            em.waits("v", [m1])
                            lb = tt(sc[0][:, 0:512], pb[fi][:, 0:512],
                                    dsr[:, t0:t0 + 512], OP.mult)
                            pbusy[fi] = lb
                            if br == 0:
                                em.wait("s", lb)
                                sw = act(sc[1][:, 0:512], sc[0][:, 0:512],
                                         AF.Silu)
                                em.waits("v", [sw])
                                ht = cp(scb[2][:, t0:t0 + 512], sc[1][:, 0:512])
                            else:
                                rr = em.dma("sy", scb[3][:, t0:t0 + 512],
                                            sp_s1[hm, :, t0:t0 + 512])
                                em.wait("v", rr)
                                ht = tt(scb[2][:, t0:t0 + 512],
                                        sc[0][:, 0:512],
                                        scb[3][:, t0:t0 + 512], OP.mult)
                        mm_prev = ("te", em.cnt["te"])
                        dst = sp_s1 if br == 0 else sp_h
                        em.waits("sy", [hprev, ht])
                        hprev = em.dma("sy", dst[hm], scb[2][:])
                        em.wait("v", hprev)
                    em.barrier()

            emit_qbuild(0)
            emit_w12pass(0)
            emit_qbuild(1)
            emit_w12pass(1)

        with contextlib.ExitStack() as ctx4B:
            rowsb = sbuf("rowsb", [1, T, 3], F32, ctx=ctx4B)
            tmaxb = sbuf("tmaxb", [128, 8, 1], F32, ctx=ctx4B)
            qi8 = sbuf("qi8b", [128, T], I8, ctx=ctx4B)

            with contextlib.ExitStack() as ctx4h:
                hrow = sbuf("hrow2", [1, T, 2], F32, ctx=ctx4h)
                memset(tmaxb[:, :, 0], 0.0)
                memset(hrow[0:1, :, 0], 0.0)
                rtk = mxk = None
                em.wait("gp", ("v", em.cnt["v"]))
                for hm in range(HC):
                    em.wait("sy", ("v", em.cnt["v"]))
                    em.wait("sy", ("s", em.cnt["s"]))
                    hr = em.dma("sy", scb[2][:], sp_h[hm])
                    em.wait("s", hr)
                    for fi, t0 in enumerate((0, 512)):
                        em.wait("s", ("te", em.cnt["te"]))
                        hsq = act(scb[3][:, 0:512], scb[2][:, t0:t0 + 512],
                                  AF.Square)
                        em.wait("te", hsq)
                        pwait(4)
                        s1 = mm(pb[4][0:1, 0:512], c_ocb[:, 0:1], scb[3][:, 0:512],
                                True, True)
                        em.waits("v", [s1])
                        pbusy[4] = tt(hrow[0:1, t0:t0 + 512, 0],
                                      hrow[0:1, t0:t0 + 512, 0], pb[4][0:1, 0:512],
                                      OP.add)
                        gx = ts(scb[3][:, 0:512], scb[2][:, t0:t0 + 512],
                                c_g3[:, hm:hm + 1], None, OP.mult)
                        for tb in range(4):
                            tj = t0 // 128 + tb
                            em.waits("te", [gx, rtk, mxk])
                            trt = tpose(pbt[:], scb[3][:, tb * 128:(tb + 1) * 128])
                            em.wait("v", trt)
                            em.wait("v", mxk)
                            rtk = red(sc[5][:, 0:1], pbt[:], OP.max, absval=True)
                            em.wait("gp", rtk)
                            mxk = tt(tmaxb[:, tj, 0:1], tmaxb[:, tj, 0:1],
                                    sc[5][:, 0:1], OP.max)
                ts(hrow[0:1, :, 0], hrow[0:1, :, 0], 1.0 / H, None, OP.mult)
                vh = ts(hrow[0:1, :, 1], hrow[0:1, :, 0], float(EPS_RMS), None,
                        OP.add)
                em.wait("s", vh)
                sqh = act(hrow[0:1, :, 1], hrow[0:1, :, 1], AF.Sqrt)
                em.wait("v", sqh)
                recip(rowsb[0:1, 0:T, 0], hrow[0:1, :, 1])
                em.waits("gp", [mxk])
                for tb in range(8):
                    mt = em.dma("gp", rowsb[0:1, tb * 128:(tb + 1) * 128, 1],
                                tmaxb[:, tb, 0:1], slow=True)
                em.wait("v", mt)
                tt(rowsb[0:1, 0:T, 1], rowsb[0:1, 0:T, 1], rowsb[0:1, 0:T, 0], OP.mult)
                ts(rowsb[0:1, 0:T, 1], rowsb[0:1, 0:T, 1], 1e-4, None, OP.max)
                recip(rowsb[0:1, 0:T, 2], rowsb[0:1, 0:T, 1])
                tt(rowsb[0:1, 0:T, 2], rowsb[0:1, 0:T, 2], rowsb[0:1, 0:T, 0], OP.mult)
                ts(rowsb[0:1, 0:T, 2], rowsb[0:1, 0:T, 2], 127.0, None, OP.mult)
                ts(rowsb[0:1, 0:T, 1], rowsb[0:1, 0:T, 1], 1.0 / 127.0, None, OP.mult)
                ts(rowsb[0:1, 0:T, 1], rowsb[0:1, 0:T, 1],
                   trow[0:1, 28:29], None, OP.mult)
                em.barrier()

            with contextlib.ExitStack() as ctx4x:
                w3s = [sbuf("w3sl0", [128, DC, 128], F32, ctx=ctx4x)]
                w3g = [sbuf(f"w3g{i}", [128, DC, 128], BF16, ctx=ctx4x)
                       for i in range(4)]
                q3g = [sbuf(f"q3g{i}", [128, T], BF16, ctx=ctx4x) for i in range(4)]
                d3s = sbuf("d3s", [128, T], F32, ctx=ctx4x)
                pwait(2)
                pwait(3)
                pwait(4)
                pwait(5)
                b1 = mm(pb[2][:, 0:512], c_o1b[0:1, :], rowsb[0:1, 0:512, 2], True,
                        True)
                b2 = mm(pb[3][:, 0:512], c_o1b[0:1, :], rowsb[0:1, 512:1024, 2],
                        True, True)
                d1 = mm(pb[4][:, 0:512], c_o1b[0:1, :], rowsb[0:1, 0:512, 1], True,
                        True)
                d2 = mm(pb[5][:, 0:512], c_o1b[0:1, :], rowsb[0:1, 512:1024, 1],
                        True, True)
                em.waits("v", [b1, b2, d1, d2])
                cp(d3s[:, 0:512], pb[4][:, 0:512])
                pbusy[4] = pbusy[5] = cp(d3s[:, 512:1024], pb[5][:, 0:512])
                spar = [None]
                for g in range(16):
                    for ki in range(4):
                        kc = g * 4 + ki
                        par = 0
                        tl = em.dma("s", w3s[par][:],
                                    w3_t[kc].rearrange("k p c -> p k c"))
                        em.wait("s", tl)
                        for seg in range(2):
                            th_ = act(sc[0][:],
                                      w3s[par][:, seg * 8:(seg + 1) * 8]
                                      .rearrange("p k c -> p (k c)"), AF.Tanh,
                                      scale=tiny[:, 24:25])
                            em.wait("v", th_)
                            cp(qi8[:], sc[0][:])
                            spar[par] = cp(w3g[ki][:, seg * 8:(seg + 1) * 8]
                                           .rearrange("p k c -> p (k c)"), qi8[:])
                        em.wait("sy", ("v", em.cnt["v"]))
                        hr = em.dma("sy", scb[2][:], sp_h[kc])
                        em.wait("v", hr)
                        for fi, t0 in enumerate((0, 512)):
                            ts(sc[1][:, 0:512], scb[2][:, t0:t0 + 512],
                               c_g3[:, kc:kc + 1], None, OP.mult)
                            tt(sc[1][:, 0:512], sc[1][:, 0:512],
                               pb[2 + fi][:, 0:512], OP.mult)
                            cp(qi8[:, 0:512], sc[1][:, 0:512])
                            cp(q3g[ki][:, t0:t0 + 512], qi8[:, 0:512])
                    qtok = ("v", em.cnt["v"])
                    for mp in range(DC):
                        for fi, t0 in enumerate((0, 512)):
                            bidx = (mp * 2 + fi) % 2
                            bank = pb[bidx]
                            em.waits("te", [qtok])
                            pwait(bidx)
                            for ki in range(4):
                                wm = mm(bank[:, 0:512], w3g[ki][:, mp, :],
                                        q3g[ki][:, t0:t0 + 512], ki == 0, ki == 3)
                            em.waits("v", [wm])
                            pbusy[bidx] = tt(sc[3][:, 0:512], bank[:, 0:512],
                                             d3s[:, t0:t0 + 512], OP.mult)
                            tt(x_res[:, mp, 4 + t0:4 + t0 + 512],
                               x_res[:, mp, 4 + t0:4 + t0 + 512], sc[3][:, 0:512],
                               OP.add)
                    em.barrier()

        fin = em.dma("sy", outT, x_res[:, :, 4:TH])
        em.waits("gp", [fin])
        em.flush()
    return nc


def _prep_inputs(inp):
    C = np.ascontiguousarray
    d = {}
    W_in = inp["L0_attn_W_in"]
    d["win_t"] = C(W_in.reshape(73, 128, 16, 128).transpose(0, 2, 3, 1))
    d["wdt_t"] = C(inp["L0_attn_W_dt"].reshape(32, 128, 128).transpose(0, 2, 1))
    d["alog"] = C(inp["L0_attn_A_log"].reshape(32, 128, 16).transpose(1, 0, 2))
    d["bdt"] = C(inp["L0_attn_b_dt"].reshape(32, 128).T)
    d["convw"] = C(inp["L0_attn_conv_w"][:, 0, :].reshape(32, 128, 4)
                   .transpose(1, 0, 2))
    d["convb"] = C(inp["L0_attn_conv_b"].reshape(32, 128).T)
    d["dpar"] = C(inp["L0_attn_D_param"].reshape(32, 128).T)
    d["wout_t"] = C(inp["L0_attn_W_out"].reshape(16, 128, 32, 128)
                    .transpose(0, 2, 3, 1))
    wg = inp["L0_ffn_w_gate"].reshape(16, 128, 64, 128).transpose(2, 0, 1, 3)
    wu = inp["L0_ffn_w_up"].reshape(16, 128, 64, 128).transpose(2, 0, 1, 3)
    d["wgu_t"] = C(np.stack([wg, wu], axis=1))
    d["wdn_t"] = C(inp["L0_ffn_w_down"].reshape(64, 128, 16, 128)
                   .transpose(0, 2, 1, 3))
    d["sgate"] = C(inp["L0_ffn_s_gate"].reshape(64, 128).T)
    d["sup"] = C(inp["L0_ffn_s_up"].reshape(64, 128).T)
    d["sdown"] = C(inp["L0_ffn_s_down"].reshape(16, 128).T)
    vecs = [inp["L1_attn_" + n] for n in
            ["ln1_w", "ln1_b", "ln2_w", "ln2_b", "tm_u", "tm_w", "tm_mix_k",
             "tm_mix_v", "tm_mix_r", "cm_mix_k", "cm_mix_r", "tm_bo"]]
    d["l1v"] = C(np.stack(vecs, 0).reshape(12, 16, 128).transpose(2, 1, 0))
    tile2k = lambda w: w.reshape(16, 128, 16, 128).transpose(0, 2, 3, 1)
    d["wkvr_t"] = C(np.stack([tile2k(inp["L1_attn_tm_Wk"]),
                              tile2k(inp["L1_attn_tm_Wv"]),
                              tile2k(inp["L1_attn_tm_Wr"])], axis=1))
    d["wo_t"] = C(tile2k(inp["L1_attn_tm_Wo"]))
    d["wcm_t"] = C(np.stack([tile2k(inp["L1_attn_cm_Wk"]),
                             tile2k(inp["L1_attn_cm_Wv"]),
                             tile2k(inp["L1_attn_cm_Wr"])], axis=1))
    w1 = inp["L1_ffn_w1"].reshape(64, 128, 16, 128).transpose(0, 2, 3, 1)
    w2 = inp["L1_ffn_w2"].reshape(64, 128, 16, 128).transpose(0, 2, 3, 1)
    d["w12_t"] = C(np.stack([w1, w2], axis=1))
    d["w3_t"] = C(inp["L1_ffn_w3"].reshape(16, 128, 64, 128).transpose(2, 0, 3, 1))
    d["g1i"] = C(inp["L1_ffn_g1"].reshape(16, 128).T)
    d["g2i"] = C(inp["L1_ffn_g2"].reshape(16, 128).T)
    d["g3i"] = C(inp["L1_ffn_g3"].reshape(64, 128).T)
    d["ident"] = np.eye(128, dtype=BF)
    s16 = np.zeros((16, 128), dtype=BF)
    for p in range(128):
        s16[p % 16, p] = 1
    d["sel16"] = s16
    o16 = np.zeros((16, 16, 128), dtype=BF)
    for n in range(16):
        o16[n, n, :] = 1
    d["oh16"] = C(o16)
    slc = np.zeros((128, 16, 128), dtype=BF)
    for n in range(16):
        for tgt in range(128):
            slc[(tgt // 16) * 16 + n, n, tgt] = 1
    d["selc"] = slc
    d["onesc_b"] = np.ones((128, 1), dtype=BF)
    d["onesc_f"] = np.ones((128, 1), dtype=np.float32)
    d["ones1_b"] = np.ones((1, 128), dtype=np.float32)
    return d


def kernel(**inp):
    upto = int(inp.pop("_upto", 99))
    dbg = bool(inp.pop("_dbg", False))
    x = np.asarray(inp["L0_attn_hidden_states"], dtype=np.float32)
    shared = _prep_inputs({k: np.asarray(v) for k, v in inp.items()})
    nc = build_nc(upto=upto, dbg=dbg)
    in_maps = []
    for core in range(NCORE):
        b, q = core // 4, core % 4
        m = dict(shared)
        t0 = q * T
        xp_ = np.zeros((TH, D), np.float32)
        lo = max(t0 - 4, 0)
        xp_[4 - (t0 - lo):] = x[b, lo:t0 + T]
        m["x0c"] = np.ascontiguousarray(
            xp_.T.reshape(16, 128, TH).transpose(1, 0, 2))
        sp = np.zeros((128, 3, 8), np.float32)
        ep = np.zeros((128, 3), np.float32)
        for j in range(3):
            if j < q:
                sp[:, j, b * 4 + j] = 1.0
            else:
                ep[:, j] = 1.0
        m["selp"] = sp
        m["empt"] = ep
        sb_ = np.zeros((128, 8), np.float32)
        if q > 0:
            sb_[:, b * 4 + q - 1] = 1.0
        m["selb"] = sb_
        m["mhalo"] = np.full((128, 1), 1.0 if q > 0 else 0.0, np.float32)
        in_maps.append(m)
    res = run_bass_kernel_spmd(nc, in_maps, core_ids=list(range(NCORE)))
    out = np.empty((2, 4096, D), np.float32)
    for core in range(NCORE):
        b, q = core // 4, core % 4
        o = res.results[core]["outT"]
        out[b, q * T:(q + 1) * T] = o.transpose(1, 0, 2).reshape(D, T).T
    return out
    print("saved", out.shape)


# version 16
# speedup vs baseline: 1.0766x; 1.0766x over previous
import contextlib
import numpy as np
import ml_dtypes
import concourse.bass as bass
import concourse.mybir as mybir
from concourse.bass_utils import run_bass_kernel_spmd

F32 = mybir.dt.float32
BF16 = mybir.dt.bfloat16
I8 = mybir.dt.int8
AF = mybir.ActivationFunctionType
OP = mybir.AluOpType
AX = mybir.AxisListType
BF = ml_dtypes.bfloat16

D = 2048
DI = 4096
NST = 16
T = 1024
TH = 1028
NCORE = 8
DC = 16
DIC = 32
HC = 64
H = 8192
EPS_LN = 1e-5
EPS_RMS = 1e-8

ENGINES = ("te", "v", "s", "gp", "sy")
ENG_ATTR = {"te": "tensor", "v": "vector", "s": "scalar", "gp": "gpsimd", "sy": "sync"}
L1V = ["ln1w", "ln1b", "ln2w", "ln2b", "tmu", "tmw", "mk", "mv", "mr", "cmk", "cmr", "bo"]


class Emit:
    def __init__(self, nc, ctx):
        self.nc = nc
        self.ops = {e: [] for e in ENGINES}
        self.sems = {e: ctx.enter_context(nc.semaphore("sem_" + e)) for e in ENGINES}
        self.cnt = {e: 0 for e in ENGINES}
        self.lastw = {e: {o: 0 for o in ENGINES} for e in ENGINES}

    def op(self, eng, fn, dma=False):
        n = 16 if dma else 1
        self.cnt[eng] += n
        tok = (eng, self.cnt[eng])
        sem = self.sems[eng]

        def wrapped(e, fn=fn, sem=sem, n=n):
            fn(e).then_inc(sem, n)
        self.ops[eng].append(wrapped)
        return tok

    def dma(self, eng, out, in_, slow=False):
        kw = {"allow_slow_non_contiguous": True} if slow else {}
        tok = self.op(eng, lambda e, o=out, i=in_, kw=kw: e.dma_start(
            out=o, in_=i, **kw), dma=True)
        self.wait(eng, tok)
        return tok

    def wait(self, eng, tok):
        if tok is None:
            return
        s, v = tok
        if self.lastw[eng][s] >= v:
            return
        self.lastw[eng][s] = v
        sem = self.sems[s]
        self.ops[eng].append(lambda e, sem=sem, v=v: e.wait_ge(sem, v))

    def waits(self, eng, toks):
        best = {}
        for t in toks:
            if t is None:
                continue
            s, v = t
            best[s] = max(best.get(s, 0), v)
        for s, v in best.items():
            self.wait(eng, (s, v))

    def barrier(self):
        snap = dict(self.cnt)
        for e in ENGINES:
            for o in ENGINES:
                if o != e and snap[o] > 0:
                    self.wait(e, (o, snap[o]))

    def flush(self):
        with self.nc.Block() as block:
            for eng in ENGINES:
                ops = self.ops[eng]
                if not ops:
                    continue

                def run(e, ops=tuple(ops)):
                    for f in ops:
                        f(e)
                getattr(block, ENG_ATTR[eng])(run)
            self.ops = {e: [] for e in ENGINES}


def build_nc(upto=99, dbg=False):
    nc = bass.Bass("TRN2", target_bir_lowering=False, debug=False)
    dtr = nc.dram_tensor

    def inp(name, shape, dtype=F32):
        return dtr(name, shape, dtype, kind="ExternalInput").ap()

    x0c = inp("x0c", [128, DC, TH])
    win_t = inp("win_t", [73, 16, 128, 128])
    wdt_t = inp("wdt_t", [DIC, 128, 128])
    alog = inp("alog", [128, DIC, NST])
    bdt = inp("bdt", [128, DIC])
    convw = inp("convw", [128, DIC, 4])
    convb = inp("convb", [128, DIC])
    dpar = inp("dpar", [128, DIC])
    wout_t = inp("wout_t", [DC, DIC, 128, 128])
    wgu_t = inp("wgu_t", [HC, 2, DC, 128, 128], I8)
    wdn_t = inp("wdn_t", [HC, DC, 128, 128], I8)
    sgate = inp("sgate", [128, HC])
    sup = inp("sup", [128, HC])
    sdown = inp("sdown", [128, DC])
    l1v = inp("l1v", [128, DC, 12])
    wkvr_t = inp("wkvr_t", [DC, 3, DC, 128, 128])
    wo_t = inp("wo_t", [DC, DC, 128, 128])
    wcm_t = inp("wcm_t", [DC, 3, DC, 128, 128])
    w12_t = inp("w12_t", [HC, 2, DC, 128, 128])
    w3_t = inp("w3_t", [HC, DC, 128, 128])
    g1i = inp("g1i", [128, DC])
    g2i = inp("g2i", [128, DC])
    g3i = inp("g3i", [128, HC])
    ident = inp("ident", [128, 128], BF16)
    sel16 = inp("sel16", [16, 128], BF16)
    oh16 = inp("oh16", [16, NST, 128], BF16)
    selc = inp("selc", [128, NST, 128], BF16)
    onesc_b = inp("onesc_b", [128, 1], BF16)
    onesc_f = inp("onesc_f", [128, 1])
    ones1_b = inp("ones1_b", [1, 128])
    selp = inp("selp", [128, 3, 8])
    empt = inp("empt", [128, 3])
    selb = inp("selb", [128, 8])
    mhalo = inp("mhalo", [128, 1])

    outT = dtr("outT", [128, DC, T], F32, kind="ExternalOutput").ap()
    kd = {"kind": "ExternalOutput"} if dbg else {}
    dbg_sini = dtr("dbg_sini", [128, 512], F32, **kd).ap() if dbg else None
    dbg_cc1 = dtr("dbg_cc1", [NCORE * 128, 1024], F32, **kd).ap() if dbg else None
    dbg_cc1i = dtr("dbg_cc1i", [128, 1024], F32, **kd).ap() if dbg else None
    dbg_wl = dtr("dbg_wl", [128, DIC, NST], F32, **kd).ap() if dbg else None
    dbg_wg = dtr("dbg_wg", [128, DIC, NST], F32, **kd).ap() if dbg else None

    sp_z = dtr("sp_z", [DIC, 128, T], BF16, **kd).ap()
    sp_u = dtr("sp_u", [DIC, 128, T], BF16, **kd).ap()
    sp_cm = dtr("sp_cm", [DIC, 128, T], BF16, **kd).ap()
    sp_y = dtr("sp_y", [DIC, 128, T], BF16, **kd).ap()
    sp_k = dtr("sp_k", [DC, 128, T], BF16).ap()
    sp_v = dtr("sp_v", [DC, 128, T], BF16).ap()
    sp_r = dtr("sp_r", [DC, 128, T], BF16).ap()
    sp_mr = dtr("sp_mr", [DC, 128, T], BF16).ap()
    sp_x2 = dtr("sp_x2", [DC, 128, T], F32).ap()
    sp_h = dtr("sp_h", [HC, 128, T], BF16).ap()
    sp_s1 = dtr("sp_s1", [HC, 128, T], BF16).ap()
    sp_tm = dtr("sp_tm", [128, 8], F32).ap()
    cc1i = dtr("cc1i", [128, 1024], F32).ap()
    cc1o = dtr("cc1o", [NCORE * 128, 1024], F32).ap()
    cc2i = dtr("cc2i", [128, DC], F32).ap()
    cc2o = dtr("cc2o", [NCORE * 128, DC], F32).ap()
    cc3i = dtr("cc3i", [128, 5 * DC], F32).ap()
    cc3o = dtr("cc3o", [NCORE * 128, 5 * DC], F32).ap()

    octx = contextlib.ExitStack()
    with octx:
        def sbuf(name, shape, dtp=F32, ctx=octx):
            return ctx.enter_context(nc.sbuf_tensor(name, shape, dtp))
        em = Emit(nc, octx)

        x_res = sbuf("x_res", [128, DC, TH])
        c_l1 = sbuf("c_l1", [128, DC, 12])
        c_nw = sbuf("c_nw", [128, DC])
        c_nwT = sbuf("c_nwT", [128, DC])
        c_sg = sbuf("c_sg", [128, HC]); c_su = sbuf("c_su", [128, HC])
        c_sd = sbuf("c_sd", [128, DC])
        c_g1 = sbuf("c_g1", [128, DC]); c_g2 = sbuf("c_g2", [128, DC])
        c_g3 = sbuf("c_g3", [128, HC])
        c_id = sbuf("c_id", [128, 128], BF16)
        c_ocb = sbuf("c_ocb", [128, 1], BF16)
        c_ocf = sbuf("c_ocf", [128, 1])
        c_o1b = sbuf("c_o1b", [1, 128])
        c_selp = sbuf("c_selp", [128, 3, 8])
        c_empt = sbuf("c_empt", [128, 3])
        c_selb = sbuf("c_selb", [128, 8])
        c_mhalo = sbuf("c_mhalo", [128, 1])
        sc = [sbuf(f"sc{i}", [128, T]) for i in range(8)]
        scb = [sbuf(f"scb{i}", [128, T], BF16) for i in range(4)]
        lnr = sbuf("lnr", [1, 512, 5])
        tiny = sbuf("tiny", [128, 64])
        trow = sbuf("trow", [1, 64])
        abuf = sbuf("abuf", [128, DC, 4])

        pb = [octx.enter_context(nc.psum_tensor(f"pb{i}", [128, 512], F32))
              for i in range(7)]
        pbt = octx.enter_context(nc.psum_tensor("pbt", [128, 128], BF16))

        def ts(out, in0, s1, s2, o0, o1=None, eng="v"):
            kw = {}
            if o1 is not None:
                kw["op1"] = o1
            return em.op(eng, lambda e, out=out, in0=in0, s1=s1, s2=s2, o0=o0, kw=kw:
                         e.tensor_scalar(out=out, in0=in0, scalar1=s1, scalar2=s2,
                                         op0=o0, **kw))

        def tt(out, in0, in1, op, eng="v"):
            return em.op(eng, lambda e, out=out, in0=in0, in1=in1, op=op:
                         e.tensor_tensor(out=out, in0=in0, in1=in1, op=op))

        def stt(out, in0, scalar, in1, op0, op1, eng="v"):
            return em.op(eng, lambda e, out=out, in0=in0, scalar=scalar, in1=in1,
                         op0=op0, op1=op1: e.scalar_tensor_tensor(
                             out=out, in0=in0, scalar=scalar, in1=in1, op0=op0,
                             op1=op1))

        def act(out, in_, func, bias=0.0, scale=1.0, accum=None, eng="s"):
            kw = {}
            if accum is not None:
                kw["accum_out"] = accum
            if eng == "s":
                em.wait("s", ("v", em.cnt["v"]))
            tok = em.op(eng, lambda e, out=out, in_=in_, func=func, bias=bias,
                        scale=scale, kw=kw: e.activation(
                            out=out, in_=in_, func=func, bias=bias, scale=scale,
                            **kw))
            if accum is not None:
                em.wait(eng, tok)
            return tok

        def cp(out, in_, eng="v"):
            if eng == "s":
                em.wait("s", ("v", em.cnt["v"]))
                return em.op(eng, lambda e, out=out, in_=in_: e.copy(out=out,
                                                                     in_=in_))
            return em.op(eng, lambda e, out=out, in_=in_: e.tensor_copy(out=out,
                                                                        in_=in_))

        def mm(out, lhsT, rhs, start, stop):
            return em.op("te", lambda e, out=out, l=lhsT, r=rhs, st=start, sp=stop:
                         e.matmul(out=out, lhsT=l, rhs=r, start=st, stop=sp))

        def tpose(out, in_):
            return em.op("te", lambda e, out=out, in_=in_:
                         e.transpose(out, in_, c_id[:]))

        def scan(out, d0, d1, initial):
            tok = em.op("v", lambda e, out=out, d0=d0, d1=d1, init=initial:
                        e.tensor_tensor_scan(out=out, data0=d0, data1=d1,
                                             initial=init, op0=OP.mult,
                                             op1=OP.add))
            em.wait("v", tok)
            return tok

        def red(out, in_, op, absval=False):
            tok = em.op("v", lambda e, out=out, in_=in_, op=op, av=absval:
                        e.tensor_reduce(out=out, in_=in_, axis=AX.X, op=op,
                                        apply_absolute_value=av))
            em.wait("v", tok)
            return tok

        def memset(ap_, val):
            tok = em.op("v", lambda e, ap_=ap_, val=val: e.memset(ap_, val))
            em.wait("v", tok)
            return tok

        def recip(out, in_):
            tok = em.op("v", lambda e, out=out, in_=in_: e.reciprocal(out=out,
                                                                      in_=in_))
            em.wait("v", tok)
            return tok

        pbusy = {}

        def pwait(i):
            em.waits("te", [pbusy.get(i)])

        def collective(in_ap, out_ap):
            def fn(e, i=in_ap, o=out_ap):
                return e.collective_compute(
                    "AllGather", OP.bypass, replica_groups=[list(range(NCORE))],
                    ins=[i.opt()], outs=[o.opt()])
            return em.op("gp", fn)

        def l1c(name):
            return L1V.index(name)

        def lw(name, ch):
            return c_l1[:, ch, l1c(name):l1c(name) + 1]

        def emit_ln(src, n, out_fn, affine=None):
            for th0 in range(0, n, 512):
                tn = min(512, n - th0)
                memset(lnr[0:1, 0:tn, 0], 0.0)
                memset(lnr[0:1, 0:tn, 1], 0.0)
                mmprev = [None, None]
                for ch in range(DC):
                    em.wait("s", mmprev[ch % 2])
                    ctok = cp(scb[ch % 2][:, 0:tn], src(ch)[:, th0:th0 + tn], eng="s")
                    sqtok = act(scb[2 + ch % 2][:, 0:tn], scb[ch % 2][:, 0:tn],
                                AF.Square)
                    em.wait("te", ctok)
                    pwait(4)
                    t1 = mm(pb[4][0:1, 0:tn], c_ocb[:, 0:1], scb[ch % 2][:, 0:tn],
                            True, True)
                    em.wait("te", sqtok)
                    pwait(5)
                    t2 = mm(pb[5][0:1, 0:tn], c_ocb[:, 0:1], scb[2 + ch % 2][:, 0:tn],
                            True, True)
                    em.waits("v", [t1, t2])
                    mmprev[ch % 2] = t2
                    pbusy[4] = tt(lnr[0:1, 0:tn, 0], lnr[0:1, 0:tn, 0],
                                  pb[4][0:1, 0:tn], OP.add)
                    pbusy[5] = tt(lnr[0:1, 0:tn, 1], lnr[0:1, 0:tn, 1],
                                  pb[5][0:1, 0:tn], OP.add)
                ts(lnr[0:1, 0:tn, 0], lnr[0:1, 0:tn, 0], 1.0 / D, None,
                   OP.mult)
                ts(lnr[0:1, 0:tn, 1], lnr[0:1, 0:tn, 1], 1.0 / D, None,
                   OP.mult)
                tt(lnr[0:1, 0:tn, 2], lnr[0:1, 0:tn, 0],
                   lnr[0:1, 0:tn, 0], OP.mult)
                tt(lnr[0:1, 0:tn, 2], lnr[0:1, 0:tn, 1],
                   lnr[0:1, 0:tn, 2], OP.subtract)
                vtok = ts(lnr[0:1, 0:tn, 3], lnr[0:1, 0:tn, 2],
                          float(EPS_LN), None, OP.add)
                em.wait("s", vtok)
                sq = act(lnr[0:1, 0:tn, 3], lnr[0:1, 0:tn, 3],
                         AF.Sqrt)
                em.wait("v", sq)
                rtok = recip(lnr[0:1, 0:tn, 4], lnr[0:1, 0:tn, 3])
                em.waits("te", [rtok])
                bmu = mm(pb[4][:, 0:tn], c_o1b[0:1, :], lnr[0:1, 0:tn, 0],
                         True, True)
                brv = mm(pb[5][:, 0:tn], c_o1b[0:1, :], lnr[0:1, 0:tn, 4],
                         True, True)
                em.waits("v", [bmu, brv])
                for ch in range(DC):
                    tt(sc[7][:, 0:tn], src(ch)[:, th0:th0 + tn], pb[4][:, 0:tn],
                       OP.subtract)
                    tt(sc[7][:, 0:tn], sc[7][:, 0:tn], pb[5][:, 0:tn], OP.mult)
                    if affine is not None:
                        wcol, bcol = affine
                        ts(sc[7][:, 0:tn], sc[7][:, 0:tn], wcol(ch), bcol(ch),
                           OP.mult, OP.add)
                    out_fn(ch, th0, tn)
                em.wait("te", ("v", em.cnt["v"]))

        def emit_tok_ln(src_col, dst_col, wname, bname, slot=0):
            t1 = mm(pb[4][0:1, 0:DC], c_ocf[:, 0:1], src_col, True, True)
            sqt = act(tiny[:, 32:32 + DC], src_col, AF.Square)
            em.wait("te", sqt)
            t2 = mm(pb[5][0:1, 0:DC], c_ocf[:, 0:1], tiny[:, 32:32 + DC], True, True)
            em.waits("v", [t1, t2])
            rd1 = red(trow[0:1, slot + 8:slot + 9],
                      pb[4][0:1, 0:DC].rearrange("x (a c) -> x a c", a=1), OP.add)
            rd2 = red(trow[0:1, slot + 9:slot + 10],
                      pb[5][0:1, 0:DC].rearrange("x (a c) -> x a c", a=1), OP.add)
            em.waits("gp", [rd2])
            hop = cp(trow[0:1, slot:slot + 2], trow[0:1, slot + 8:slot + 10],
                     eng="v")
            em.wait("v", hop)
            ts(trow[0:1, slot:slot + 1], trow[0:1, slot:slot + 1], 1.0 / D, None,
               OP.mult)
            ts(trow[0:1, slot + 1:slot + 2], trow[0:1, slot + 1:slot + 2], 1.0 / D,
               None, OP.mult)
            tt(trow[0:1, slot + 2:slot + 3], trow[0:1, slot:slot + 1],
               trow[0:1, slot:slot + 1], OP.mult)
            tt(trow[0:1, slot + 2:slot + 3], trow[0:1, slot + 1:slot + 2],
               trow[0:1, slot + 2:slot + 3], OP.subtract)
            vt = ts(trow[0:1, slot + 3:slot + 4], trow[0:1, slot + 2:slot + 3],
                    float(EPS_LN), None, OP.add)
            em.wait("s", vt)
            sq2 = act(trow[0:1, slot + 3:slot + 4], trow[0:1, slot + 3:slot + 4],
                      AF.Sqrt)
            em.wait("v", sq2)
            rct = recip(trow[0:1, slot + 4:slot + 5], trow[0:1, slot + 3:slot + 4])
            em.waits("te", [rct])
            bmu = mm(pb[4][:, 0:1], c_o1b[0:1, :], trow[0:1, slot:slot + 1], True,
                     True)
            brv = mm(pb[5][:, 0:1], c_o1b[0:1, :], trow[0:1, slot + 4:slot + 5],
                     True, True)
            em.waits("v", [bmu, brv])
            for ch in range(DC):
                ts(dst_col[:, ch:ch + 1], src_col[:, ch:ch + 1], pb[4][:, 0:1], None,
                   OP.subtract)
                tt(dst_col[:, ch:ch + 1], dst_col[:, ch:ch + 1], pb[5][:, 0:1],
                   OP.mult)
                tt(dst_col[:, ch:ch + 1], dst_col[:, ch:ch + 1], lw(wname, ch),
                   OP.mult)
                tt(dst_col[:, ch:ch + 1], dst_col[:, ch:ch + 1], lw(bname, ch),
                   OP.add)
                ts(dst_col[:, ch:ch + 1], dst_col[:, ch:ch + 1], c_mhalo[:, 0:1],
                   None, OP.mult)
            em.wait("te", ("v", em.cnt["v"]))

        lt = []
        for dst, src_ in ((c_l1, l1v), (c_sg, sgate), (c_su, sup),
                          (c_sd, sdown), (c_g1, g1i), (c_g2, g2i), (c_g3, g3i),
                          (c_ocf, onesc_f), (c_selp, selp), (c_empt, empt),
                          (c_selb, selb), (c_mhalo, mhalo)):
            lt.append(em.dma("sy", dst[:], src_))
        for dst, src_ in ((c_id, ident), (c_ocb, onesc_b)):
            lt.append(em.dma("gp", dst[:], src_))
        lt.append(em.dma("sy", c_o1b[:], ones1_b))
        t_x0 = em.dma("sy", x_res[:], x0c)
        em.waits("v", lt)
        ts(c_nw[:], c_l1[:, :, 5], -1.0, None, OP.mult)
        ts(c_nwT[:], c_l1[:, :, 5], -float(T), None, OP.mult)
        em.barrier()

        with contextlib.ExitStack() as ctx1:
            xp = sbuf("xp", [128, 4, TH], BF16, ctx=ctx1)
            dtlb = sbuf("dtlb", [128, T], BF16, ctx=ctx1)
            bmb = sbuf("bmb", [128, 4, T], BF16, ctx=ctx1)
            c_wdt = sbuf("c_wdt", [128, DIC, 128], BF16, ctx=ctx1)
            wgg = sbuf("wgg", [128, DIC, NST], F32, ctx=ctx1)
            wloc = sbuf("wloc", [128, DIC, NST], F32, ctx=ctx1)
            sini = sbuf("sini", [128, DIC, NST], F32, ctx=ctx1)
            c_a = sbuf("c_a", [128, DIC, NST], ctx=ctx1)
            c_bdt2 = sbuf("c_bdt2", [128, DIC], ctx=ctx1)
            c_convw = sbuf("c_convw", [128, DIC, 4], ctx=ctx1)
            c_convb = sbuf("c_convb", [128, DIC], ctx=ctx1)
            c_dpar = sbuf("c_dpar", [128, DIC], ctx=ctx1)
            c_s16 = sbuf("c_s16", [16, 128], BF16, ctx=ctx1)
            c_oh16 = sbuf("c_oh16", [16, NST, 128], BF16, ctx=ctx1)
            c_selc = sbuf("c_selc", [128, NST, 128], BF16, ctx=ctx1)
            st16b = sbuf("st16b", [16, TH], BF16, ctx=ctx1)
            st16c = sbuf("st16c", [16, T], BF16, ctx=ctx1)
            xrep = sbuf("xrep", [128, TH], BF16, ctx=ctx1)

            lt1 = []
            for dst, src_ in ((c_bdt2, bdt), (c_convw, convw),
                              (c_convb, convb), (c_dpar, dpar)):
                lt1.append(em.dma("sy", dst[:], src_))
            for dst, src_ in ((c_s16, sel16), (c_oh16, oh16), (c_selc, selc)):
                lt1.append(em.dma("gp", dst[:], src_))
            t_a = em.dma("sy", c_a[:], alog)
            em.wait("s", t_a)
            ae = act(c_a[:], c_a[:], AF.Exp)
            em.waits("v", lt1 + [ae])
            ts(c_a[:], c_a[:], -1.0, None, OP.mult)
            ts(c_bdt2[:], c_bdt2[:], 2.0, None, OP.mult)
            em.barrier()

            t_wdt = em.dma("gp", c_wdt[:], wdt_t.rearrange("c p m -> p c m"))

            with contextlib.ExitStack() as ctx1a:
                xln0 = sbuf("xln0", [128, DC, TH], BF16, ctx=ctx1a)
                wbuf = [sbuf(f"wbuf{i}", [128, 16, 128], BF16, ctx=ctx1a)
                        for i in range(2)]

                def out_ln0(ch, th0, tn):
                    cp(xln0[:, ch, th0:th0 + tn], sc[7][:, 0:tn])
                em.wait("s", t_x0)
                em.wait("v", t_x0)
                emit_ln(lambda ch: x_res[:, ch, :], TH, out_ln0)
                em.barrier()

                wpar = [None, None]
                for m in range(73):
                    par = m % 2
                    em.waits("gp", [wpar[par]])
                    twl = em.dma("gp", wbuf[par][:],
                                 win_t[m].rearrange("k p c -> p k c"))
                    if 32 <= m < 36:
                        tts_ = [(0, 512), (512, 512), (1024, 4)]
                    else:
                        tts_ = [(4, 512), (516, 512)]
                    for fi, (t0, tn) in enumerate(tts_):
                        bank = pb[fi % 2]
                        em.waits("te", [twl])
                        pwait(fi % 2)
                        for kc in range(DC):
                            last = mm(bank[:, 0:tn], wbuf[par][:, kc, :],
                                      xln0[:, kc, t0:t0 + tn], kc == 0, kc == DC - 1)
                        rt = t0 - 4
                        if m < 32:
                            em.waits("s", [last, wpar[par]])
                            etok = act(scb[m % 2][:, 0:tn], bank[:, 0:tn], AF.Silu)
                            em.wait("sy", etok)
                            wpar[par] = em.dma("sy", sp_z[m, :, rt:rt + tn],
                                               scb[m % 2][:, 0:tn])
                            pbusy[fi % 2] = etok
                        elif m < 36:
                            em.waits("v", [last])
                            wpar[par] = cp(xp[:, m - 32, t0:t0 + tn], bank[:, 0:tn])
                            pbusy[fi % 2] = wpar[par]
                        elif m < 40:
                            em.waits("v", [last])
                            wpar[par] = cp(bmb[:, m - 36, rt:rt + tn], bank[:, 0:tn])
                            pbusy[fi % 2] = wpar[par]
                        elif m < 72:
                            em.waits("v", [last, wpar[par]])
                            etok = cp(scb[2 + m % 2][:, 0:tn], bank[:, 0:tn])
                            em.wait("sy", etok)
                            wpar[par] = em.dma("sy", sp_cm[m - 40, :, rt:rt + tn],
                                               scb[2 + m % 2][:, 0:tn])
                            pbusy[fi % 2] = etok
                        else:
                            em.waits("v", [last])
                            wpar[par] = cp(dtlb[:, rt:rt + tn], bank[:, 0:tn])
                            pbusy[fi % 2] = wpar[par]
                em.barrier()

            def emit_delta(c, accum=None):
                em.wait("te", t_wdt)
                for t0 in range(0, T, 512):
                    pwait(5)
                    tk = mm(pb[5][:, 0:512], c_wdt[:, c, :], dtlb[:, t0:t0 + 512],
                            True, True)
                    em.wait("s", tk)
                    e1 = act(sc[0][:, t0:t0 + 512], pb[5][:, 0:512], AF.Exp,
                             bias=c_bdt2[:, c:c + 1])
                    pbusy[5] = e1
                return act(sc[0][:], sc[0][:], AF.Ln, bias=1.0, accum=accum)

            def emit_u(c, prev_te):
                em.waits("gp", [prev_te])
                tg = em.dma("gp", st16b[:, 0:TH],
                            xp[(c % 8) * 16:(c % 8) * 16 + 16, c // 8, :])
                xtoks = []
                for fi, (t0, tn) in enumerate(((0, 512), (512, 512), (1024, 4))):
                    em.waits("te", [tg])
                    pwait(4)
                    tk = mm(pb[4][:, 0:tn], c_s16[:], st16b[0:16, t0:t0 + tn],
                            True, True)
                    em.waits("v", [tk])
                    xtoks.append(cp(xrep[:, t0:t0 + tn], pb[4][:, 0:tn]))
                    pbusy[4] = xtoks[-1]
                ts(sc[1][:], xrep[:, 1:1025], c_convw[:, c, 0:1], None, OP.mult)
                for k in range(1, 4):
                    stt(sc[1][:], xrep[:, 1 + k:1025 + k], c_convw[:, c, k:k + 1],
                        sc[1][:], OP.mult, OP.add)
                em.wait("s", ("v", em.cnt["v"]))
                return act(sc[1][:], sc[1][:], AF.Silu, bias=c_convb[:, c:c + 1])

            def bcast_B(n, stok):
                out = []
                for fi, t0 in enumerate((0, 512)):
                    em.waits("te", [stok])
                    pwait(2 + fi)
                    out.append(mm(pb[2 + fi][:, 0:512], c_oh16[:, n, :],
                                  st16c[0:16, t0:t0 + 512], True, True))
                return out

            prev_te = None
            uspill = None
            for c in range(DIC):
                dtok = emit_delta(c, accum=tiny[:, 0:1])
                utok = emit_u(c, prev_te)
                em.waits("v", [utok, uspill])
                ub = cp(scb[1][:], sc[1][:])
                em.wait("sy", ub)
                uspill = em.dma("sy", sp_u[c], scb[1][:])
                em.waits("v", [dtok])
                dut = tt(sc[2][:], sc[0][:], sc[1][:], OP.mult)
                gt = tt(wgg[:, c, :], c_a[:, c, :],
                        tiny[:, 0:1].broadcast_to([128, NST]), OP.mult)
                em.wait("s", gt)
                act(wgg[:, c, :], wgg[:, c, :], AF.Exp)
                em.waits("gp", [prev_te])
                bstage = em.dma("gp", st16c[:, 0:T],
                                bmb[(c % 8) * 16:(c % 8) * 16 + 16, c // 8, :])
                for n in range(NST):
                    em.wait("s", dut)
                    datok = act(sc[3][:], sc[0][:], AF.Exp, scale=c_a[:, c, n:n + 1])
                    btoks = bcast_B(n, bstage)
                    em.waits("v", [datok] + btoks)
                    pbusy[2] = tt(sc[4][:, 0:512], sc[2][:, 0:512], pb[2][:, 0:512],
                                  OP.mult)
                    pbusy[3] = tt(sc[4][:, 512:1024], sc[2][:, 512:1024],
                                  pb[3][:, 0:512], OP.mult)
                    scan(sc[5][:], sc[3][:], sc[4][:], 0.0)
                    cp(wloc[:, c, n:n + 1], sc[5][:, T - 1:T])
                prev_te = ("te", em.cnt["te"])
            em.barrier()

            if dbg:
                em.dma("sy", dbg_wl, wloc[:])
                em.dma("sy", dbg_wg, wgg[:])
            pw = em.dma("gp", cc1i[:, 0:512],
                        wloc[:].rearrange("p a b -> p (a b)"))
            pg = em.dma("gp", cc1i[:, 512:1024],
                        wgg[:].rearrange("p a b -> p (a b)"))
            em.waits("gp", [pw, pg])
            if dbg:
                em.waits("gp", [pw, pg])
                em.dma("gp", dbg_cc1i, cc1i)
            cct = collective(cc1i, cc1o)
            if dbg:
                em.waits("gp", [cct])
                em.dma("gp", dbg_cc1, cc1o)
            with contextlib.ExitStack() as ctx1b:
                accW = [sbuf(f"accW{j}", [128, 512], F32, ctx=ctx1b)
                        for j in range(3)]
                accG = [sbuf(f"accG{j}", [128, 512], F32, ctx=ctx1b)
                        for j in range(3)]
                rb1 = [sbuf(f"rb1{i}", [128, 1024], F32, ctx=ctx1b)
                       for i in range(2)]
                for j in range(3):
                    memset(accW[j][:], 0.0)
                    memset(accG[j][:], 0.0)
                em.wait("gp", ("v", em.cnt["v"]))
                rtp = [None, None]
                for cp_ in range(NCORE):
                    em.waits("gp", [cct, rtp[cp_ % 2]])
                    rt = em.dma("gp", rb1[cp_ % 2][:],
                                cc1o[cp_ * 128:(cp_ + 1) * 128, :])
                    em.wait("v", rt)
                    for j in range(3):
                        stt(accW[j][:], rb1[cp_ % 2][:, 0:512],
                            c_selp[:, j, cp_:cp_ + 1], accW[j][:], OP.mult, OP.add)
                        rtp[cp_ % 2] = stt(accG[j][:], rb1[cp_ % 2][:, 512:1024],
                                           c_selp[:, j, cp_:cp_ + 1], accG[j][:],
                                           OP.mult, OP.add)
                sv = sini[:].rearrange("p a b -> p (a b)")
                memset(sv, 0.0)
                for j in range(3):
                    ts(accG[j][:], accG[j][:], c_empt[:, j:j + 1], None, OP.add)
                    tt(sv, sv, accG[j][:], OP.mult)
                    tt(sv, sv, accW[j][:], OP.add)
                if dbg:
                    em.wait("sy", ("v", em.cnt["v"]))
                    em.dma("sy", dbg_sini, sv)
                em.barrier()

            prev_te = None
            yspill = None
            for c in range(DIC):
                dtok = emit_delta(c)
                em.wait("sy", ("v", em.cnt["v"]))
                ur = em.dma("sy", scb[1][:], sp_u[c])
                em.wait("v", ur)
                cp(sc[1][:], scb[1][:])
                em.waits("v", [dtok])
                dut = tt(sc[2][:], sc[0][:], sc[1][:], OP.mult)
                em.waits("sy", [prev_te])
                cmr = em.dma("sy", scb[2][:], sp_cm[c])
                em.waits("gp", [prev_te])
                bstage = em.dma("gp", st16c[:, 0:T],
                                bmb[(c % 8) * 16:(c % 8) * 16 + 16, c // 8, :])
                memset(sc[6][:], 0.0)
                for n in range(NST):
                    em.wait("s", dut)
                    datok = act(sc[3][:], sc[0][:], AF.Exp, scale=c_a[:, c, n:n + 1])
                    btoks = bcast_B(n, bstage)
                    em.waits("v", [datok] + btoks)
                    pbusy[2] = tt(sc[4][:, 0:512], sc[2][:, 0:512], pb[2][:, 0:512],
                                  OP.mult)
                    pbusy[3] = tt(sc[4][:, 512:1024], sc[2][:, 512:1024],
                                  pb[3][:, 0:512], OP.mult)
                    scan(sc[5][:], sc[3][:], sc[4][:], sini[:, c, n:n + 1])
                    em.wait("te", cmr)
                    pwait(0)
                    pwait(1)
                    ctoks = [mm(pb[0][:, 0:512], c_selc[:, n, :], scb[2][:, 0:512],
                                True, True),
                             mm(pb[1][:, 0:512], c_selc[:, n, :], scb[2][:, 512:1024],
                                True, True)]
                    em.waits("v", ctoks)
                    pbusy[0] = tt(sc[7][:, 0:512], sc[5][:, 0:512], pb[0][:, 0:512],
                                  OP.mult)
                    pbusy[1] = tt(sc[7][:, 512:1024], sc[5][:, 512:1024],
                                  pb[1][:, 0:512], OP.mult)
                    tt(sc[6][:], sc[6][:], sc[7][:], OP.add)
                zr = em.dma("sy", scb[3][:], sp_z[c])
                stt(sc[6][:], sc[1][:], c_dpar[:, c:c + 1], sc[6][:], OP.mult,
                    OP.add)
                em.waits("v", [zr, yspill])
                yb = tt(scb[1][:], sc[6][:], scb[3][:], OP.mult)
                em.wait("sy", yb)
                yspill = em.dma("sy", sp_y[c], scb[1][:])
                prev_te = ("te", em.cnt["te"])
            em.barrier()

            with contextlib.ExitStack() as ctx1c:
                ygrp = sbuf("ygrp", [128, 8, T], BF16, ctx=ctx1c)
                wob = [sbuf(f"wob{i}", [128, 8, 128], BF16, ctx=ctx1c)
                       for i in range(2)]
                for g in range(4):
                    gr = []
                    em.wait("sy", ("te", em.cnt["te"]))
                    for ki in range(8):
                        gr.append(em.dma("sy", ygrp[:, ki, :], sp_y[g * 8 + ki]))
                    wpar = [None, None]
                    for m in range(DC):
                        par = m % 2
                        em.waits("gp", [wpar[par]])
                        twl = em.dma("gp", wob[par][:],
                                     wout_t[m, g * 8:(g + 1) * 8].rearrange(
                                         "k p c -> p k c"))
                        for fi, t0 in enumerate((0, 512)):
                            bank = pb[fi]
                            em.waits("te", [twl] + gr)
                            pwait(fi)
                            for ki in range(8):
                                last = mm(bank[:, 0:512], wob[par][:, ki, :],
                                          ygrp[:, ki, t0:t0 + 512], ki == 0,
                                          ki == 7)
                            em.waits("v", [last])
                            wpar[par] = tt(x_res[:, m, 4 + t0:4 + t0 + 512],
                                           x_res[:, m, 4 + t0:4 + t0 + 512],
                                           bank[:, 0:512], OP.add)
                            pbusy[fi] = wpar[par]
                    em.barrier()

        if upto <= 1:
            em.barrier()
            fin = em.dma("sy", outT, x_res[:, :, 4:TH])
            em.waits("gp", [fin])
            em.flush()
            return nc

        with contextlib.ExitStack() as ctx2:
            xlnb = sbuf("xlnb", [128, DC, T], BF16, ctx=ctx2)
            fuse = [sbuf(f"fuse{i}", [128, T], BF16, ctx=ctx2) for i in range(8)]
            wg8 = [sbuf(f"wg8{i}", [128, 2, DC, 128], BF16, ctx=ctx2)
                   for i in range(2)]
            wd8 = [sbuf(f"wd8{i}", [128, 8, 128], BF16, ctx=ctx2)
                   for i in range(2)]

            def out_lnf(ch, th0, tn):
                cp(xlnb[:, ch, th0:th0 + tn], sc[7][:, 0:tn])
            emit_ln(lambda ch: x_res[:, ch, 4:TH], T, out_lnf)
            em.barrier()

            wpar = [None, None]
            dpar2 = [None, None]
            for g in range(8):
                ftoks = []
                for ki in range(8):
                    hm = g * 8 + ki
                    par = hm % 2
                    em.waits("gp", [wpar[par]])
                    twl = em.dma("gp", wg8[par][:],
                                 wgu_t[hm].rearrange("two k p c -> p two k c"))
                    for fi, t0 in enumerate((0, 512)):
                        em.waits("te", [twl])
                        pwait(fi)
                        for kc in range(DC):
                            gm = mm(pb[fi][:, 0:512], wg8[par][:, 0, kc, :],
                                    xlnb[:, kc, t0:t0 + 512], kc == 0, kc == DC - 1)
                        pwait(2 + fi)
                        for kc in range(DC):
                            um = mm(pb[2 + fi][:, 0:512], wg8[par][:, 1, kc, :],
                                    xlnb[:, kc, t0:t0 + 512], kc == 0, kc == DC - 1)
                        em.waits("s", [gm])
                        gt = act(sc[0][:, t0:t0 + 512], pb[fi][:, 0:512], AF.Silu,
                                 scale=c_sg[:, hm:hm + 1])
                        pbusy[fi] = gt
                        em.waits("v", [um, gt])
                        ft = stt(fuse[ki][:, t0:t0 + 512], pb[2 + fi][:, 0:512],
                                 c_su[:, hm:hm + 1], sc[0][:, t0:t0 + 512],
                                 OP.mult, OP.mult)
                        pbusy[2 + fi] = ft
                        ftoks.append(ft)
                    wpar[par] = ftoks[-1]
                for mp in range(DC):
                    par = mp % 2
                    em.waits("gp", [dpar2[par]])
                    twl = em.dma("gp", wd8[par][:],
                                 wdn_t[g * 8:(g + 1) * 8, mp].rearrange(
                                     "k p c -> p k c"))
                    for fi, t0 in enumerate((0, 512)):
                        bank = pb[4 + fi]
                        em.waits("te", [twl] + ftoks)
                        pwait(4 + fi)
                        for ki in range(8):
                            dm = mm(bank[:, 0:512], wd8[par][:, ki, :],
                                    fuse[ki][:, t0:t0 + 512], ki == 0, ki == 7)
                        em.waits("v", [dm])
                        dpar2[par] = stt(x_res[:, mp, 4 + t0:4 + t0 + 512],
                                         bank[:, 0:512], c_sd[:, mp:mp + 1],
                                         x_res[:, mp, 4 + t0:4 + t0 + 512],
                                         OP.mult, OP.add)
                        pbusy[4 + fi] = dpar2[par]
                em.barrier()
            em.barrier()

        if upto <= 2:
            fin = em.dma("sy", outT, x_res[:, :, 4:TH])
            em.waits("gp", [fin])
            em.flush()
            return nc

        pbnd = em.dma("sy", cc2i, x_res[:, :, TH - 1:TH].rearrange(
            "p c t -> p (c t)"), slow=True)
        em.waits("gp", [pbnd])
        cc2t = collective(cc2i, cc2o)
        memset(abuf[:, :, 0], 0.0)
        em.wait("gp", ("v", em.cnt["v"]))
        hs = None
        for cp_ in range(NCORE):
            em.waits("gp", [cc2t, hs])
            rt = em.dma("gp", abuf[:, :, 1], cc2o[cp_ * 128:(cp_ + 1) * 128, :],
                        slow=True)
            em.wait("v", rt)
            hs = stt(abuf[:, :, 0], abuf[:, :, 1], c_selb[:, cp_:cp_ + 1],
                     abuf[:, :, 0], OP.mult, OP.add)
        em.barrier()

        with contextlib.ExitStack() as ctx3:
            slA = sbuf("slA", [128, DC, T + 1], BF16, ctx=ctx3)
            slB = sbuf("slB", [128, DC, T], BF16, ctx=ctx3)
            wv3 = [sbuf(f"wv3{i}", [128, DC, 128], BF16, ctx=ctx3)
                   for i in range(2)]
            aag = sbuf("aag", [128, 5, DC], F32, ctx=ctx3)

            sx = []
            for ch in range(DC):
                sx.append(em.dma("sy", sp_x2[ch], x_res[:, ch, 4:TH]))
            em.waits("v", sx)
            em.waits("s", sx)

            def out_xb(ch, th0, tn):
                cp(x_res[:, ch, 4 + th0:4 + th0 + tn], sc[7][:, 0:tn])
            emit_ln(lambda ch: x_res[:, ch, 4:TH], T, out_xb)
            em.barrier()

            t1 = mm(pb[4][0:1, 0:DC], c_ocf[:, 0:1], abuf[:, :, 0], True, True)
            sqt = act(abuf[:, :, 1], abuf[:, :, 0], AF.Square)
            em.wait("te", sqt)
            t2 = mm(pb[5][0:1, 0:DC], c_ocf[:, 0:1], abuf[:, :, 1], True, True)
            em.waits("v", [t1, t2])
            rd1 = red(trow[0:1, 30:31],
                      pb[4][0:1, 0:DC].rearrange("x (a c) -> x a c", a=1), OP.add)
            rd2 = red(trow[0:1, 31:32],
                      pb[5][0:1, 0:DC].rearrange("x (a c) -> x a c", a=1), OP.add)
            em.waits("gp", [rd2])
            hop = cp(trow[0:1, 0:2], trow[0:1, 30:32])
            em.wait("v", hop)
            ts(trow[0:1, 0:1], trow[0:1, 0:1], 1.0 / D, None, OP.mult)
            ts(trow[0:1, 1:2], trow[0:1, 1:2], 1.0 / D, None, OP.mult)
            tt(trow[0:1, 2:3], trow[0:1, 0:1], trow[0:1, 0:1], OP.mult)
            tt(trow[0:1, 2:3], trow[0:1, 1:2], trow[0:1, 2:3], OP.subtract)
            vt = ts(trow[0:1, 3:4], trow[0:1, 2:3], float(EPS_LN), None, OP.add)
            em.wait("s", vt)
            sq2 = act(trow[0:1, 3:4], trow[0:1, 3:4], AF.Sqrt)
            em.wait("v", sq2)
            rct = recip(trow[0:1, 4:5], trow[0:1, 3:4])
            em.waits("te", [rct])
            bmu = mm(pb[4][:, 0:1], c_o1b[0:1, :], trow[0:1, 0:1], True, True)
            brv = mm(pb[5][:, 0:1], c_o1b[0:1, :], trow[0:1, 4:5], True, True)
            em.waits("v", [bmu, brv])
            for ch in range(DC):
                ts(abuf[:, ch, 2:3], abuf[:, ch, 0:1], pb[4][:, 0:1], None,
                   OP.subtract)
                tt(abuf[:, ch, 2:3], abuf[:, ch, 2:3], pb[5][:, 0:1], OP.mult)
                tt(abuf[:, ch, 3:4], abuf[:, ch, 2:3], lw("ln1w", ch), OP.mult)
                tt(abuf[:, ch, 3:4], abuf[:, ch, 3:4], lw("ln1b", ch), OP.add)
                ts(abuf[:, ch, 3:4], abuf[:, ch, 3:4], c_mhalo[:, 0:1], None,
                   OP.mult)
            em.wait("te", ("v", em.cnt["v"]))

            def out_ln1(ch, th0, tn):
                cp(slA[:, ch, 1 + th0:1 + th0 + tn], sc[7][:, 0:tn])
            emit_ln(lambda ch: x_res[:, ch, 4:TH], T, out_ln1,
                    affine=(lambda ch: lw("ln1w", ch), lambda ch: lw("ln1b", ch)))
            for ch in range(DC):
                cp(slA[:, ch, 0:1], abuf[:, ch, 3:4])
            em.barrier()

            def emit_mix(src, mname, ch, out_ap):
                tt(sc[0][:], src[:, ch, 1:T + 1], src[:, ch, 0:T], OP.subtract)
                return stt(out_ap, sc[0][:], lw(mname, ch), src[:, ch, 0:T],
                           OP.mult, OP.add)

            def kvr_mm(widx, dst_sp, sigmoid_first):
                wpar = [None, None]
                prev_s = None
                for ch in range(DC):
                    if sigmoid_first:
                        em.waits("v", [prev_s])
                        mt = emit_mix(slA, "mr", ch, scb[0][:])
                        em.wait("s", mt)
                        prev_s = act(slB[:, ch, :], scb[0][:], AF.Sigmoid)
                    else:
                        emit_mix(slA, "mk" if widx == 0 else "mv", ch, slB[:, ch, :])
                em.barrier()
                for m in range(DC):
                    par = m % 2
                    em.waits("gp", [wpar[par]])
                    twl = em.dma("gp", wv3[par][:],
                                 wkvr_t[m, widx].rearrange("k p c -> p k c"))
                    for fi, t0 in enumerate((0, 512)):
                        bank = pb[fi]
                        em.waits("te", [twl])
                        pwait(fi)
                        for kc in range(DC):
                            km = mm(bank[:, 0:512], wv3[par][:, kc, :],
                                    slB[:, kc, t0:t0 + 512], kc == 0, kc == DC - 1)
                        em.waits("v", [km, wpar[par]])
                        et = cp(scb[fi][:, t0:t0 + 512], bank[:, 0:512])
                        pbusy[fi] = et
                        em.wait("sy", et)
                        wpar[par] = em.dma("sy", dst_sp[m, :, t0:t0 + 512],
                                           scb[fi][:, t0:t0 + 512])
                em.barrier()

            kvr_mm(0, sp_k, False)
            kvr_mm(1, sp_v, False)
            kvr_mm(2, sp_r, True)

            def emit_e12(ch):
                em.wait("sy", ("v", em.cnt["v"]))
                em.wait("v", ("s", em.cnt["s"]))
                kr = em.dma("sy", scb[0][:], sp_k[ch])
                vr = em.dma("sy", scb[1][:], sp_v[ch])
                em.wait("v", kr)
                ts(sc[0][:], scb[0][:], lw("tmu", ch), None, OP.add)
                qt = ts(sc[1][:], sc[0][:], lw("tmw", ch), None, OP.max)
                em.wait("s", qt)
                e1t = act(sc[2][:], sc[1][:], AF.Exp, bias=c_nw[:, ch:ch + 1],
                          scale=-1.0)
                em.wait("v", e1t)
                sbt = tt(sc[3][:], sc[0][:], sc[1][:], OP.subtract)
                em.wait("s", sbt)
                e2t = act(sc[3][:], sc[3][:], AF.Exp)
                em.waits("v", [e2t, vr])
                tt(sc[4][:], sc[3][:], scb[1][:], OP.mult)

            for ch in range(DC):
                emit_e12(ch)
                scan(sc[5][:], sc[2][:], sc[4][:], 0.0)
                cp(aag[:, 0, ch:ch + 1], sc[5][:, T - 1:T])
                scan(sc[5][:], sc[2][:], sc[3][:], 0.0)
                cp(aag[:, 1, ch:ch + 1], sc[5][:, T - 1:T])
                act(sc[6][:], sc[1][:], AF.Copy, accum=tiny[:, 0:1])
                act(aag[:, 2, ch:ch + 1], tiny[:, 0:1], AF.Exp, scale=-1.0,
                    bias=c_nwT[:, ch:ch + 1])
                cp(aag[:, 3, ch:ch + 1], x_res[:, ch, TH - 1:TH])
            rlr = em.dma("gp", tiny[:, 32:32 + DC],
                         sp_r[:, :, T - 1:T].rearrange("c p t -> p (c t)"),
                         slow=True)
            em.wait("v", rlr)
            cp(aag[:, 4, :], tiny[:, 32:32 + DC])
            em.barrier()

            pub = em.dma("sy", cc3i, aag[:].rearrange("p f c -> p (f c)"))
            em.waits("gp", [pub])
            cc3t = collective(cc3i, cc3o)
            a_in = abuf[:, :, 0]
            b_in = abuf[:, :, 1]
            ypv = abuf[:, :, 2]
            rpv = abuf[:, :, 3]
            accA = [sc[0][:, 0:DC], sc[0][:, 64:64 + DC], sc[0][:, 128:128 + DC]]
            accB = [sc[1][:, 0:DC], sc[1][:, 64:64 + DC], sc[1][:, 128:128 + DC]]
            accD = [sc[2][:, 0:DC], sc[2][:, 64:64 + DC], sc[2][:, 128:128 + DC]]
            for j in range(3):
                memset(accA[j], 0.0)
                memset(accB[j], 0.0)
                memset(accD[j], 0.0)
            memset(ypv, 0.0)
            memset(rpv, 0.0)
            em.wait("gp", ("v", em.cnt["v"]))
            rtp = None
            for cp_ in range(NCORE):
                em.waits("gp", [cc3t, rtp])
                rt = em.dma("gp", sc[3][:, 0:5 * DC],
                            cc3o[cp_ * 128:(cp_ + 1) * 128, :])
                em.wait("v", rt)
                for j in range(3):
                    stt(accA[j], sc[3][:, 0:DC], c_selp[:, j, cp_:cp_ + 1], accA[j],
                        OP.mult, OP.add)
                    stt(accB[j], sc[3][:, DC:2 * DC], c_selp[:, j, cp_:cp_ + 1],
                        accB[j], OP.mult, OP.add)
                    stt(accD[j], sc[3][:, 2 * DC:3 * DC], c_selp[:, j, cp_:cp_ + 1],
                        accD[j], OP.mult, OP.add)
                stt(ypv, sc[3][:, 3 * DC:4 * DC], c_selb[:, cp_:cp_ + 1], ypv,
                    OP.mult, OP.add)
                rtp = stt(rpv, sc[3][:, 4 * DC:5 * DC], c_selb[:, cp_:cp_ + 1], rpv,
                          OP.mult, OP.add)
            memset(a_in, 0.0)
            memset(b_in, 0.0)
            for j in range(3):
                ts(accD[j], accD[j], c_empt[:, j:j + 1], None, OP.add)
                tt(a_in, a_in, accD[j], OP.mult)
                tt(a_in, a_in, accA[j], OP.add)
                tt(b_in, b_in, accD[j], OP.mult)
                tt(b_in, b_in, accB[j], OP.add)
            em.barrier()

            wkvr = slA
            ts(tiny[:, 30:31], c_mhalo[:, 0:1], -1.0, 1.0, OP.mult, OP.add)
            ts(sc[0][:, 0:DC], b_in, tiny[:, 30:31], None, OP.add)
            recip(sc[0][:, 0:DC], sc[0][:, 0:DC])
            tt(sc[1][:, 0:DC], a_in, sc[0][:, 0:DC], OP.mult)
            tt(sc[1][:, 0:DC], sc[1][:, 0:DC], rpv, OP.mult)
            cp(wkvr[:, :, 0], sc[1][:, 0:DC])
            for ch in range(DC):
                emit_e12(ch)
                rr = em.dma("sy", scb[2][:], sp_r[ch])
                scan(sc[5][:], sc[2][:], sc[4][:], a_in[:, ch:ch + 1])
                scan(sc[6][:], sc[2][:], sc[3][:], b_in[:, ch:ch + 1])
                recip(sc[7][:], sc[6][:])
                tt(sc[5][:], sc[5][:], sc[7][:], OP.mult)
                em.wait("v", rr)
                tt(wkvr[:, ch, 1:T + 1], sc[5][:], scb[2][:], OP.mult)
            em.barrier()

            wpar = [None, None]
            for m in range(DC):
                par = m % 2
                em.waits("gp", [wpar[par]])
                twl = em.dma("gp", wv3[par][:],
                             wo_t[m].rearrange("k p c -> p k c"))
                for fi, (t0, tn) in enumerate(((1, 512), (513, 512), (0, 1))):
                    bi_ = fi % 2 if tn > 1 else 2
                    bank = pb[bi_]
                    em.waits("te", [twl])
                    pwait(bi_)
                    for kc in range(DC):
                        am = mm(bank[:, 0:tn], wv3[par][:, kc, :],
                                wkvr[:, kc, t0:t0 + tn], kc == 0, kc == DC - 1)
                    em.waits("v", [am])
                    if tn > 1:
                        wpar[par] = stt(x_res[:, m, 3 + t0:3 + t0 + tn],
                                        bank[:, 0:tn], lw("bo", m),
                                        x_res[:, m, 3 + t0:3 + t0 + tn],
                                        OP.add, OP.add)
                    else:
                        ts(abuf[:, m, 1:2], bank[:, 0:1], lw("bo", m), None, OP.add)
                        wpar[par] = tt(abuf[:, m, 1:2], abuf[:, m, 1:2],
                                       ypv[:, m:m + 1], OP.add)
                    pbusy[bi_] = wpar[par]
            em.barrier()

            emit_tok_ln(abuf[:, :, 1], abuf[:, :, 2], "ln2w", "ln2b", slot=8)

            ln2x = slA

            def out_ln2(ch, th0, tn):
                cp(ln2x[:, ch, 1 + th0:1 + th0 + tn], sc[7][:, 0:tn])
            emit_ln(lambda ch: x_res[:, ch, 4:TH], T, out_ln2,
                    affine=(lambda ch: lw("ln2w", ch), lambda ch: lw("ln2b", ch)))
            for ch in range(DC):
                cp(ln2x[:, ch, 0:1], abuf[:, ch, 2:3])
            em.barrier()

            for ch in range(DC):
                emit_mix(ln2x, "cmk", ch, slB[:, ch, :])
            prev = None
            for ch in range(DC):
                em.waits("v", [prev])
                mt = emit_mix(ln2x, "cmr", ch, scb[ch % 2][:])
                em.waits("sy", [mt])
                prev = em.dma("sy", sp_mr[ch], scb[ch % 2][:])
            em.barrier()

            k2b = slA
            wpar = [None, None]
            for m in range(DC):
                par = m % 2
                em.waits("gp", [wpar[par]])
                twl = em.dma("gp", wv3[par][:],
                             wcm_t[m, 0].rearrange("k p c -> p k c"))
                for fi, t0 in enumerate((0, 512)):
                    bank = pb[fi]
                    em.waits("te", [twl])
                    pwait(fi)
                    for kc in range(DC):
                        km = mm(bank[:, 0:512], wv3[par][:, kc, :],
                                slB[:, kc, t0:t0 + 512], kc == 0, kc == DC - 1)
                    em.waits("v", [km])
                    em.wait("v", ("s", em.cnt["s"]))
                    rl = ts(sc[0][:, 0:512], bank[:, 0:512], 0.0, None, OP.max)
                    pbusy[fi] = rl
                    em.wait("s", rl)
                    wpar[par] = act(k2b[:, m, t0:t0 + 512], sc[0][:, 0:512],
                                    AF.Square)
            em.barrier()
            rt = None
            for ch in range(DC):
                rt = em.dma("sy", slB[:, ch, :], sp_mr[ch])
            em.wait("v", rt)
            em.wait("te", rt)
            wpar = [None, None]
            for m in range(DC):
                par = m % 2
                em.waits("gp", [wpar[par]])
                twl = em.dma("gp", wv3[par][:],
                             wcm_t[m, 2].rearrange("k p c -> p k c"))
                crt = None
                for fi, t0 in enumerate((0, 512)):
                    em.waits("te", [twl])
                    pwait(2 + fi)
                    for kc in range(DC):
                        rm = mm(pb[2 + fi][:, 0:512], wv3[par][:, kc, :],
                                slB[:, kc, t0:t0 + 512], kc == 0, kc == DC - 1)
                    em.waits("s", [rm])
                    crt = act(sc[2][:, t0:t0 + 512], pb[2 + fi][:, 0:512],
                              AF.Sigmoid)
                    pbusy[2 + fi] = crt
                em.waits("gp", [wpar[par], crt])
                twl2 = em.dma("gp", wv3[par][:],
                              wcm_t[m, 1].rearrange("k p c -> p k c"))
                for fi, t0 in enumerate((0, 512)):
                    em.waits("te", [twl2, crt])
                    pwait(fi)
                    for kc in range(DC):
                        vm = mm(pb[fi][:, 0:512], wv3[par][:, kc, :],
                                k2b[:, kc, t0:t0 + 512], kc == 0, kc == DC - 1)
                    em.waits("v", [vm, crt])
                    pbusy[fi] = tt(sc[1][:, 0:512], sc[2][:, t0:t0 + 512],
                                   pb[fi][:, 0:512], OP.mult)
                    wpar[par] = tt(x_res[:, m, 4 + t0:4 + t0 + 512],
                                   x_res[:, m, 4 + t0:4 + t0 + 512],
                                   sc[1][:, 0:512], OP.add)
            em.barrier()
            prevs = [None, None]
            for ch in range(DC):
                em.waits("sy", [prevs[ch % 2]])
                rt = em.dma("sy", sc[ch % 2][:], sp_x2[ch])
                em.wait("v", rt)
                prevs[ch % 2] = tt(x_res[:, ch, 4:TH], x_res[:, ch, 4:TH],
                                   sc[ch % 2][:], OP.add)
            em.barrier()

        if upto <= 3:
            fin = em.dma("sy", outT, x_res[:, :, 4:TH])
            em.waits("gp", [fin])
            em.flush()
            return nc

        with contextlib.ExitStack() as ctx4:
            rows = sbuf("rows", [1, T, 6], F32, ctx=ctx4)
            qb = sbuf("q1b", [128, DC, T], BF16, ctx=ctx4)
            tmax = sbuf("tmax", [128, 8, 2], F32, ctx=ctx4)
            qi8 = sbuf("qi8", [128, T], I8, ctx=ctx4)

            with contextlib.ExitStack() as ctx4s:
                w12s = [sbuf("w12s0", [128, 2, DC, 128], F32, ctx=ctx4s)]
                memset(tiny[:, 8:11], 0.0)
                spar = [None]
                for hm in range(HC):
                    par = 0
                    tl = em.dma("s", w12s[par][:],
                                w12_t[hm].rearrange("two k p c -> p two k c"))
                    em.wait("s", tl)
                    for half in range(2):
                        for seg in range(2):
                            a1 = act(sc[0][:],
                                     w12s[par][:, half, seg * 8:(seg + 1) * 8]
                                     .rearrange("p k c -> p (k c)"),
                                     AF.Abs, accum=tiny[:, 12:13])
                            em.wait("v", a1)
                            spar[par] = tt(tiny[:, 8 + half:9 + half],
                                           tiny[:, 8 + half:9 + half],
                                           tiny[:, 12:13], OP.add)
                for kc in range(HC):
                    par = 0
                    tl = em.dma("s", w12s[par][:, 0],
                                w3_t[kc].rearrange("k p c -> p k c"))
                    em.wait("s", tl)
                    for seg in range(2):
                        a1 = act(sc[0][:],
                                 w12s[par][:, 0, seg * 8:(seg + 1) * 8]
                                 .rearrange("p k c -> p (k c)"),
                                 AF.Abs, accum=tiny[:, 14:15])
                        em.wait("v", a1)
                        spar[par] = tt(tiny[:, 10:11], tiny[:, 10:11],
                                       tiny[:, 14:15], OP.add)
                em.barrier()
            t1 = mm(pb[4][0:1, 0:3], c_ocf[:, 0:1], tiny[:, 8:11], True, True)
            em.wait("v", t1)
            ts(trow[0:1, 8:11], pb[4][0:1, 0:3], 1.0 / (D * H), None, OP.mult)
            ts(trow[0:1, 11:14], trow[0:1, 8:11], float(EPS_RMS), None, OP.add)
            ivt = recip(trow[0:1, 14:17], trow[0:1, 11:14])
            em.wait("s", ivt)
            act(trow[0:1, 17:20], trow[0:1, 8:11], AF.Ln, bias=1.0)
            nt = ts(trow[0:1, 20:23], trow[0:1, 8:11], -1.0, 1.0, OP.mult, OP.add)
            em.wait("s", nt)
            lt2 = act(trow[0:1, 23:26], trow[0:1, 20:23], AF.Ln)
            em.wait("v", lt2)
            tt(trow[0:1, 26:29], trow[0:1, 17:20], trow[0:1, 23:26], OP.subtract)
            al = ts(trow[0:1, 26:29], trow[0:1, 26:29], 0.5, None, OP.mult)
            em.waits("te", [ivt, al])
            bi_ = mm(pb[4][:, 0:9], c_o1b[0:1, :], trow[0:1, 8:17], True, True)
            em.wait("v", bi_)
            cp(tiny[:, 16:25], pb[4][:, 0:9])
            em.barrier()

            for th0 in range(0, T, 512):
                memset(rows[0:1, th0:th0 + 512, 0], 0.0)
                memset(lnr[0:1, 0:512, 1], 0.0)
                mmprev = [None, None]
                for ch in range(DC):
                    em.wait("s", mmprev[ch % 2])
                    ctok = cp(scb[ch % 2][:, 0:512],
                              x_res[:, ch, 4 + th0:4 + th0 + 512], eng="s")
                    sqtok = act(scb[2 + ch % 2][:, 0:512], scb[ch % 2][:, 0:512],
                                AF.Square)
                    em.wait("te", ctok)
                    pwait(4)
                    t1 = mm(pb[4][0:1, 0:512], c_ocb[:, 0:1], scb[ch % 2][:, 0:512],
                            True, True)
                    em.wait("te", sqtok)
                    pwait(5)
                    t2 = mm(pb[5][0:1, 0:512], c_ocb[:, 0:1],
                            scb[2 + ch % 2][:, 0:512], True, True)
                    em.waits("v", [t1, t2])
                    mmprev[ch % 2] = t2
                    pbusy[4] = tt(rows[0:1, th0:th0 + 512, 0],
                                  rows[0:1, th0:th0 + 512, 0], pb[4][0:1, 0:512],
                                  OP.add)
                    pbusy[5] = tt(lnr[0:1, 0:512, 1], lnr[0:1, 0:512, 1],
                                  pb[5][0:1, 0:512], OP.add)
                ts(rows[0:1, th0:th0 + 512, 0], rows[0:1, th0:th0 + 512, 0],
                   1.0 / D, None, OP.mult)
                ts(lnr[0:1, 0:512, 1], lnr[0:1, 0:512, 1], 1.0 / D, None,
                   OP.mult)
                tt(lnr[0:1, 0:512, 2], rows[0:1, th0:th0 + 512, 0],
                   rows[0:1, th0:th0 + 512, 0], OP.mult)
                tt(lnr[0:1, 0:512, 2], lnr[0:1, 0:512, 1],
                   lnr[0:1, 0:512, 2], OP.subtract)
                vtok = ts(lnr[0:1, 0:512, 3], lnr[0:1, 0:512, 2],
                          float(EPS_LN), None, OP.add)
                em.wait("s", vtok)
                sq = act(lnr[0:1, 0:512, 3], lnr[0:1, 0:512, 3],
                         AF.Sqrt)
                em.wait("v", sq)
                recip(rows[0:1, th0:th0 + 512, 1], lnr[0:1, 0:512, 3])
                tt(rows[0:1, th0:th0 + 512, 2], rows[0:1, th0:th0 + 512, 1],
                   rows[0:1, th0:th0 + 512, 1], OP.mult)
                tt(rows[0:1, th0:th0 + 512, 2], rows[0:1, th0:th0 + 512, 2],
                   lnr[0:1, 0:512, 2], OP.mult)
                vt2 = ts(lnr[0:1, 0:512, 4], rows[0:1, th0:th0 + 512, 2],
                         float(EPS_RMS), None, OP.add)
                em.wait("s", vt2)
                sq3 = act(lnr[0:1, 0:512, 4], lnr[0:1, 0:512, 4],
                          AF.Sqrt)
                em.wait("v", sq3)
                recip(rows[0:1, th0:th0 + 512, 2], lnr[0:1, 0:512, 4])
            em.barrier()

            memset(tmax[:], 0.0)
            rtk = mxk = None
            em.wait("gp", ("v", em.cnt["v"]))
            for th0 in range(0, T, 512):
                pwait(4)
                bmu = mm(pb[4][:, 0:512], c_o1b[0:1, :], rows[0:1, th0:th0 + 512, 0],
                         True, True)
                pwait(5)
                brv = mm(pb[5][:, 0:512], c_o1b[0:1, :], rows[0:1, th0:th0 + 512, 1],
                         True, True)
                em.waits("v", [bmu, brv])
                for ch in range(DC):
                    tt(sc[0][:, 0:512], x_res[:, ch, 4 + th0:4 + th0 + 512],
                       pb[4][:, 0:512], OP.subtract)
                    xh = tt(sc[0][:, 0:512], sc[0][:, 0:512], pb[5][:, 0:512],
                            OP.mult)
                    pbusy[4] = xh
                    pbusy[5] = xh
                    for br in range(2):
                        gcol = c_g1[:, ch:ch + 1] if br == 0 else c_g2[:, ch:ch + 1]
                        gx = ts(scb[br][:, 0:512], sc[0][:, 0:512], gcol, None,
                                OP.mult)
                        for tb in range(4):
                            tj = th0 // 128 + tb
                            em.waits("te", [gx, rtk, mxk])
                            trt = tpose(pbt[:], scb[br][:, tb * 128:(tb + 1) * 128])
                            em.wait("v", trt)
                            em.wait("v", mxk)
                            rtk = red(sc[5][:, 0:1], pbt[:], OP.max, absval=True)
                            em.wait("gp", rtk)
                            mxk = tt(tmax[:, tj, br:br + 1],
                                     tmax[:, tj, br:br + 1],
                                     sc[5][:, 0:1], OP.max)
            em.barrier()
            mt = None
            for br in range(2):
                for tb in range(8):
                    mt = em.dma("gp", rows[0:1, tb * 128:(tb + 1) * 128, 4 + br],
                                tmax[:, tb, br:br + 1], slow=True)
            em.wait("v", mt)
            for br in range(2):
                tt(rows[0:1, 0:T, 4 + br], rows[0:1, 0:T, 4 + br],
                   rows[0:1, 0:T, 2], OP.mult)
                ts(rows[0:1, 0:T, 4 + br], rows[0:1, 0:T, 4 + br], 1e-4, None,
                   OP.max)
            em.barrier()

            def emit_qbuild(br):
                gc = c_g1 if br == 0 else c_g2
                for th0 in range(0, T, 512):
                    pwait(4)
                    bmu = mm(pb[4][:, 0:512], c_o1b[0:1, :],
                             rows[0:1, th0:th0 + 512, 0], True, True)
                    pwait(5)
                    brv = mm(pb[5][:, 0:512], c_o1b[0:1, :],
                             rows[0:1, th0:th0 + 512, 1], True, True)
                    recip(lnr[0:1, 0:512, 0], rows[0:1, th0:th0 + 512, 4 + br])
                    tt(lnr[0:1, 0:512, 0], lnr[0:1, 0:512, 0],
                       rows[0:1, th0:th0 + 512, 2], OP.mult)
                    sf = ts(lnr[0:1, 0:512, 0], lnr[0:1, 0:512, 0],
                            127.0, None, OP.mult)
                    em.waits("te", [sf])
                    pwait(0)
                    bsf = mm(pb[0][:, 0:512], c_o1b[0:1, :],
                             lnr[0:1, 0:512, 0], True, True)
                    em.waits("v", [bmu, brv, bsf])
                    lastq = None
                    for ch in range(DC):
                        tt(sc[0][:, 0:512], x_res[:, ch, 4 + th0:4 + th0 + 512],
                           pb[4][:, 0:512], OP.subtract)
                        tt(sc[0][:, 0:512], sc[0][:, 0:512], pb[5][:, 0:512],
                           OP.mult)
                        ts(sc[0][:, 0:512], sc[0][:, 0:512], gc[:, ch:ch + 1],
                           None, OP.mult)
                        lastq = tt(sc[0][:, 0:512], sc[0][:, 0:512],
                                   pb[0][:, 0:512], OP.mult)
                        cp(qi8[:, 0:512], sc[0][:, 0:512])
                        cp(qb[:, ch, th0:th0 + 512], qi8[:, 0:512])
                    pbusy[0] = lastq
                    pbusy[4] = lastq
                    pbusy[5] = lastq
                em.barrier()
                ts(rows[0:1, 0:T, 4 + br], rows[0:1, 0:T, 4 + br], 1.0 / 127.0,
                   None, OP.mult)
                ts(rows[0:1, 0:T, 4 + br], rows[0:1, 0:T, 4 + br],
                   trow[0:1, 26 + br:27 + br], None, OP.mult)
                em.barrier()

            def emit_w12pass(br):
                with contextlib.ExitStack() as ctxw:
                    w12s = sbuf("w12sh%d" % br, [128, DC, 128], F32, ctx=ctxw)
                    w12b = sbuf("w12bh%d" % br, [128, DC, 128], BF16, ctx=ctxw)
                    dsr = sbuf("dsr%d" % br, [128, T], F32, ctx=ctxw)
                    for fi, t0 in enumerate((0, 512)):
                        pwait(4)
                        dd = mm(pb[4][:, 0:512], c_o1b[0:1, :],
                                rows[0:1, t0:t0 + 512, 4 + br], True, True)
                        em.waits("v", [dd])
                        pbusy[4] = cp(dsr[:, t0:t0 + 512], pb[4][:, 0:512])
                    hprev = None
                    mm_prev = None
                    for hm in range(HC):
                        tl = em.dma("s", w12s[:],
                                    w12_t[hm, br].rearrange("k p c -> p k c"))
                        em.wait("s", tl)
                        em.waits("v", [mm_prev])
                        bt = None
                        for seg in range(2):
                            th_ = act(sc[0][:],
                                      w12s[:, seg * 8:(seg + 1) * 8]
                                      .rearrange("p k c -> p (k c)"), AF.Tanh,
                                      scale=tiny[:, 22 + br:23 + br])
                            em.wait("v", th_)
                            cp(qi8[:], sc[0][:])
                            bt = cp(w12b[:, seg * 8:(seg + 1) * 8]
                                    .rearrange("p k c -> p (k c)"), qi8[:])
                        em.waits("te", [bt])
                        ht = None
                        for fi, t0 in enumerate((0, 512)):
                            pwait(fi)
                            m1 = None
                            for kc in range(DC):
                                m1 = mm(pb[fi][:, 0:512], w12b[:, kc, :],
                                        qb[:, kc, t0:t0 + 512], kc == 0,
                                        kc == DC - 1)
                            em.waits("v", [m1])
                            lb = tt(sc[0][:, 0:512], pb[fi][:, 0:512],
                                    dsr[:, t0:t0 + 512], OP.mult)
                            pbusy[fi] = lb
                            if br == 0:
                                em.wait("s", lb)
                                sw = act(sc[1][:, 0:512], sc[0][:, 0:512],
                                         AF.Silu)
                                em.waits("v", [sw])
                                ht = cp(scb[2][:, t0:t0 + 512], sc[1][:, 0:512])
                            else:
                                rr = em.dma("sy", scb[3][:, t0:t0 + 512],
                                            sp_s1[hm, :, t0:t0 + 512])
                                em.wait("v", rr)
                                ht = tt(scb[2][:, t0:t0 + 512],
                                        sc[0][:, 0:512],
                                        scb[3][:, t0:t0 + 512], OP.mult)
                        mm_prev = ("te", em.cnt["te"])
                        dst = sp_s1 if br == 0 else sp_h
                        em.waits("sy", [hprev, ht])
                        hprev = em.dma("sy", dst[hm], scb[2][:])
                        em.wait("v", hprev)
                    em.barrier()

            emit_qbuild(0)
            emit_w12pass(0)
            emit_qbuild(1)
            emit_w12pass(1)

        with contextlib.ExitStack() as ctx4B:
            rowsb = sbuf("rowsb", [1, T, 3], F32, ctx=ctx4B)
            tmaxb = sbuf("tmaxb", [128, 8, 1], F32, ctx=ctx4B)
            qi8 = sbuf("qi8b", [128, T], I8, ctx=ctx4B)

            with contextlib.ExitStack() as ctx4h:
                hrow = sbuf("hrow2", [1, T, 2], F32, ctx=ctx4h)
                memset(tmaxb[:, :, 0], 0.0)
                memset(hrow[0:1, :, 0], 0.0)
                rtk = mxk = None
                em.wait("gp", ("v", em.cnt["v"]))
                for hm in range(HC):
                    em.wait("sy", ("v", em.cnt["v"]))
                    em.wait("sy", ("s", em.cnt["s"]))
                    hr = em.dma("sy", scb[2][:], sp_h[hm])
                    em.wait("s", hr)
                    for fi, t0 in enumerate((0, 512)):
                        em.wait("s", ("te", em.cnt["te"]))
                        hsq = act(scb[3][:, 0:512], scb[2][:, t0:t0 + 512],
                                  AF.Square)
                        em.wait("te", hsq)
                        pwait(4)
                        s1 = mm(pb[4][0:1, 0:512], c_ocb[:, 0:1], scb[3][:, 0:512],
                                True, True)
                        em.waits("v", [s1])
                        pbusy[4] = tt(hrow[0:1, t0:t0 + 512, 0],
                                      hrow[0:1, t0:t0 + 512, 0], pb[4][0:1, 0:512],
                                      OP.add)
                        gx = ts(scb[3][:, 0:512], scb[2][:, t0:t0 + 512],
                                c_g3[:, hm:hm + 1], None, OP.mult)
                        for tb in range(4):
                            tj = t0 // 128 + tb
                            em.waits("te", [gx, rtk, mxk])
                            trt = tpose(pbt[:], scb[3][:, tb * 128:(tb + 1) * 128])
                            em.wait("v", trt)
                            em.wait("v", mxk)
                            rtk = red(sc[5][:, 0:1], pbt[:], OP.max, absval=True)
                            em.wait("gp", rtk)
                            mxk = tt(tmaxb[:, tj, 0:1], tmaxb[:, tj, 0:1],
                                    sc[5][:, 0:1], OP.max)
                ts(hrow[0:1, :, 0], hrow[0:1, :, 0], 1.0 / H, None, OP.mult)
                vh = ts(hrow[0:1, :, 1], hrow[0:1, :, 0], float(EPS_RMS), None,
                        OP.add)
                em.wait("s", vh)
                sqh = act(hrow[0:1, :, 1], hrow[0:1, :, 1], AF.Sqrt)
                em.wait("v", sqh)
                recip(rowsb[0:1, 0:T, 0], hrow[0:1, :, 1])
                em.waits("gp", [mxk])
                for tb in range(8):
                    mt = em.dma("gp", rowsb[0:1, tb * 128:(tb + 1) * 128, 1],
                                tmaxb[:, tb, 0:1], slow=True)
                em.wait("v", mt)
                tt(rowsb[0:1, 0:T, 1], rowsb[0:1, 0:T, 1], rowsb[0:1, 0:T, 0], OP.mult)
                ts(rowsb[0:1, 0:T, 1], rowsb[0:1, 0:T, 1], 1e-4, None, OP.max)
                recip(rowsb[0:1, 0:T, 2], rowsb[0:1, 0:T, 1])
                tt(rowsb[0:1, 0:T, 2], rowsb[0:1, 0:T, 2], rowsb[0:1, 0:T, 0], OP.mult)
                ts(rowsb[0:1, 0:T, 2], rowsb[0:1, 0:T, 2], 127.0, None, OP.mult)
                ts(rowsb[0:1, 0:T, 1], rowsb[0:1, 0:T, 1], 1.0 / 127.0, None, OP.mult)
                ts(rowsb[0:1, 0:T, 1], rowsb[0:1, 0:T, 1],
                   trow[0:1, 28:29], None, OP.mult)
                em.barrier()

            with contextlib.ExitStack() as ctx4x:
                w3s = [sbuf("w3sl0", [128, DC, 128], F32, ctx=ctx4x)]
                w3g = [sbuf(f"w3g{i}", [128, DC, 128], BF16, ctx=ctx4x)
                       for i in range(4)]
                q3g = [sbuf(f"q3g{i}", [128, T], BF16, ctx=ctx4x) for i in range(4)]
                d3s = sbuf("d3s", [128, T], F32, ctx=ctx4x)
                pwait(2)
                pwait(3)
                pwait(4)
                pwait(5)
                b1 = mm(pb[2][:, 0:512], c_o1b[0:1, :], rowsb[0:1, 0:512, 2], True,
                        True)
                b2 = mm(pb[3][:, 0:512], c_o1b[0:1, :], rowsb[0:1, 512:1024, 2],
                        True, True)
                d1 = mm(pb[4][:, 0:512], c_o1b[0:1, :], rowsb[0:1, 0:512, 1], True,
                        True)
                d2 = mm(pb[5][:, 0:512], c_o1b[0:1, :], rowsb[0:1, 512:1024, 1],
                        True, True)
                em.waits("v", [b1, b2, d1, d2])
                cp(d3s[:, 0:512], pb[4][:, 0:512])
                pbusy[4] = pbusy[5] = cp(d3s[:, 512:1024], pb[5][:, 0:512])
                spar = [None]
                for g in range(16):
                    for ki in range(4):
                        kc = g * 4 + ki
                        par = 0
                        tl = em.dma("s", w3s[par][:],
                                    w3_t[kc].rearrange("k p c -> p k c"))
                        em.wait("s", tl)
                        for seg in range(2):
                            th_ = act(sc[0][:],
                                      w3s[par][:, seg * 8:(seg + 1) * 8]
                                      .rearrange("p k c -> p (k c)"), AF.Tanh,
                                      scale=tiny[:, 24:25])
                            em.wait("v", th_)
                            cp(qi8[:], sc[0][:])
                            spar[par] = cp(w3g[ki][:, seg * 8:(seg + 1) * 8]
                                           .rearrange("p k c -> p (k c)"), qi8[:])
                        em.wait("sy", ("v", em.cnt["v"]))
                        hr = em.dma("sy", scb[2][:], sp_h[kc])
                        em.wait("v", hr)
                        for fi, t0 in enumerate((0, 512)):
                            ts(sc[1][:, 0:512], scb[2][:, t0:t0 + 512],
                               c_g3[:, kc:kc + 1], None, OP.mult)
                            tt(sc[1][:, 0:512], sc[1][:, 0:512],
                               pb[2 + fi][:, 0:512], OP.mult)
                            cp(qi8[:, 0:512], sc[1][:, 0:512])
                            cp(q3g[ki][:, t0:t0 + 512], qi8[:, 0:512])
                    qtok = ("v", em.cnt["v"])
                    for mp in range(DC):
                        for fi, t0 in enumerate((0, 512)):
                            bidx = (mp * 2 + fi) % 2
                            bank = pb[bidx]
                            em.waits("te", [qtok])
                            pwait(bidx)
                            for ki in range(4):
                                wm = mm(bank[:, 0:512], w3g[ki][:, mp, :],
                                        q3g[ki][:, t0:t0 + 512], ki == 0, ki == 3)
                            em.waits("v", [wm])
                            pbusy[bidx] = tt(sc[3][:, 0:512], bank[:, 0:512],
                                             d3s[:, t0:t0 + 512], OP.mult)
                            tt(x_res[:, mp, 4 + t0:4 + t0 + 512],
                               x_res[:, mp, 4 + t0:4 + t0 + 512], sc[3][:, 0:512],
                               OP.add)
                    em.barrier()

        fin = em.dma("sy", outT, x_res[:, :, 4:TH])
        em.waits("gp", [fin])
        em.flush()
    return nc


def _prep_inputs(inp):
    C = np.ascontiguousarray
    d = {}
    W_in = inp["L0_attn_W_in"]
    d["win_t"] = C(W_in.reshape(73, 128, 16, 128).transpose(0, 2, 3, 1))
    d["wdt_t"] = C(inp["L0_attn_W_dt"].reshape(32, 128, 128).transpose(0, 2, 1))
    d["alog"] = C(inp["L0_attn_A_log"].reshape(32, 128, 16).transpose(1, 0, 2))
    d["bdt"] = C(inp["L0_attn_b_dt"].reshape(32, 128).T)
    d["convw"] = C(inp["L0_attn_conv_w"][:, 0, :].reshape(32, 128, 4)
                   .transpose(1, 0, 2))
    d["convb"] = C(inp["L0_attn_conv_b"].reshape(32, 128).T)
    d["dpar"] = C(inp["L0_attn_D_param"].reshape(32, 128).T)
    d["wout_t"] = C(inp["L0_attn_W_out"].reshape(16, 128, 32, 128)
                    .transpose(0, 2, 3, 1))
    wg = inp["L0_ffn_w_gate"].reshape(16, 128, 64, 128).transpose(2, 0, 1, 3)
    wu = inp["L0_ffn_w_up"].reshape(16, 128, 64, 128).transpose(2, 0, 1, 3)
    d["wgu_t"] = C(np.stack([wg, wu], axis=1))
    d["wdn_t"] = C(inp["L0_ffn_w_down"].reshape(64, 128, 16, 128)
                   .transpose(0, 2, 1, 3))
    d["sgate"] = C(inp["L0_ffn_s_gate"].reshape(64, 128).T)
    d["sup"] = C(inp["L0_ffn_s_up"].reshape(64, 128).T)
    d["sdown"] = C(inp["L0_ffn_s_down"].reshape(16, 128).T)
    vecs = [inp["L1_attn_" + n] for n in
            ["ln1_w", "ln1_b", "ln2_w", "ln2_b", "tm_u", "tm_w", "tm_mix_k",
             "tm_mix_v", "tm_mix_r", "cm_mix_k", "cm_mix_r", "tm_bo"]]
    d["l1v"] = C(np.stack(vecs, 0).reshape(12, 16, 128).transpose(2, 1, 0))
    tile2k = lambda w: w.reshape(16, 128, 16, 128).transpose(0, 2, 3, 1)
    d["wkvr_t"] = C(np.stack([tile2k(inp["L1_attn_tm_Wk"]),
                              tile2k(inp["L1_attn_tm_Wv"]),
                              tile2k(inp["L1_attn_tm_Wr"])], axis=1))
    d["wo_t"] = C(tile2k(inp["L1_attn_tm_Wo"]))
    d["wcm_t"] = C(np.stack([tile2k(inp["L1_attn_cm_Wk"]),
                             tile2k(inp["L1_attn_cm_Wv"]),
                             tile2k(inp["L1_attn_cm_Wr"])], axis=1))
    w1 = inp["L1_ffn_w1"].reshape(64, 128, 16, 128).transpose(0, 2, 3, 1)
    w2 = inp["L1_ffn_w2"].reshape(64, 128, 16, 128).transpose(0, 2, 3, 1)
    d["w12_t"] = C(np.stack([w1, w2], axis=1))
    d["w3_t"] = C(inp["L1_ffn_w3"].reshape(16, 128, 64, 128).transpose(2, 0, 3, 1))
    d["g1i"] = C(inp["L1_ffn_g1"].reshape(16, 128).T)
    d["g2i"] = C(inp["L1_ffn_g2"].reshape(16, 128).T)
    d["g3i"] = C(inp["L1_ffn_g3"].reshape(64, 128).T)
    d["ident"] = np.eye(128, dtype=BF)
    s16 = np.zeros((16, 128), dtype=BF)
    for p in range(128):
        s16[p % 16, p] = 1
    d["sel16"] = s16
    o16 = np.zeros((16, 16, 128), dtype=BF)
    for n in range(16):
        o16[n, n, :] = 1
    d["oh16"] = C(o16)
    slc = np.zeros((128, 16, 128), dtype=BF)
    for n in range(16):
        for tgt in range(128):
            slc[(tgt // 16) * 16 + n, n, tgt] = 1
    d["selc"] = slc
    d["onesc_b"] = np.ones((128, 1), dtype=BF)
    d["onesc_f"] = np.ones((128, 1), dtype=np.float32)
    d["ones1_b"] = np.ones((1, 128), dtype=np.float32)
    return d


def kernel(**inp):
    upto = int(inp.pop("_upto", 99))
    dbg = bool(inp.pop("_dbg", False))
    x = np.asarray(inp["L0_attn_hidden_states"], dtype=np.float32)
    shared = _prep_inputs({k: np.asarray(v) for k, v in inp.items()})
    nc = build_nc(upto=upto, dbg=dbg)
    in_maps = []
    for core in range(NCORE):
        b, q = core // 4, core % 4
        m = dict(shared)
        t0 = q * T
        xp_ = np.zeros((TH, D), np.float32)
        lo = max(t0 - 4, 0)
        xp_[4 - (t0 - lo):] = x[b, lo:t0 + T]
        m["x0c"] = np.ascontiguousarray(
            xp_.T.reshape(16, 128, TH).transpose(1, 0, 2))
        sp = np.zeros((128, 3, 8), np.float32)
        ep = np.zeros((128, 3), np.float32)
        for j in range(3):
            if j < q:
                sp[:, j, b * 4 + j] = 1.0
            else:
                ep[:, j] = 1.0
        m["selp"] = sp
        m["empt"] = ep
        sb_ = np.zeros((128, 8), np.float32)
        if q > 0:
            sb_[:, b * 4 + q - 1] = 1.0
        m["selb"] = sb_
        m["mhalo"] = np.full((128, 1), 1.0 if q > 0 else 0.0, np.float32)
        in_maps.append(m)
    res = run_bass_kernel_spmd(nc, in_maps, core_ids=list(range(NCORE)))
    out = np.empty((2, 4096, D), np.float32)
    for core in range(NCORE):
        b, q = core // 4, core % 4
        o = res.results[core]["outT"]
        out[b, q * T:(q + 1) * T] = o.transpose(1, 0, 2).reshape(D, T).T
    return out
    print("saved", out.shape)


# version 18
# speedup vs baseline: 1.1144x; 1.0351x over previous
import contextlib
import numpy as np
import ml_dtypes
import concourse.bass as bass
import concourse.mybir as mybir
from concourse.bass_utils import run_bass_kernel_spmd

F32 = mybir.dt.float32
BF16 = mybir.dt.bfloat16
I8 = mybir.dt.int8
AF = mybir.ActivationFunctionType
OP = mybir.AluOpType
AX = mybir.AxisListType
BF = ml_dtypes.bfloat16

D = 2048
DI = 4096
NST = 16
T = 1024
TH = 1028
NCORE = 8
DC = 16
DIC = 32
HC = 64
H = 8192
EPS_LN = 1e-5
EPS_RMS = 1e-8

ENGINES = ("te", "v", "s", "gp", "sy")
ENG_ATTR = {"te": "tensor", "v": "vector", "s": "scalar", "gp": "gpsimd", "sy": "sync"}
L1V = ["ln1w", "ln1b", "ln2w", "ln2b", "tmu", "tmw", "mk", "mv", "mr", "cmk", "cmr", "bo"]


NCH = 8


class Emit:
    def __init__(self, nc, ctx):
        self.nc = nc
        self.ops = {e: [] for e in ENGINES}
        self.sems = {e: ctx.enter_context(nc.semaphore("sem_" + e)) for e in ENGINES}
        for i in range(NCH):
            self.sems["ch%d" % i] = ctx.enter_context(
                nc.semaphore("sem_ch%d" % i))
        self.cnt = {e: 0 for e in self.sems}
        self.lastw = {e: {o: 0 for o in self.sems} for e in ENGINES}

    def op(self, eng, fn, dma=False):
        n = 16 if dma else 1
        self.cnt[eng] += n
        tok = (eng, self.cnt[eng])
        sem = self.sems[eng]

        def wrapped(e, fn=fn, sem=sem, n=n):
            fn(e).then_inc(sem, n)
        self.ops[eng].append(wrapped)
        return tok

    def dma(self, eng, out, in_, slow=False, ch=None):
        kw = {"allow_slow_non_contiguous": True} if slow else {}
        if ch is None:
            tok = self.op(eng, lambda e, o=out, i=in_, kw=kw: e.dma_start(
                out=o, in_=i, **kw), dma=True)
            self.wait(eng, tok)
            return tok
        name = "ch%d" % ch
        self.cnt[name] += 16
        tok = (name, self.cnt[name])
        sem = self.sems[name]

        def wrapped(e, o=out, i=in_, kw=kw, sem=sem):
            e.dma_start(out=o, in_=i, **kw).then_inc(sem, 16)
        self.ops[eng].append(wrapped)
        return tok

    def wait(self, eng, tok):
        if tok is None:
            return
        s, v = tok
        if self.lastw[eng][s] >= v:
            return
        self.lastw[eng][s] = v
        sem = self.sems[s]
        self.ops[eng].append(lambda e, sem=sem, v=v: e.wait_ge(sem, v))

    def waits(self, eng, toks):
        best = {}
        for t in toks:
            if t is None:
                continue
            s, v = t
            best[s] = max(best.get(s, 0), v)
        for s, v in best.items():
            self.wait(eng, (s, v))

    def barrier(self):
        snap = dict(self.cnt)
        for e in ENGINES:
            for o in self.sems:
                if o != e and snap[o] > 0:
                    self.wait(e, (o, snap[o]))

    def flush(self):
        with self.nc.Block() as block:
            for eng in ENGINES:
                ops = self.ops[eng]
                if not ops:
                    continue

                def run(e, ops=tuple(ops)):
                    for f in ops:
                        f(e)
                getattr(block, ENG_ATTR[eng])(run)
            self.ops = {e: [] for e in ENGINES}


def build_nc(upto=99, dbg=False):
    nc = bass.Bass("TRN2", target_bir_lowering=False, debug=False)
    dtr = nc.dram_tensor

    def inp(name, shape, dtype=F32):
        return dtr(name, shape, dtype, kind="ExternalInput").ap()

    x0c = inp("x0c", [128, DC, TH])
    win_t = inp("win_t", [73, 16, 128, 128])
    wdt_t = inp("wdt_t", [DIC, 128, 128])
    alog = inp("alog", [128, DIC, NST])
    bdt = inp("bdt", [128, DIC])
    convw = inp("convw", [128, DIC, 4])
    convb = inp("convb", [128, DIC])
    dpar = inp("dpar", [128, DIC])
    wout_t = inp("wout_t", [DC, DIC, 128, 128])
    wgu_t = inp("wgu_t", [HC, 2, DC, 128, 128], I8)
    wdn_t = inp("wdn_t", [HC, DC, 128, 128], I8)
    sgate = inp("sgate", [128, HC])
    sup = inp("sup", [128, HC])
    sdown = inp("sdown", [128, DC])
    l1v = inp("l1v", [128, DC, 12])
    wkvr_t = inp("wkvr_t", [DC, 3, DC, 128, 128])
    wo_t = inp("wo_t", [DC, DC, 128, 128])
    wcm_t = inp("wcm_t", [DC, 3, DC, 128, 128])
    w12_t = inp("w12_t", [HC, 2, DC, 128, 128])
    w3_t = inp("w3_t", [HC, DC, 128, 128])
    g1i = inp("g1i", [128, DC])
    g2i = inp("g2i", [128, DC])
    g3i = inp("g3i", [128, HC])
    ident = inp("ident", [128, 128], BF16)
    sel16 = inp("sel16", [16, 128], BF16)
    oh16 = inp("oh16", [16, NST, 128], BF16)
    selc = inp("selc", [128, NST, 128], BF16)
    onesc_b = inp("onesc_b", [128, 1], BF16)
    onesc_f = inp("onesc_f", [128, 1])
    ones1_b = inp("ones1_b", [1, 128])
    selp = inp("selp", [128, 3, 8])
    empt = inp("empt", [128, 3])
    selb = inp("selb", [128, 8])
    mhalo = inp("mhalo", [128, 1])

    outT = dtr("outT", [128, DC, T], F32, kind="ExternalOutput").ap()
    kd = {"kind": "ExternalOutput"} if dbg else {}
    dbg_sini = dtr("dbg_sini", [128, 512], F32, **kd).ap() if dbg else None
    dbg_cc1 = dtr("dbg_cc1", [NCORE * 128, 1024], F32, **kd).ap() if dbg else None
    dbg_cc1i = dtr("dbg_cc1i", [128, 1024], F32, **kd).ap() if dbg else None
    dbg_wl = dtr("dbg_wl", [128, DIC, NST], F32, **kd).ap() if dbg else None
    dbg_wg = dtr("dbg_wg", [128, DIC, NST], F32, **kd).ap() if dbg else None

    sp_z = dtr("sp_z", [DIC, 128, T], BF16, **kd).ap()
    sp_u = dtr("sp_u", [DIC, 128, T], BF16, **kd).ap()
    sp_cm = dtr("sp_cm", [DIC, 128, T], BF16, **kd).ap()
    sp_y = dtr("sp_y", [DIC, 128, T], BF16, **kd).ap()
    sp_k = dtr("sp_k", [DC, 128, T], BF16).ap()
    sp_v = dtr("sp_v", [DC, 128, T], BF16).ap()
    sp_r = dtr("sp_r", [DC, 128, T], BF16).ap()
    sp_mr = dtr("sp_mr", [DC, 128, T], BF16).ap()
    sp_x2 = dtr("sp_x2", [DC, 128, T], F32).ap()
    sp_h = dtr("sp_h", [HC, 128, T], BF16).ap()
    sp_s1 = dtr("sp_s1", [HC, 128, T], BF16).ap()
    sp_tm = dtr("sp_tm", [128, 8], F32).ap()
    cc1i = dtr("cc1i", [128, 1024], F32).ap()
    cc1o = dtr("cc1o", [NCORE * 128, 1024], F32).ap()
    cc2i = dtr("cc2i", [128, DC], F32).ap()
    cc2o = dtr("cc2o", [NCORE * 128, DC], F32).ap()
    cc3i = dtr("cc3i", [128, 5 * DC], F32).ap()
    cc3o = dtr("cc3o", [NCORE * 128, 5 * DC], F32).ap()

    octx = contextlib.ExitStack()
    with octx:
        def sbuf(name, shape, dtp=F32, ctx=octx):
            return ctx.enter_context(nc.sbuf_tensor(name, shape, dtp))
        em = Emit(nc, octx)

        x_res = sbuf("x_res", [128, DC, TH])
        c_l1 = sbuf("c_l1", [128, DC, 12])
        c_nw = sbuf("c_nw", [128, DC])
        c_nwT = sbuf("c_nwT", [128, DC])
        c_sg = sbuf("c_sg", [128, HC]); c_su = sbuf("c_su", [128, HC])
        c_sd = sbuf("c_sd", [128, DC])
        c_g1 = sbuf("c_g1", [128, DC]); c_g2 = sbuf("c_g2", [128, DC])
        c_g3 = sbuf("c_g3", [128, HC])
        c_id = sbuf("c_id", [128, 128], BF16)
        c_ocb = sbuf("c_ocb", [128, 1], BF16)
        c_ocf = sbuf("c_ocf", [128, 1])
        c_o1b = sbuf("c_o1b", [1, 128])
        c_selp = sbuf("c_selp", [128, 3, 8])
        c_empt = sbuf("c_empt", [128, 3])
        c_selb = sbuf("c_selb", [128, 8])
        c_mhalo = sbuf("c_mhalo", [128, 1])
        sc = [sbuf(f"sc{i}", [128, T]) for i in range(8)]
        scb = [sbuf(f"scb{i}", [128, T], BF16) for i in range(4)]
        lnr = sbuf("lnr", [1, 512, 5])
        tiny = sbuf("tiny", [128, 64])
        trow = sbuf("trow", [1, 64])
        abuf = sbuf("abuf", [128, DC, 4])

        pb = [octx.enter_context(nc.psum_tensor(f"pb{i}", [128, 512], F32))
              for i in range(7)]
        pbt = octx.enter_context(nc.psum_tensor("pbt", [128, 128], BF16))

        def ts(out, in0, s1, s2, o0, o1=None, eng="v"):
            kw = {}
            if o1 is not None:
                kw["op1"] = o1
            return em.op(eng, lambda e, out=out, in0=in0, s1=s1, s2=s2, o0=o0, kw=kw:
                         e.tensor_scalar(out=out, in0=in0, scalar1=s1, scalar2=s2,
                                         op0=o0, **kw))

        def tt(out, in0, in1, op, eng="v"):
            return em.op(eng, lambda e, out=out, in0=in0, in1=in1, op=op:
                         e.tensor_tensor(out=out, in0=in0, in1=in1, op=op))

        def stt(out, in0, scalar, in1, op0, op1, eng="v"):
            return em.op(eng, lambda e, out=out, in0=in0, scalar=scalar, in1=in1,
                         op0=op0, op1=op1: e.scalar_tensor_tensor(
                             out=out, in0=in0, scalar=scalar, in1=in1, op0=op0,
                             op1=op1))

        def act(out, in_, func, bias=0.0, scale=1.0, accum=None, eng="s"):
            kw = {}
            if accum is not None:
                kw["accum_out"] = accum
            if eng == "s":
                em.wait("s", ("v", em.cnt["v"]))
            tok = em.op(eng, lambda e, out=out, in_=in_, func=func, bias=bias,
                        scale=scale, kw=kw: e.activation(
                            out=out, in_=in_, func=func, bias=bias, scale=scale,
                            **kw))
            if accum is not None:
                em.wait(eng, tok)
            return tok

        def cp(out, in_, eng="v"):
            if eng == "s":
                em.wait("s", ("v", em.cnt["v"]))
                return em.op(eng, lambda e, out=out, in_=in_: e.copy(out=out,
                                                                     in_=in_))
            return em.op(eng, lambda e, out=out, in_=in_: e.tensor_copy(out=out,
                                                                        in_=in_))

        def mm(out, lhsT, rhs, start, stop):
            return em.op("te", lambda e, out=out, l=lhsT, r=rhs, st=start, sp=stop:
                         e.matmul(out=out, lhsT=l, rhs=r, start=st, stop=sp))

        def tpose(out, in_):
            return em.op("te", lambda e, out=out, in_=in_:
                         e.transpose(out, in_, c_id[:]))

        def scan(out, d0, d1, initial):
            tok = em.op("v", lambda e, out=out, d0=d0, d1=d1, init=initial:
                        e.tensor_tensor_scan(out=out, data0=d0, data1=d1,
                                             initial=init, op0=OP.mult,
                                             op1=OP.add))
            em.wait("v", tok)
            return tok

        def red(out, in_, op, absval=False):
            tok = em.op("v", lambda e, out=out, in_=in_, op=op, av=absval:
                        e.tensor_reduce(out=out, in_=in_, axis=AX.X, op=op,
                                        apply_absolute_value=av))
            em.wait("v", tok)
            return tok

        def memset(ap_, val):
            tok = em.op("v", lambda e, ap_=ap_, val=val: e.memset(ap_, val))
            em.wait("v", tok)
            return tok

        def recip(out, in_):
            tok = em.op("v", lambda e, out=out, in_=in_: e.reciprocal(out=out,
                                                                      in_=in_))
            em.wait("v", tok)
            return tok

        pbusy = {}

        def pwait(i):
            em.waits("te", [pbusy.get(i)])

        def collective(in_ap, out_ap):
            def fn(e, i=in_ap, o=out_ap):
                return e.collective_compute(
                    "AllGather", OP.bypass, replica_groups=[list(range(NCORE))],
                    ins=[i.opt()], outs=[o.opt()])
            return em.op("gp", fn)

        def l1c(name):
            return L1V.index(name)

        def lw(name, ch):
            return c_l1[:, ch, l1c(name):l1c(name) + 1]

        def emit_ln(src, n, out_fn, affine=None):
            for th0 in range(0, n, 512):
                tn = min(512, n - th0)
                memset(lnr[0:1, 0:tn, 0], 0.0)
                memset(lnr[0:1, 0:tn, 1], 0.0)
                mmprev = [None, None]
                for ch in range(DC):
                    em.wait("s", mmprev[ch % 2])
                    ctok = cp(scb[ch % 2][:, 0:tn], src(ch)[:, th0:th0 + tn], eng="s")
                    sqtok = act(scb[2 + ch % 2][:, 0:tn], scb[ch % 2][:, 0:tn],
                                AF.Square)
                    em.wait("te", ctok)
                    pwait(4)
                    t1 = mm(pb[4][0:1, 0:tn], c_ocb[:, 0:1], scb[ch % 2][:, 0:tn],
                            True, True)
                    em.wait("te", sqtok)
                    pwait(5)
                    t2 = mm(pb[5][0:1, 0:tn], c_ocb[:, 0:1], scb[2 + ch % 2][:, 0:tn],
                            True, True)
                    em.waits("v", [t1, t2])
                    mmprev[ch % 2] = t2
                    pbusy[4] = tt(lnr[0:1, 0:tn, 0], lnr[0:1, 0:tn, 0],
                                  pb[4][0:1, 0:tn], OP.add)
                    pbusy[5] = tt(lnr[0:1, 0:tn, 1], lnr[0:1, 0:tn, 1],
                                  pb[5][0:1, 0:tn], OP.add)
                ts(lnr[0:1, 0:tn, 0], lnr[0:1, 0:tn, 0], 1.0 / D, None,
                   OP.mult)
                ts(lnr[0:1, 0:tn, 1], lnr[0:1, 0:tn, 1], 1.0 / D, None,
                   OP.mult)
                tt(lnr[0:1, 0:tn, 2], lnr[0:1, 0:tn, 0],
                   lnr[0:1, 0:tn, 0], OP.mult)
                tt(lnr[0:1, 0:tn, 2], lnr[0:1, 0:tn, 1],
                   lnr[0:1, 0:tn, 2], OP.subtract)
                vtok = ts(lnr[0:1, 0:tn, 3], lnr[0:1, 0:tn, 2],
                          float(EPS_LN), None, OP.add)
                em.wait("s", vtok)
                sq = act(lnr[0:1, 0:tn, 3], lnr[0:1, 0:tn, 3],
                         AF.Sqrt)
                em.wait("v", sq)
                rtok = recip(lnr[0:1, 0:tn, 4], lnr[0:1, 0:tn, 3])
                em.waits("te", [rtok])
                bmu = mm(pb[4][:, 0:tn], c_o1b[0:1, :], lnr[0:1, 0:tn, 0],
                         True, True)
                brv = mm(pb[5][:, 0:tn], c_o1b[0:1, :], lnr[0:1, 0:tn, 4],
                         True, True)
                em.waits("v", [bmu, brv])
                for ch in range(DC):
                    tt(sc[7][:, 0:tn], src(ch)[:, th0:th0 + tn], pb[4][:, 0:tn],
                       OP.subtract)
                    tt(sc[7][:, 0:tn], sc[7][:, 0:tn], pb[5][:, 0:tn], OP.mult)
                    if affine is not None:
                        wcol, bcol = affine
                        ts(sc[7][:, 0:tn], sc[7][:, 0:tn], wcol(ch), bcol(ch),
                           OP.mult, OP.add)
                    out_fn(ch, th0, tn)
                em.wait("te", ("v", em.cnt["v"]))

        def emit_tok_ln(src_col, dst_col, wname, bname, slot=0):
            t1 = mm(pb[4][0:1, 0:DC], c_ocf[:, 0:1], src_col, True, True)
            sqt = act(tiny[:, 32:32 + DC], src_col, AF.Square)
            em.wait("te", sqt)
            t2 = mm(pb[5][0:1, 0:DC], c_ocf[:, 0:1], tiny[:, 32:32 + DC], True, True)
            em.waits("v", [t1, t2])
            rd1 = red(trow[0:1, slot + 8:slot + 9],
                      pb[4][0:1, 0:DC].rearrange("x (a c) -> x a c", a=1), OP.add)
            rd2 = red(trow[0:1, slot + 9:slot + 10],
                      pb[5][0:1, 0:DC].rearrange("x (a c) -> x a c", a=1), OP.add)
            em.waits("gp", [rd2])
            hop = cp(trow[0:1, slot:slot + 2], trow[0:1, slot + 8:slot + 10],
                     eng="v")
            em.wait("v", hop)
            ts(trow[0:1, slot:slot + 1], trow[0:1, slot:slot + 1], 1.0 / D, None,
               OP.mult)
            ts(trow[0:1, slot + 1:slot + 2], trow[0:1, slot + 1:slot + 2], 1.0 / D,
               None, OP.mult)
            tt(trow[0:1, slot + 2:slot + 3], trow[0:1, slot:slot + 1],
               trow[0:1, slot:slot + 1], OP.mult)
            tt(trow[0:1, slot + 2:slot + 3], trow[0:1, slot + 1:slot + 2],
               trow[0:1, slot + 2:slot + 3], OP.subtract)
            vt = ts(trow[0:1, slot + 3:slot + 4], trow[0:1, slot + 2:slot + 3],
                    float(EPS_LN), None, OP.add)
            em.wait("s", vt)
            sq2 = act(trow[0:1, slot + 3:slot + 4], trow[0:1, slot + 3:slot + 4],
                      AF.Sqrt)
            em.wait("v", sq2)
            rct = recip(trow[0:1, slot + 4:slot + 5], trow[0:1, slot + 3:slot + 4])
            em.waits("te", [rct])
            bmu = mm(pb[4][:, 0:1], c_o1b[0:1, :], trow[0:1, slot:slot + 1], True,
                     True)
            brv = mm(pb[5][:, 0:1], c_o1b[0:1, :], trow[0:1, slot + 4:slot + 5],
                     True, True)
            em.waits("v", [bmu, brv])
            for ch in range(DC):
                ts(dst_col[:, ch:ch + 1], src_col[:, ch:ch + 1], pb[4][:, 0:1], None,
                   OP.subtract)
                tt(dst_col[:, ch:ch + 1], dst_col[:, ch:ch + 1], pb[5][:, 0:1],
                   OP.mult)
                tt(dst_col[:, ch:ch + 1], dst_col[:, ch:ch + 1], lw(wname, ch),
                   OP.mult)
                tt(dst_col[:, ch:ch + 1], dst_col[:, ch:ch + 1], lw(bname, ch),
                   OP.add)
                ts(dst_col[:, ch:ch + 1], dst_col[:, ch:ch + 1], c_mhalo[:, 0:1],
                   None, OP.mult)
            em.wait("te", ("v", em.cnt["v"]))

        lt = []
        for dst, src_ in ((c_l1, l1v), (c_sg, sgate), (c_su, sup),
                          (c_sd, sdown), (c_g1, g1i), (c_g2, g2i), (c_g3, g3i),
                          (c_ocf, onesc_f), (c_selp, selp), (c_empt, empt),
                          (c_selb, selb), (c_mhalo, mhalo)):
            lt.append(em.dma("sy", dst[:], src_))
        for dst, src_ in ((c_id, ident), (c_ocb, onesc_b)):
            lt.append(em.dma("gp", dst[:], src_))
        lt.append(em.dma("sy", c_o1b[:], ones1_b))
        t_x0 = em.dma("sy", x_res[:], x0c)
        em.waits("v", lt)
        ts(c_nw[:], c_l1[:, :, 5], -1.0, None, OP.mult)
        ts(c_nwT[:], c_l1[:, :, 5], -float(T), None, OP.mult)
        em.barrier()

        with contextlib.ExitStack() as ctx1:
            xp = sbuf("xp", [128, 4, TH], BF16, ctx=ctx1)
            dtlb = sbuf("dtlb", [128, T], BF16, ctx=ctx1)
            bmb = sbuf("bmb", [128, 4, T], BF16, ctx=ctx1)
            c_wdt = sbuf("c_wdt", [128, DIC, 128], BF16, ctx=ctx1)
            wgg = sbuf("wgg", [128, DIC, NST], F32, ctx=ctx1)
            wloc = sbuf("wloc", [128, DIC, NST], F32, ctx=ctx1)
            sini = sbuf("sini", [128, DIC, NST], F32, ctx=ctx1)
            c_a = sbuf("c_a", [128, DIC, NST], ctx=ctx1)
            c_bdt2 = sbuf("c_bdt2", [128, DIC], ctx=ctx1)
            c_convw = sbuf("c_convw", [128, DIC, 4], ctx=ctx1)
            c_convb = sbuf("c_convb", [128, DIC], ctx=ctx1)
            c_dpar = sbuf("c_dpar", [128, DIC], ctx=ctx1)
            c_s16 = sbuf("c_s16", [16, 128], BF16, ctx=ctx1)
            c_oh16 = sbuf("c_oh16", [16, NST, 128], BF16, ctx=ctx1)
            c_selc = sbuf("c_selc", [128, NST, 128], BF16, ctx=ctx1)
            st16b = sbuf("st16b", [16, TH], BF16, ctx=ctx1)
            st16c = sbuf("st16c", [16, T], BF16, ctx=ctx1)
            xrep = sbuf("xrep", [128, TH], BF16, ctx=ctx1)

            lt1 = []
            for dst, src_ in ((c_bdt2, bdt), (c_convw, convw),
                              (c_convb, convb), (c_dpar, dpar)):
                lt1.append(em.dma("sy", dst[:], src_))
            for dst, src_ in ((c_s16, sel16), (c_oh16, oh16), (c_selc, selc)):
                lt1.append(em.dma("gp", dst[:], src_))
            t_a = em.dma("sy", c_a[:], alog)
            em.wait("s", t_a)
            ae = act(c_a[:], c_a[:], AF.Exp)
            em.waits("v", lt1 + [ae])
            ts(c_a[:], c_a[:], -1.0, None, OP.mult)
            ts(c_bdt2[:], c_bdt2[:], 2.0, None, OP.mult)
            em.barrier()

            t_wdt = em.dma("gp", c_wdt[:], wdt_t.rearrange("c p m -> p c m"))

            with contextlib.ExitStack() as ctx1a:
                xln0 = sbuf("xln0", [128, DC, TH], BF16, ctx=ctx1a)
                wbuf = [sbuf(f"wbuf{i}", [128, 16, 128], BF16, ctx=ctx1a)
                        for i in range(2)]

                def out_ln0(ch, th0, tn):
                    cp(xln0[:, ch, th0:th0 + tn], sc[7][:, 0:tn])
                em.wait("s", t_x0)
                em.wait("v", t_x0)
                emit_ln(lambda ch: x_res[:, ch, :], TH, out_ln0)
                em.barrier()

                wpar = [None, None]
                for m in range(73):
                    par = m % 2
                    em.waits("gp", [wpar[par]])
                    twl = em.dma("gp", wbuf[par][:],
                                 win_t[m].rearrange("k p c -> p k c"))
                    if 32 <= m < 36:
                        tts_ = [(0, 512), (512, 512), (1024, 4)]
                    else:
                        tts_ = [(4, 512), (516, 512)]
                    for fi, (t0, tn) in enumerate(tts_):
                        bank = pb[fi % 2]
                        em.waits("te", [twl])
                        pwait(fi % 2)
                        for kc in range(DC):
                            last = mm(bank[:, 0:tn], wbuf[par][:, kc, :],
                                      xln0[:, kc, t0:t0 + tn], kc == 0, kc == DC - 1)
                        rt = t0 - 4
                        if m < 32:
                            em.waits("s", [last, wpar[par]])
                            etok = act(scb[m % 2][:, 0:tn], bank[:, 0:tn], AF.Silu)
                            em.wait("sy", etok)
                            wpar[par] = em.dma("sy", sp_z[m, :, rt:rt + tn],
                                               scb[m % 2][:, 0:tn])
                            pbusy[fi % 2] = etok
                        elif m < 36:
                            em.waits("v", [last])
                            wpar[par] = cp(xp[:, m - 32, t0:t0 + tn], bank[:, 0:tn])
                            pbusy[fi % 2] = wpar[par]
                        elif m < 40:
                            em.waits("v", [last])
                            wpar[par] = cp(bmb[:, m - 36, rt:rt + tn], bank[:, 0:tn])
                            pbusy[fi % 2] = wpar[par]
                        elif m < 72:
                            em.waits("v", [last, wpar[par]])
                            etok = cp(scb[2 + m % 2][:, 0:tn], bank[:, 0:tn])
                            em.wait("sy", etok)
                            wpar[par] = em.dma("sy", sp_cm[m - 40, :, rt:rt + tn],
                                               scb[2 + m % 2][:, 0:tn])
                            pbusy[fi % 2] = etok
                        else:
                            em.waits("v", [last])
                            wpar[par] = cp(dtlb[:, rt:rt + tn], bank[:, 0:tn])
                            pbusy[fi % 2] = wpar[par]
                em.barrier()

            def emit_delta(c, accum=None):
                em.wait("te", t_wdt)
                for t0 in range(0, T, 512):
                    pwait(5)
                    tk = mm(pb[5][:, 0:512], c_wdt[:, c, :], dtlb[:, t0:t0 + 512],
                            True, True)
                    em.wait("s", tk)
                    e1 = act(sc[0][:, t0:t0 + 512], pb[5][:, 0:512], AF.Exp,
                             bias=c_bdt2[:, c:c + 1])
                    pbusy[5] = e1
                return act(sc[0][:], sc[0][:], AF.Ln, bias=1.0, accum=accum)

            def emit_u(c, prev_te):
                em.waits("gp", [prev_te])
                tg = em.dma("gp", st16b[:, 0:TH],
                            xp[(c % 8) * 16:(c % 8) * 16 + 16, c // 8, :])
                xtoks = []
                for fi, (t0, tn) in enumerate(((0, 512), (512, 512), (1024, 4))):
                    em.waits("te", [tg])
                    pwait(4)
                    tk = mm(pb[4][:, 0:tn], c_s16[:], st16b[0:16, t0:t0 + tn],
                            True, True)
                    em.waits("v", [tk])
                    xtoks.append(cp(xrep[:, t0:t0 + tn], pb[4][:, 0:tn]))
                    pbusy[4] = xtoks[-1]
                ts(sc[1][:], xrep[:, 1:1025], c_convw[:, c, 0:1], None, OP.mult)
                for k in range(1, 4):
                    stt(sc[1][:], xrep[:, 1 + k:1025 + k], c_convw[:, c, k:k + 1],
                        sc[1][:], OP.mult, OP.add)
                em.wait("s", ("v", em.cnt["v"]))
                return act(sc[1][:], sc[1][:], AF.Silu, bias=c_convb[:, c:c + 1])

            def bcast_B(n, stok):
                out = []
                for fi, t0 in enumerate((0, 512)):
                    em.waits("te", [stok])
                    pwait(2 + fi)
                    out.append(mm(pb[2 + fi][:, 0:512], c_oh16[:, n, :],
                                  st16c[0:16, t0:t0 + 512], True, True))
                return out

            prev_te = None
            uspill = None
            for c in range(DIC):
                dtok = emit_delta(c, accum=tiny[:, 0:1])
                utok = emit_u(c, prev_te)
                em.waits("v", [utok, uspill])
                ub = cp(scb[1][:], sc[1][:])
                em.wait("sy", ub)
                uspill = em.dma("sy", sp_u[c], scb[1][:], ch=4)
                em.waits("v", [dtok])
                dut = tt(sc[2][:], sc[0][:], sc[1][:], OP.mult)
                gt = tt(wgg[:, c, :], c_a[:, c, :],
                        tiny[:, 0:1].broadcast_to([128, NST]), OP.mult)
                em.wait("s", gt)
                act(wgg[:, c, :], wgg[:, c, :], AF.Exp)
                em.waits("gp", [prev_te])
                bstage = em.dma("gp", st16c[:, 0:T],
                                bmb[(c % 8) * 16:(c % 8) * 16 + 16, c // 8, :],
                                ch=3)
                for n in range(NST):
                    em.wait("s", dut)
                    datok = act(sc[3][:], sc[0][:], AF.Exp, scale=c_a[:, c, n:n + 1])
                    btoks = bcast_B(n, bstage)
                    em.waits("v", [datok] + btoks)
                    pbusy[2] = tt(sc[4][:, 0:512], sc[2][:, 0:512], pb[2][:, 0:512],
                                  OP.mult)
                    pbusy[3] = tt(sc[4][:, 512:1024], sc[2][:, 512:1024],
                                  pb[3][:, 0:512], OP.mult)
                    scan(sc[5][:], sc[3][:], sc[4][:], 0.0)
                    cp(wloc[:, c, n:n + 1], sc[5][:, T - 1:T])
                prev_te = ("te", em.cnt["te"])
            em.barrier()

            if dbg:
                em.dma("sy", dbg_wl, wloc[:])
                em.dma("sy", dbg_wg, wgg[:])
            pw = em.dma("gp", cc1i[:, 0:512],
                        wloc[:].rearrange("p a b -> p (a b)"))
            pg = em.dma("gp", cc1i[:, 512:1024],
                        wgg[:].rearrange("p a b -> p (a b)"))
            em.waits("gp", [pw, pg])
            if dbg:
                em.waits("gp", [pw, pg])
                em.dma("gp", dbg_cc1i, cc1i)
            cct = collective(cc1i, cc1o)
            if dbg:
                em.waits("gp", [cct])
                em.dma("gp", dbg_cc1, cc1o)
            with contextlib.ExitStack() as ctx1b:
                accW = [sbuf(f"accW{j}", [128, 512], F32, ctx=ctx1b)
                        for j in range(3)]
                accG = [sbuf(f"accG{j}", [128, 512], F32, ctx=ctx1b)
                        for j in range(3)]
                rb1 = [sbuf(f"rb1{i}", [128, 1024], F32, ctx=ctx1b)
                       for i in range(2)]
                for j in range(3):
                    memset(accW[j][:], 0.0)
                    memset(accG[j][:], 0.0)
                em.wait("gp", ("v", em.cnt["v"]))
                rtp = [None, None]
                for cp_ in range(NCORE):
                    em.waits("gp", [cct, rtp[cp_ % 2]])
                    rt = em.dma("gp", rb1[cp_ % 2][:],
                                cc1o[cp_ * 128:(cp_ + 1) * 128, :])
                    em.wait("v", rt)
                    for j in range(3):
                        stt(accW[j][:], rb1[cp_ % 2][:, 0:512],
                            c_selp[:, j, cp_:cp_ + 1], accW[j][:], OP.mult, OP.add)
                        rtp[cp_ % 2] = stt(accG[j][:], rb1[cp_ % 2][:, 512:1024],
                                           c_selp[:, j, cp_:cp_ + 1], accG[j][:],
                                           OP.mult, OP.add)
                sv = sini[:].rearrange("p a b -> p (a b)")
                memset(sv, 0.0)
                for j in range(3):
                    ts(accG[j][:], accG[j][:], c_empt[:, j:j + 1], None, OP.add)
                    tt(sv, sv, accG[j][:], OP.mult)
                    tt(sv, sv, accW[j][:], OP.add)
                if dbg:
                    em.wait("sy", ("v", em.cnt["v"]))
                    em.dma("sy", dbg_sini, sv)
                em.barrier()

            prev_te = None
            yspill = None
            for c in range(DIC):
                dtok = emit_delta(c)
                em.wait("sy", ("v", em.cnt["v"]))
                ur = em.dma("sy", scb[1][:], sp_u[c])
                em.wait("v", ur)
                cp(sc[1][:], scb[1][:])
                em.waits("v", [dtok])
                dut = tt(sc[2][:], sc[0][:], sc[1][:], OP.mult)
                em.waits("sy", [prev_te])
                cmr = em.dma("sy", scb[2][:], sp_cm[c])
                em.waits("gp", [prev_te])
                bstage = em.dma("gp", st16c[:, 0:T],
                                bmb[(c % 8) * 16:(c % 8) * 16 + 16, c // 8, :],
                                ch=3)
                memset(sc[6][:], 0.0)
                for n in range(NST):
                    em.wait("s", dut)
                    datok = act(sc[3][:], sc[0][:], AF.Exp, scale=c_a[:, c, n:n + 1])
                    btoks = bcast_B(n, bstage)
                    em.waits("v", [datok] + btoks)
                    pbusy[2] = tt(sc[4][:, 0:512], sc[2][:, 0:512], pb[2][:, 0:512],
                                  OP.mult)
                    pbusy[3] = tt(sc[4][:, 512:1024], sc[2][:, 512:1024],
                                  pb[3][:, 0:512], OP.mult)
                    scan(sc[5][:], sc[3][:], sc[4][:], sini[:, c, n:n + 1])
                    em.wait("te", cmr)
                    pwait(0)
                    pwait(1)
                    ctoks = [mm(pb[0][:, 0:512], c_selc[:, n, :], scb[2][:, 0:512],
                                True, True),
                             mm(pb[1][:, 0:512], c_selc[:, n, :], scb[2][:, 512:1024],
                                True, True)]
                    em.waits("v", ctoks)
                    pbusy[0] = tt(sc[7][:, 0:512], sc[5][:, 0:512], pb[0][:, 0:512],
                                  OP.mult)
                    pbusy[1] = tt(sc[7][:, 512:1024], sc[5][:, 512:1024],
                                  pb[1][:, 0:512], OP.mult)
                    tt(sc[6][:], sc[6][:], sc[7][:], OP.add)
                zr = em.dma("sy", scb[3][:], sp_z[c])
                stt(sc[6][:], sc[1][:], c_dpar[:, c:c + 1], sc[6][:], OP.mult,
                    OP.add)
                em.waits("v", [zr, yspill])
                yb = tt(scb[1][:], sc[6][:], scb[3][:], OP.mult)
                em.wait("sy", yb)
                yspill = em.dma("sy", sp_y[c], scb[1][:])
                prev_te = ("te", em.cnt["te"])
            em.barrier()

            with contextlib.ExitStack() as ctx1c:
                ygrp = sbuf("ygrp", [128, 8, T], BF16, ctx=ctx1c)
                wob = [sbuf(f"wob{i}", [128, 8, 128], BF16, ctx=ctx1c)
                       for i in range(2)]
                for g in range(4):
                    gr = []
                    em.wait("sy", ("te", em.cnt["te"]))
                    for ki in range(8):
                        gr.append(em.dma("sy", ygrp[:, ki, :], sp_y[g * 8 + ki]))
                    wpar = [None, None]
                    for m in range(DC):
                        par = m % 2
                        em.waits("gp", [wpar[par]])
                        twl = em.dma("gp", wob[par][:],
                                     wout_t[m, g * 8:(g + 1) * 8].rearrange(
                                         "k p c -> p k c"), ch=par)
                        for fi, t0 in enumerate((0, 512)):
                            bank = pb[fi]
                            em.waits("te", [twl] + gr)
                            pwait(fi)
                            for ki in range(8):
                                last = mm(bank[:, 0:512], wob[par][:, ki, :],
                                          ygrp[:, ki, t0:t0 + 512], ki == 0,
                                          ki == 7)
                            em.waits("v", [last])
                            wpar[par] = tt(x_res[:, m, 4 + t0:4 + t0 + 512],
                                           x_res[:, m, 4 + t0:4 + t0 + 512],
                                           bank[:, 0:512], OP.add)
                            pbusy[fi] = wpar[par]
                    em.barrier()

        if upto <= 1:
            em.barrier()
            fin = em.dma("sy", outT, x_res[:, :, 4:TH])
            em.waits("gp", [fin])
            em.flush()
            return nc

        with contextlib.ExitStack() as ctx2:
            xlnb = sbuf("xlnb", [128, DC, T], BF16, ctx=ctx2)
            fuse = [sbuf(f"fuse{i}", [128, T], BF16, ctx=ctx2) for i in range(8)]
            wg8 = [sbuf(f"wg8{i}", [128, 2, DC, 128], BF16, ctx=ctx2)
                   for i in range(2)]
            wd8 = [sbuf(f"wd8{i}", [128, 8, 128], BF16, ctx=ctx2)
                   for i in range(2)]

            def out_lnf(ch, th0, tn):
                cp(xlnb[:, ch, th0:th0 + tn], sc[7][:, 0:tn])
            emit_ln(lambda ch: x_res[:, ch, 4:TH], T, out_lnf)
            em.barrier()

            wpar = [None, None]
            dpar2 = [None, None]
            for g in range(8):
                ftoks = []
                for ki in range(8):
                    hm = g * 8 + ki
                    par = hm % 2
                    em.waits("gp", [wpar[par]])
                    twl = em.dma("gp", wg8[par][:],
                                 wgu_t[hm].rearrange("two k p c -> p two k c"),
                                 ch=par)
                    for fi, t0 in enumerate((0, 512)):
                        em.waits("te", [twl])
                        pwait(fi)
                        for kc in range(DC):
                            gm = mm(pb[fi][:, 0:512], wg8[par][:, 0, kc, :],
                                    xlnb[:, kc, t0:t0 + 512], kc == 0, kc == DC - 1)
                        pwait(2 + fi)
                        for kc in range(DC):
                            um = mm(pb[2 + fi][:, 0:512], wg8[par][:, 1, kc, :],
                                    xlnb[:, kc, t0:t0 + 512], kc == 0, kc == DC - 1)
                        em.waits("s", [gm])
                        gt = act(sc[0][:, t0:t0 + 512], pb[fi][:, 0:512], AF.Silu,
                                 scale=c_sg[:, hm:hm + 1])
                        pbusy[fi] = gt
                        em.waits("v", [um, gt])
                        ft = stt(fuse[ki][:, t0:t0 + 512], pb[2 + fi][:, 0:512],
                                 c_su[:, hm:hm + 1], sc[0][:, t0:t0 + 512],
                                 OP.mult, OP.mult)
                        pbusy[2 + fi] = ft
                        ftoks.append(ft)
                    wpar[par] = ftoks[-1]
                for mp in range(DC):
                    par = mp % 2
                    em.waits("gp", [dpar2[par]])
                    twl = em.dma("gp", wd8[par][:],
                                 wdn_t[g * 8:(g + 1) * 8, mp].rearrange(
                                     "k p c -> p k c"), ch=2 + par)
                    for fi, t0 in enumerate((0, 512)):
                        bank = pb[4 + fi]
                        em.waits("te", [twl] + ftoks)
                        pwait(4 + fi)
                        for ki in range(8):
                            dm = mm(bank[:, 0:512], wd8[par][:, ki, :],
                                    fuse[ki][:, t0:t0 + 512], ki == 0, ki == 7)
                        em.waits("v", [dm])
                        dpar2[par] = stt(x_res[:, mp, 4 + t0:4 + t0 + 512],
                                         bank[:, 0:512], c_sd[:, mp:mp + 1],
                                         x_res[:, mp, 4 + t0:4 + t0 + 512],
                                         OP.mult, OP.add)
                        pbusy[4 + fi] = dpar2[par]
                em.barrier()
            em.barrier()

        if upto <= 2:
            fin = em.dma("sy", outT, x_res[:, :, 4:TH])
            em.waits("gp", [fin])
            em.flush()
            return nc

        pbnd = em.dma("sy", cc2i, x_res[:, :, TH - 1:TH].rearrange(
            "p c t -> p (c t)"), slow=True)
        em.waits("gp", [pbnd])
        cc2t = collective(cc2i, cc2o)
        memset(abuf[:, :, 0], 0.0)
        em.wait("gp", ("v", em.cnt["v"]))
        hs = None
        for cp_ in range(NCORE):
            em.waits("gp", [cc2t, hs])
            rt = em.dma("gp", abuf[:, :, 1], cc2o[cp_ * 128:(cp_ + 1) * 128, :],
                        slow=True)
            em.wait("v", rt)
            hs = stt(abuf[:, :, 0], abuf[:, :, 1], c_selb[:, cp_:cp_ + 1],
                     abuf[:, :, 0], OP.mult, OP.add)
        em.barrier()

        with contextlib.ExitStack() as ctx3:
            slA = sbuf("slA", [128, DC, T + 1], BF16, ctx=ctx3)
            slB = sbuf("slB", [128, DC, T], BF16, ctx=ctx3)
            wv3 = [sbuf(f"wv3{i}", [128, DC, 128], BF16, ctx=ctx3)
                   for i in range(2)]
            aag = sbuf("aag", [128, 5, DC], F32, ctx=ctx3)

            sx = []
            for ch in range(DC):
                sx.append(em.dma("sy", sp_x2[ch], x_res[:, ch, 4:TH]))
            em.waits("v", sx)
            em.waits("s", sx)

            def out_xb(ch, th0, tn):
                cp(x_res[:, ch, 4 + th0:4 + th0 + tn], sc[7][:, 0:tn])
            emit_ln(lambda ch: x_res[:, ch, 4:TH], T, out_xb)
            em.barrier()

            t1 = mm(pb[4][0:1, 0:DC], c_ocf[:, 0:1], abuf[:, :, 0], True, True)
            sqt = act(abuf[:, :, 1], abuf[:, :, 0], AF.Square)
            em.wait("te", sqt)
            t2 = mm(pb[5][0:1, 0:DC], c_ocf[:, 0:1], abuf[:, :, 1], True, True)
            em.waits("v", [t1, t2])
            rd1 = red(trow[0:1, 30:31],
                      pb[4][0:1, 0:DC].rearrange("x (a c) -> x a c", a=1), OP.add)
            rd2 = red(trow[0:1, 31:32],
                      pb[5][0:1, 0:DC].rearrange("x (a c) -> x a c", a=1), OP.add)
            em.waits("gp", [rd2])
            hop = cp(trow[0:1, 0:2], trow[0:1, 30:32])
            em.wait("v", hop)
            ts(trow[0:1, 0:1], trow[0:1, 0:1], 1.0 / D, None, OP.mult)
            ts(trow[0:1, 1:2], trow[0:1, 1:2], 1.0 / D, None, OP.mult)
            tt(trow[0:1, 2:3], trow[0:1, 0:1], trow[0:1, 0:1], OP.mult)
            tt(trow[0:1, 2:3], trow[0:1, 1:2], trow[0:1, 2:3], OP.subtract)
            vt = ts(trow[0:1, 3:4], trow[0:1, 2:3], float(EPS_LN), None, OP.add)
            em.wait("s", vt)
            sq2 = act(trow[0:1, 3:4], trow[0:1, 3:4], AF.Sqrt)
            em.wait("v", sq2)
            rct = recip(trow[0:1, 4:5], trow[0:1, 3:4])
            em.waits("te", [rct])
            bmu = mm(pb[4][:, 0:1], c_o1b[0:1, :], trow[0:1, 0:1], True, True)
            brv = mm(pb[5][:, 0:1], c_o1b[0:1, :], trow[0:1, 4:5], True, True)
            em.waits("v", [bmu, brv])
            for ch in range(DC):
                ts(abuf[:, ch, 2:3], abuf[:, ch, 0:1], pb[4][:, 0:1], None,
                   OP.subtract)
                tt(abuf[:, ch, 2:3], abuf[:, ch, 2:3], pb[5][:, 0:1], OP.mult)
                tt(abuf[:, ch, 3:4], abuf[:, ch, 2:3], lw("ln1w", ch), OP.mult)
                tt(abuf[:, ch, 3:4], abuf[:, ch, 3:4], lw("ln1b", ch), OP.add)
                ts(abuf[:, ch, 3:4], abuf[:, ch, 3:4], c_mhalo[:, 0:1], None,
                   OP.mult)
            em.wait("te", ("v", em.cnt["v"]))

            def out_ln1(ch, th0, tn):
                cp(slA[:, ch, 1 + th0:1 + th0 + tn], sc[7][:, 0:tn])
            emit_ln(lambda ch: x_res[:, ch, 4:TH], T, out_ln1,
                    affine=(lambda ch: lw("ln1w", ch), lambda ch: lw("ln1b", ch)))
            for ch in range(DC):
                cp(slA[:, ch, 0:1], abuf[:, ch, 3:4])
            em.barrier()

            def emit_mix(src, mname, ch, out_ap):
                tt(sc[0][:], src[:, ch, 1:T + 1], src[:, ch, 0:T], OP.subtract)
                return stt(out_ap, sc[0][:], lw(mname, ch), src[:, ch, 0:T],
                           OP.mult, OP.add)

            def kvr_mm(widx, dst_sp, sigmoid_first):
                wpar = [None, None]
                prev_s = None
                for ch in range(DC):
                    if sigmoid_first:
                        em.waits("v", [prev_s])
                        mt = emit_mix(slA, "mr", ch, scb[0][:])
                        em.wait("s", mt)
                        prev_s = act(slB[:, ch, :], scb[0][:], AF.Sigmoid)
                    else:
                        emit_mix(slA, "mk" if widx == 0 else "mv", ch, slB[:, ch, :])
                em.barrier()
                for m in range(DC):
                    par = m % 2
                    em.waits("gp", [wpar[par]])
                    twl = em.dma("gp", wv3[par][:],
                                 wkvr_t[m, widx].rearrange("k p c -> p k c"),
                                 ch=par)
                    for fi, t0 in enumerate((0, 512)):
                        bank = pb[fi]
                        em.waits("te", [twl])
                        pwait(fi)
                        for kc in range(DC):
                            km = mm(bank[:, 0:512], wv3[par][:, kc, :],
                                    slB[:, kc, t0:t0 + 512], kc == 0, kc == DC - 1)
                        em.waits("v", [km, wpar[par]])
                        et = cp(scb[fi][:, t0:t0 + 512], bank[:, 0:512])
                        pbusy[fi] = et
                        em.wait("sy", et)
                        wpar[par] = em.dma("sy", dst_sp[m, :, t0:t0 + 512],
                                           scb[fi][:, t0:t0 + 512], ch=4 + fi)
                em.barrier()

            kvr_mm(0, sp_k, False)
            kvr_mm(1, sp_v, False)
            kvr_mm(2, sp_r, True)

            def emit_e12(ch):
                em.wait("sy", ("v", em.cnt["v"]))
                em.wait("v", ("s", em.cnt["s"]))
                kr = em.dma("sy", scb[0][:], sp_k[ch], ch=4)
                vr = em.dma("sy", scb[1][:], sp_v[ch], ch=5)
                em.wait("v", kr)
                ts(sc[0][:], scb[0][:], lw("tmu", ch), None, OP.add)
                qt = ts(sc[1][:], sc[0][:], lw("tmw", ch), None, OP.max)
                em.wait("s", qt)
                e1t = act(sc[2][:], sc[1][:], AF.Exp, bias=c_nw[:, ch:ch + 1],
                          scale=-1.0)
                em.wait("v", e1t)
                sbt = tt(sc[3][:], sc[0][:], sc[1][:], OP.subtract)
                em.wait("s", sbt)
                e2t = act(sc[3][:], sc[3][:], AF.Exp)
                em.waits("v", [e2t, vr])
                tt(sc[4][:], sc[3][:], scb[1][:], OP.mult)

            for ch in range(DC):
                emit_e12(ch)
                scan(sc[5][:], sc[2][:], sc[4][:], 0.0)
                cp(aag[:, 0, ch:ch + 1], sc[5][:, T - 1:T])
                scan(sc[5][:], sc[2][:], sc[3][:], 0.0)
                cp(aag[:, 1, ch:ch + 1], sc[5][:, T - 1:T])
                act(sc[6][:], sc[1][:], AF.Copy, accum=tiny[:, 0:1])
                act(aag[:, 2, ch:ch + 1], tiny[:, 0:1], AF.Exp, scale=-1.0,
                    bias=c_nwT[:, ch:ch + 1])
                cp(aag[:, 3, ch:ch + 1], x_res[:, ch, TH - 1:TH])
            rlr = em.dma("gp", tiny[:, 32:32 + DC],
                         sp_r[:, :, T - 1:T].rearrange("c p t -> p (c t)"),
                         slow=True)
            em.wait("v", rlr)
            cp(aag[:, 4, :], tiny[:, 32:32 + DC])
            em.barrier()

            pub = em.dma("sy", cc3i, aag[:].rearrange("p f c -> p (f c)"))
            em.waits("gp", [pub])
            cc3t = collective(cc3i, cc3o)
            a_in = abuf[:, :, 0]
            b_in = abuf[:, :, 1]
            ypv = abuf[:, :, 2]
            rpv = abuf[:, :, 3]
            accA = [sc[0][:, 0:DC], sc[0][:, 64:64 + DC], sc[0][:, 128:128 + DC]]
            accB = [sc[1][:, 0:DC], sc[1][:, 64:64 + DC], sc[1][:, 128:128 + DC]]
            accD = [sc[2][:, 0:DC], sc[2][:, 64:64 + DC], sc[2][:, 128:128 + DC]]
            for j in range(3):
                memset(accA[j], 0.0)
                memset(accB[j], 0.0)
                memset(accD[j], 0.0)
            memset(ypv, 0.0)
            memset(rpv, 0.0)
            em.wait("gp", ("v", em.cnt["v"]))
            rtp = None
            for cp_ in range(NCORE):
                em.waits("gp", [cc3t, rtp])
                rt = em.dma("gp", sc[3][:, 0:5 * DC],
                            cc3o[cp_ * 128:(cp_ + 1) * 128, :])
                em.wait("v", rt)
                for j in range(3):
                    stt(accA[j], sc[3][:, 0:DC], c_selp[:, j, cp_:cp_ + 1], accA[j],
                        OP.mult, OP.add)
                    stt(accB[j], sc[3][:, DC:2 * DC], c_selp[:, j, cp_:cp_ + 1],
                        accB[j], OP.mult, OP.add)
                    stt(accD[j], sc[3][:, 2 * DC:3 * DC], c_selp[:, j, cp_:cp_ + 1],
                        accD[j], OP.mult, OP.add)
                stt(ypv, sc[3][:, 3 * DC:4 * DC], c_selb[:, cp_:cp_ + 1], ypv,
                    OP.mult, OP.add)
                rtp = stt(rpv, sc[3][:, 4 * DC:5 * DC], c_selb[:, cp_:cp_ + 1], rpv,
                          OP.mult, OP.add)
            memset(a_in, 0.0)
            memset(b_in, 0.0)
            for j in range(3):
                ts(accD[j], accD[j], c_empt[:, j:j + 1], None, OP.add)
                tt(a_in, a_in, accD[j], OP.mult)
                tt(a_in, a_in, accA[j], OP.add)
                tt(b_in, b_in, accD[j], OP.mult)
                tt(b_in, b_in, accB[j], OP.add)
            em.barrier()

            wkvr = slA
            ts(tiny[:, 30:31], c_mhalo[:, 0:1], -1.0, 1.0, OP.mult, OP.add)
            ts(sc[0][:, 0:DC], b_in, tiny[:, 30:31], None, OP.add)
            recip(sc[0][:, 0:DC], sc[0][:, 0:DC])
            tt(sc[1][:, 0:DC], a_in, sc[0][:, 0:DC], OP.mult)
            tt(sc[1][:, 0:DC], sc[1][:, 0:DC], rpv, OP.mult)
            cp(wkvr[:, :, 0], sc[1][:, 0:DC])
            for ch in range(DC):
                emit_e12(ch)
                rr = em.dma("sy", scb[2][:], sp_r[ch], ch=6)
                scan(sc[5][:], sc[2][:], sc[4][:], a_in[:, ch:ch + 1])
                scan(sc[6][:], sc[2][:], sc[3][:], b_in[:, ch:ch + 1])
                recip(sc[7][:], sc[6][:])
                tt(sc[5][:], sc[5][:], sc[7][:], OP.mult)
                em.wait("v", rr)
                tt(wkvr[:, ch, 1:T + 1], sc[5][:], scb[2][:], OP.mult)
            em.barrier()

            wpar = [None, None]
            for m in range(DC):
                par = m % 2
                em.waits("gp", [wpar[par]])
                twl = em.dma("gp", wv3[par][:],
                             wo_t[m].rearrange("k p c -> p k c"), ch=par)
                for fi, (t0, tn) in enumerate(((1, 512), (513, 512), (0, 1))):
                    bi_ = fi % 2 if tn > 1 else 2
                    bank = pb[bi_]
                    em.waits("te", [twl])
                    pwait(bi_)
                    for kc in range(DC):
                        am = mm(bank[:, 0:tn], wv3[par][:, kc, :],
                                wkvr[:, kc, t0:t0 + tn], kc == 0, kc == DC - 1)
                    em.waits("v", [am])
                    if tn > 1:
                        wpar[par] = stt(x_res[:, m, 3 + t0:3 + t0 + tn],
                                        bank[:, 0:tn], lw("bo", m),
                                        x_res[:, m, 3 + t0:3 + t0 + tn],
                                        OP.add, OP.add)
                    else:
                        ts(abuf[:, m, 1:2], bank[:, 0:1], lw("bo", m), None, OP.add)
                        wpar[par] = tt(abuf[:, m, 1:2], abuf[:, m, 1:2],
                                       ypv[:, m:m + 1], OP.add)
                    pbusy[bi_] = wpar[par]
            em.barrier()

            emit_tok_ln(abuf[:, :, 1], abuf[:, :, 2], "ln2w", "ln2b", slot=8)

            ln2x = slA

            def out_ln2(ch, th0, tn):
                cp(ln2x[:, ch, 1 + th0:1 + th0 + tn], sc[7][:, 0:tn])
            emit_ln(lambda ch: x_res[:, ch, 4:TH], T, out_ln2,
                    affine=(lambda ch: lw("ln2w", ch), lambda ch: lw("ln2b", ch)))
            for ch in range(DC):
                cp(ln2x[:, ch, 0:1], abuf[:, ch, 2:3])
            em.barrier()

            for ch in range(DC):
                emit_mix(ln2x, "cmk", ch, slB[:, ch, :])
            prev = None
            for ch in range(DC):
                em.waits("v", [prev])
                mt = emit_mix(ln2x, "cmr", ch, scb[ch % 2][:])
                em.waits("sy", [mt])
                prev = em.dma("sy", sp_mr[ch], scb[ch % 2][:])
            em.barrier()

            k2b = slA
            wpar = [None, None]
            for m in range(DC):
                par = m % 2
                em.waits("gp", [wpar[par]])
                twl = em.dma("gp", wv3[par][:],
                             wcm_t[m, 0].rearrange("k p c -> p k c"), ch=par)
                for fi, t0 in enumerate((0, 512)):
                    bank = pb[fi]
                    em.waits("te", [twl])
                    pwait(fi)
                    for kc in range(DC):
                        km = mm(bank[:, 0:512], wv3[par][:, kc, :],
                                slB[:, kc, t0:t0 + 512], kc == 0, kc == DC - 1)
                    em.waits("v", [km])
                    em.wait("v", ("s", em.cnt["s"]))
                    rl = ts(sc[0][:, 0:512], bank[:, 0:512], 0.0, None, OP.max)
                    pbusy[fi] = rl
                    em.wait("s", rl)
                    wpar[par] = act(k2b[:, m, t0:t0 + 512], sc[0][:, 0:512],
                                    AF.Square)
            em.barrier()
            rt = None
            for ch in range(DC):
                rt = em.dma("sy", slB[:, ch, :], sp_mr[ch])
            em.wait("v", rt)
            em.wait("te", rt)
            wpar = [None, None]
            for m in range(DC):
                par = m % 2
                em.waits("gp", [wpar[par]])
                twl = em.dma("gp", wv3[par][:],
                             wcm_t[m, 2].rearrange("k p c -> p k c"), ch=par)
                crt = None
                for fi, t0 in enumerate((0, 512)):
                    em.waits("te", [twl])
                    pwait(2 + fi)
                    for kc in range(DC):
                        rm = mm(pb[2 + fi][:, 0:512], wv3[par][:, kc, :],
                                slB[:, kc, t0:t0 + 512], kc == 0, kc == DC - 1)
                    em.waits("s", [rm])
                    crt = act(sc[2][:, t0:t0 + 512], pb[2 + fi][:, 0:512],
                              AF.Sigmoid)
                    pbusy[2 + fi] = crt
                em.waits("gp", [wpar[par], crt])
                twl2 = em.dma("gp", wv3[par][:],
                              wcm_t[m, 1].rearrange("k p c -> p k c"), ch=par)
                for fi, t0 in enumerate((0, 512)):
                    em.waits("te", [twl2, crt])
                    pwait(fi)
                    for kc in range(DC):
                        vm = mm(pb[fi][:, 0:512], wv3[par][:, kc, :],
                                k2b[:, kc, t0:t0 + 512], kc == 0, kc == DC - 1)
                    em.waits("v", [vm, crt])
                    pbusy[fi] = tt(sc[1][:, 0:512], sc[2][:, t0:t0 + 512],
                                   pb[fi][:, 0:512], OP.mult)
                    wpar[par] = tt(x_res[:, m, 4 + t0:4 + t0 + 512],
                                   x_res[:, m, 4 + t0:4 + t0 + 512],
                                   sc[1][:, 0:512], OP.add)
            em.barrier()
            prevs = [None, None]
            for ch in range(DC):
                em.waits("sy", [prevs[ch % 2]])
                rt = em.dma("sy", sc[ch % 2][:], sp_x2[ch])
                em.wait("v", rt)
                prevs[ch % 2] = tt(x_res[:, ch, 4:TH], x_res[:, ch, 4:TH],
                                   sc[ch % 2][:], OP.add)
            em.barrier()

        if upto <= 3:
            fin = em.dma("sy", outT, x_res[:, :, 4:TH])
            em.waits("gp", [fin])
            em.flush()
            return nc

        with contextlib.ExitStack() as ctx4:
            rows = sbuf("rows", [1, T, 6], F32, ctx=ctx4)
            qb = sbuf("q1b", [128, DC, T], BF16, ctx=ctx4)
            tmax = sbuf("tmax", [128, 8, 2], F32, ctx=ctx4)
            qi8 = sbuf("qi8", [128, T], I8, ctx=ctx4)

            with contextlib.ExitStack() as ctx4s:
                wab = [sbuf("wab%d" % i, [128, DC, 128], F32, ctx=ctx4s)
                       for i in range(2)]
                memset(tiny[:, 8:11], 0.0)
                srcs = ([(w12_t[hm, half], 8 + half)
                         for hm in range(HC) for half in range(2)] +
                        [(w3_t[kc], 10) for kc in range(HC)])
                tls = [None, None]
                tls[0] = em.dma("s", wab[0][:],
                                srcs[0][0].rearrange("k p c -> p k c"), ch=4)
                for j, (sap, col) in enumerate(srcs):
                    par = j % 2
                    if j + 1 < len(srcs):
                        tls[(j + 1) % 2] = em.dma(
                            "s", wab[(j + 1) % 2][:],
                            srcs[j + 1][0].rearrange("k p c -> p k c"),
                            ch=4 + (j + 1) % 2)
                    em.wait("s", tls[par])
                    for seg in range(2):
                        a1 = act(sc[0][:],
                                 wab[par][:, seg * 8:(seg + 1) * 8]
                                 .rearrange("p k c -> p (k c)"),
                                 AF.Abs, accum=tiny[:, 12:13])
                        em.wait("v", a1)
                        tt(tiny[:, col:col + 1], tiny[:, col:col + 1],
                           tiny[:, 12:13], OP.add)
                em.barrier()
            t1 = mm(pb[4][0:1, 0:3], c_ocf[:, 0:1], tiny[:, 8:11], True, True)
            em.wait("v", t1)
            ts(trow[0:1, 8:11], pb[4][0:1, 0:3], 1.0 / (D * H), None, OP.mult)
            ts(trow[0:1, 11:14], trow[0:1, 8:11], float(EPS_RMS), None, OP.add)
            ivt = recip(trow[0:1, 14:17], trow[0:1, 11:14])
            em.wait("s", ivt)
            act(trow[0:1, 17:20], trow[0:1, 8:11], AF.Ln, bias=1.0)
            nt = ts(trow[0:1, 20:23], trow[0:1, 8:11], -1.0, 1.0, OP.mult, OP.add)
            em.wait("s", nt)
            lt2 = act(trow[0:1, 23:26], trow[0:1, 20:23], AF.Ln)
            em.wait("v", lt2)
            tt(trow[0:1, 26:29], trow[0:1, 17:20], trow[0:1, 23:26], OP.subtract)
            al = ts(trow[0:1, 26:29], trow[0:1, 26:29], 0.5, None, OP.mult)
            em.waits("te", [ivt, al])
            bi_ = mm(pb[4][:, 0:9], c_o1b[0:1, :], trow[0:1, 8:17], True, True)
            em.wait("v", bi_)
            cp(tiny[:, 16:25], pb[4][:, 0:9])
            em.barrier()

            for th0 in range(0, T, 512):
                memset(rows[0:1, th0:th0 + 512, 0], 0.0)
                memset(lnr[0:1, 0:512, 1], 0.0)
                mmprev = [None, None]
                for ch in range(DC):
                    em.wait("s", mmprev[ch % 2])
                    ctok = cp(scb[ch % 2][:, 0:512],
                              x_res[:, ch, 4 + th0:4 + th0 + 512], eng="s")
                    sqtok = act(scb[2 + ch % 2][:, 0:512], scb[ch % 2][:, 0:512],
                                AF.Square)
                    em.wait("te", ctok)
                    pwait(4)
                    t1 = mm(pb[4][0:1, 0:512], c_ocb[:, 0:1], scb[ch % 2][:, 0:512],
                            True, True)
                    em.wait("te", sqtok)
                    pwait(5)
                    t2 = mm(pb[5][0:1, 0:512], c_ocb[:, 0:1],
                            scb[2 + ch % 2][:, 0:512], True, True)
                    em.waits("v", [t1, t2])
                    mmprev[ch % 2] = t2
                    pbusy[4] = tt(rows[0:1, th0:th0 + 512, 0],
                                  rows[0:1, th0:th0 + 512, 0], pb[4][0:1, 0:512],
                                  OP.add)
                    pbusy[5] = tt(lnr[0:1, 0:512, 1], lnr[0:1, 0:512, 1],
                                  pb[5][0:1, 0:512], OP.add)
                ts(rows[0:1, th0:th0 + 512, 0], rows[0:1, th0:th0 + 512, 0],
                   1.0 / D, None, OP.mult)
                ts(lnr[0:1, 0:512, 1], lnr[0:1, 0:512, 1], 1.0 / D, None,
                   OP.mult)
                tt(lnr[0:1, 0:512, 2], rows[0:1, th0:th0 + 512, 0],
                   rows[0:1, th0:th0 + 512, 0], OP.mult)
                tt(lnr[0:1, 0:512, 2], lnr[0:1, 0:512, 1],
                   lnr[0:1, 0:512, 2], OP.subtract)
                vtok = ts(lnr[0:1, 0:512, 3], lnr[0:1, 0:512, 2],
                          float(EPS_LN), None, OP.add)
                em.wait("s", vtok)
                sq = act(lnr[0:1, 0:512, 3], lnr[0:1, 0:512, 3],
                         AF.Sqrt)
                em.wait("v", sq)
                recip(rows[0:1, th0:th0 + 512, 1], lnr[0:1, 0:512, 3])
                tt(rows[0:1, th0:th0 + 512, 2], rows[0:1, th0:th0 + 512, 1],
                   rows[0:1, th0:th0 + 512, 1], OP.mult)
                tt(rows[0:1, th0:th0 + 512, 2], rows[0:1, th0:th0 + 512, 2],
                   lnr[0:1, 0:512, 2], OP.mult)
                vt2 = ts(lnr[0:1, 0:512, 4], rows[0:1, th0:th0 + 512, 2],
                         float(EPS_RMS), None, OP.add)
                em.wait("s", vt2)
                sq3 = act(lnr[0:1, 0:512, 4], lnr[0:1, 0:512, 4],
                          AF.Sqrt)
                em.wait("v", sq3)
                recip(rows[0:1, th0:th0 + 512, 2], lnr[0:1, 0:512, 4])
            em.barrier()

            memset(tmax[:], 0.0)
            rtk = mxk = None
            em.wait("gp", ("v", em.cnt["v"]))
            for th0 in range(0, T, 512):
                pwait(4)
                bmu = mm(pb[4][:, 0:512], c_o1b[0:1, :], rows[0:1, th0:th0 + 512, 0],
                         True, True)
                pwait(5)
                brv = mm(pb[5][:, 0:512], c_o1b[0:1, :], rows[0:1, th0:th0 + 512, 1],
                         True, True)
                em.waits("v", [bmu, brv])
                for ch in range(DC):
                    tt(sc[0][:, 0:512], x_res[:, ch, 4 + th0:4 + th0 + 512],
                       pb[4][:, 0:512], OP.subtract)
                    xh = tt(sc[0][:, 0:512], sc[0][:, 0:512], pb[5][:, 0:512],
                            OP.mult)
                    pbusy[4] = xh
                    pbusy[5] = xh
                    for br in range(2):
                        gcol = c_g1[:, ch:ch + 1] if br == 0 else c_g2[:, ch:ch + 1]
                        gx = ts(scb[br][:, 0:512], sc[0][:, 0:512], gcol, None,
                                OP.mult)
                        for tb in range(4):
                            tj = th0 // 128 + tb
                            em.waits("te", [gx, rtk, mxk])
                            trt = tpose(pbt[:], scb[br][:, tb * 128:(tb + 1) * 128])
                            em.wait("v", trt)
                            em.wait("v", mxk)
                            rtk = red(sc[5][:, 0:1], pbt[:], OP.max, absval=True)
                            em.wait("gp", rtk)
                            mxk = tt(tmax[:, tj, br:br + 1],
                                     tmax[:, tj, br:br + 1],
                                     sc[5][:, 0:1], OP.max)
            em.barrier()
            mt = None
            for br in range(2):
                for tb in range(8):
                    mt = em.dma("gp", rows[0:1, tb * 128:(tb + 1) * 128, 4 + br],
                                tmax[:, tb, br:br + 1], slow=True)
            em.wait("v", mt)
            for br in range(2):
                tt(rows[0:1, 0:T, 4 + br], rows[0:1, 0:T, 4 + br],
                   rows[0:1, 0:T, 2], OP.mult)
                ts(rows[0:1, 0:T, 4 + br], rows[0:1, 0:T, 4 + br], 1e-4, None,
                   OP.max)
            em.barrier()

            def emit_qbuild(br):
                gc = c_g1 if br == 0 else c_g2
                for th0 in range(0, T, 512):
                    pwait(4)
                    bmu = mm(pb[4][:, 0:512], c_o1b[0:1, :],
                             rows[0:1, th0:th0 + 512, 0], True, True)
                    pwait(5)
                    brv = mm(pb[5][:, 0:512], c_o1b[0:1, :],
                             rows[0:1, th0:th0 + 512, 1], True, True)
                    recip(lnr[0:1, 0:512, 0], rows[0:1, th0:th0 + 512, 4 + br])
                    tt(lnr[0:1, 0:512, 0], lnr[0:1, 0:512, 0],
                       rows[0:1, th0:th0 + 512, 2], OP.mult)
                    sf = ts(lnr[0:1, 0:512, 0], lnr[0:1, 0:512, 0],
                            127.0, None, OP.mult)
                    em.waits("te", [sf])
                    pwait(0)
                    bsf = mm(pb[0][:, 0:512], c_o1b[0:1, :],
                             lnr[0:1, 0:512, 0], True, True)
                    em.waits("v", [bmu, brv, bsf])
                    lastq = None
                    for ch in range(DC):
                        tt(sc[0][:, 0:512], x_res[:, ch, 4 + th0:4 + th0 + 512],
                           pb[4][:, 0:512], OP.subtract)
                        tt(sc[0][:, 0:512], sc[0][:, 0:512], pb[5][:, 0:512],
                           OP.mult)
                        ts(sc[0][:, 0:512], sc[0][:, 0:512], gc[:, ch:ch + 1],
                           None, OP.mult)
                        lastq = tt(sc[0][:, 0:512], sc[0][:, 0:512],
                                   pb[0][:, 0:512], OP.mult)
                        cp(qi8[:, 0:512], sc[0][:, 0:512])
                        cp(qb[:, ch, th0:th0 + 512], qi8[:, 0:512])
                    pbusy[0] = lastq
                    pbusy[4] = lastq
                    pbusy[5] = lastq
                em.barrier()
                ts(rows[0:1, 0:T, 4 + br], rows[0:1, 0:T, 4 + br], 1.0 / 127.0,
                   None, OP.mult)
                ts(rows[0:1, 0:T, 4 + br], rows[0:1, 0:T, 4 + br],
                   trow[0:1, 26 + br:27 + br], None, OP.mult)
                em.barrier()

            def emit_w12pass(br):
                with contextlib.ExitStack() as ctxw:
                    w12s = [sbuf("w12sh%d_%d" % (br, i), [128, DC, 128], F32,
                                 ctx=ctxw) for i in range(2)]
                    w12b = sbuf("w12bh%d" % br, [128, DC, 128], BF16, ctx=ctxw)
                    dsr = sbuf("dsr%d" % br, [128, T], F32, ctx=ctxw)
                    for fi, t0 in enumerate((0, 512)):
                        pwait(4)
                        dd = mm(pb[4][:, 0:512], c_o1b[0:1, :],
                                rows[0:1, t0:t0 + 512, 4 + br], True, True)
                        em.waits("v", [dd])
                        pbusy[4] = cp(dsr[:, t0:t0 + 512], pb[4][:, 0:512])
                    hprev = None
                    mm_prev = None
                    tls = [None, None]
                    tls[0] = em.dma("s", w12s[0][:],
                                    w12_t[0, br].rearrange("k p c -> p k c"),
                                    ch=4)
                    for hm in range(HC):
                        par = hm % 2
                        if hm + 1 < HC:
                            tls[(hm + 1) % 2] = em.dma(
                                "s", w12s[(hm + 1) % 2][:],
                                w12_t[hm + 1, br].rearrange("k p c -> p k c"),
                                ch=4 + (hm + 1) % 2)
                        em.wait("s", tls[par])
                        em.waits("v", [mm_prev])
                        bt = None
                        for seg in range(2):
                            th_ = act(sc[0][:],
                                      w12s[par][:, seg * 8:(seg + 1) * 8]
                                      .rearrange("p k c -> p (k c)"), AF.Tanh,
                                      scale=tiny[:, 22 + br:23 + br])
                            em.wait("v", th_)
                            cp(qi8[:], sc[0][:])
                            bt = cp(w12b[:, seg * 8:(seg + 1) * 8]
                                    .rearrange("p k c -> p (k c)"), qi8[:])
                        em.waits("te", [bt])
                        ht = None
                        for fi, t0 in enumerate((0, 512)):
                            pwait(fi)
                            m1 = None
                            for kc in range(DC):
                                m1 = mm(pb[fi][:, 0:512], w12b[:, kc, :],
                                        qb[:, kc, t0:t0 + 512], kc == 0,
                                        kc == DC - 1)
                            em.waits("v", [m1])
                            lb = tt(sc[0][:, 0:512], pb[fi][:, 0:512],
                                    dsr[:, t0:t0 + 512], OP.mult)
                            pbusy[fi] = lb
                            if br == 0:
                                em.wait("s", lb)
                                sw = act(sc[1][:, 0:512], sc[0][:, 0:512],
                                         AF.Silu)
                                em.waits("v", [sw])
                                ht = cp(scb[2][:, t0:t0 + 512], sc[1][:, 0:512])
                            else:
                                rr = em.dma("sy", scb[3][:, t0:t0 + 512],
                                            sp_s1[hm, :, t0:t0 + 512], ch=7)
                                em.wait("v", rr)
                                ht = tt(scb[2][:, t0:t0 + 512],
                                        sc[0][:, 0:512],
                                        scb[3][:, t0:t0 + 512], OP.mult)
                        mm_prev = ("te", em.cnt["te"])
                        dst = sp_s1 if br == 0 else sp_h
                        em.waits("sy", [hprev, ht])
                        hprev = em.dma("sy", dst[hm], scb[2][:], ch=6)
                        em.wait("v", hprev)
                    em.barrier()

            emit_qbuild(0)
            emit_w12pass(0)
            emit_qbuild(1)
            emit_w12pass(1)

        with contextlib.ExitStack() as ctx4B:
            rowsb = sbuf("rowsb", [1, T, 3], F32, ctx=ctx4B)
            tmaxb = sbuf("tmaxb", [128, 8, 1], F32, ctx=ctx4B)
            qi8 = sbuf("qi8b", [128, T], I8, ctx=ctx4B)

            with contextlib.ExitStack() as ctx4h:
                hrow = sbuf("hrow2", [1, T, 2], F32, ctx=ctx4h)
                memset(tmaxb[:, :, 0], 0.0)
                memset(hrow[0:1, :, 0], 0.0)
                rtk = mxk = None
                em.wait("gp", ("v", em.cnt["v"]))
                for hm in range(HC):
                    em.wait("sy", ("v", em.cnt["v"]))
                    em.wait("sy", ("s", em.cnt["s"]))
                    hr = em.dma("sy", scb[2][:], sp_h[hm])
                    em.wait("s", hr)
                    for fi, t0 in enumerate((0, 512)):
                        em.wait("s", ("te", em.cnt["te"]))
                        hsq = act(scb[3][:, 0:512], scb[2][:, t0:t0 + 512],
                                  AF.Square)
                        em.wait("te", hsq)
                        pwait(4)
                        s1 = mm(pb[4][0:1, 0:512], c_ocb[:, 0:1], scb[3][:, 0:512],
                                True, True)
                        em.waits("v", [s1])
                        pbusy[4] = tt(hrow[0:1, t0:t0 + 512, 0],
                                      hrow[0:1, t0:t0 + 512, 0], pb[4][0:1, 0:512],
                                      OP.add)
                        gx = ts(scb[3][:, 0:512], scb[2][:, t0:t0 + 512],
                                c_g3[:, hm:hm + 1], None, OP.mult)
                        for tb in range(4):
                            tj = t0 // 128 + tb
                            em.waits("te", [gx, rtk, mxk])
                            trt = tpose(pbt[:], scb[3][:, tb * 128:(tb + 1) * 128])
                            em.wait("v", trt)
                            em.wait("v", mxk)
                            rtk = red(sc[5][:, 0:1], pbt[:], OP.max, absval=True)
                            em.wait("gp", rtk)
                            mxk = tt(tmaxb[:, tj, 0:1], tmaxb[:, tj, 0:1],
                                    sc[5][:, 0:1], OP.max)
                ts(hrow[0:1, :, 0], hrow[0:1, :, 0], 1.0 / H, None, OP.mult)
                vh = ts(hrow[0:1, :, 1], hrow[0:1, :, 0], float(EPS_RMS), None,
                        OP.add)
                em.wait("s", vh)
                sqh = act(hrow[0:1, :, 1], hrow[0:1, :, 1], AF.Sqrt)
                em.wait("v", sqh)
                recip(rowsb[0:1, 0:T, 0], hrow[0:1, :, 1])
                em.waits("gp", [mxk])
                for tb in range(8):
                    mt = em.dma("gp", rowsb[0:1, tb * 128:(tb + 1) * 128, 1],
                                tmaxb[:, tb, 0:1], slow=True)
                em.wait("v", mt)
                tt(rowsb[0:1, 0:T, 1], rowsb[0:1, 0:T, 1], rowsb[0:1, 0:T, 0], OP.mult)
                ts(rowsb[0:1, 0:T, 1], rowsb[0:1, 0:T, 1], 1e-4, None, OP.max)
                recip(rowsb[0:1, 0:T, 2], rowsb[0:1, 0:T, 1])
                tt(rowsb[0:1, 0:T, 2], rowsb[0:1, 0:T, 2], rowsb[0:1, 0:T, 0], OP.mult)
                ts(rowsb[0:1, 0:T, 2], rowsb[0:1, 0:T, 2], 127.0, None, OP.mult)
                ts(rowsb[0:1, 0:T, 1], rowsb[0:1, 0:T, 1], 1.0 / 127.0, None, OP.mult)
                ts(rowsb[0:1, 0:T, 1], rowsb[0:1, 0:T, 1],
                   trow[0:1, 28:29], None, OP.mult)
                em.barrier()

            with contextlib.ExitStack() as ctx4x:
                w3s = [sbuf("w3sl%d" % i, [128, DC, 128], F32, ctx=ctx4x)
                       for i in range(2)]
                w3g = [sbuf(f"w3g{i}", [128, DC, 128], BF16, ctx=ctx4x)
                       for i in range(4)]
                q3g = [sbuf(f"q3g{i}", [128, T], BF16, ctx=ctx4x) for i in range(4)]
                d3s = sbuf("d3s", [128, T], F32, ctx=ctx4x)
                pwait(2)
                pwait(3)
                pwait(4)
                pwait(5)
                b1 = mm(pb[2][:, 0:512], c_o1b[0:1, :], rowsb[0:1, 0:512, 2], True,
                        True)
                b2 = mm(pb[3][:, 0:512], c_o1b[0:1, :], rowsb[0:1, 512:1024, 2],
                        True, True)
                d1 = mm(pb[4][:, 0:512], c_o1b[0:1, :], rowsb[0:1, 0:512, 1], True,
                        True)
                d2 = mm(pb[5][:, 0:512], c_o1b[0:1, :], rowsb[0:1, 512:1024, 1],
                        True, True)
                em.waits("v", [b1, b2, d1, d2])
                cp(d3s[:, 0:512], pb[4][:, 0:512])
                pbusy[4] = pbusy[5] = cp(d3s[:, 512:1024], pb[5][:, 0:512])
                tls = [None, None]
                tls[0] = em.dma("s", w3s[0][:],
                                w3_t[0].rearrange("k p c -> p k c"), ch=4)
                for g in range(16):
                    for ki in range(4):
                        kc = g * 4 + ki
                        par = kc % 2
                        if kc + 1 < HC:
                            tls[(kc + 1) % 2] = em.dma(
                                "s", w3s[(kc + 1) % 2][:],
                                w3_t[kc + 1].rearrange("k p c -> p k c"),
                                ch=4 + (kc + 1) % 2)
                        em.wait("s", tls[par])
                        for seg in range(2):
                            th_ = act(sc[0][:],
                                      w3s[par][:, seg * 8:(seg + 1) * 8]
                                      .rearrange("p k c -> p (k c)"), AF.Tanh,
                                      scale=tiny[:, 24:25])
                            em.wait("v", th_)
                            cp(qi8[:], sc[0][:])
                            cp(w3g[ki][:, seg * 8:(seg + 1) * 8]
                                           .rearrange("p k c -> p (k c)"), qi8[:])
                        em.wait("sy", ("v", em.cnt["v"]))
                        hr = em.dma("sy", scb[2][:], sp_h[kc], ch=7)
                        em.wait("v", hr)
                        for fi, t0 in enumerate((0, 512)):
                            ts(sc[1][:, 0:512], scb[2][:, t0:t0 + 512],
                               c_g3[:, kc:kc + 1], None, OP.mult)
                            tt(sc[1][:, 0:512], sc[1][:, 0:512],
                               pb[2 + fi][:, 0:512], OP.mult)
                            cp(qi8[:, 0:512], sc[1][:, 0:512])
                            cp(q3g[ki][:, t0:t0 + 512], qi8[:, 0:512])
                    qtok = ("v", em.cnt["v"])
                    for mp in range(DC):
                        for fi, t0 in enumerate((0, 512)):
                            bidx = (mp * 2 + fi) % 2
                            bank = pb[bidx]
                            em.waits("te", [qtok])
                            pwait(bidx)
                            for ki in range(4):
                                wm = mm(bank[:, 0:512], w3g[ki][:, mp, :],
                                        q3g[ki][:, t0:t0 + 512], ki == 0, ki == 3)
                            em.waits("v", [wm])
                            pbusy[bidx] = tt(sc[3][:, 0:512], bank[:, 0:512],
                                             d3s[:, t0:t0 + 512], OP.mult)
                            tt(x_res[:, mp, 4 + t0:4 + t0 + 512],
                               x_res[:, mp, 4 + t0:4 + t0 + 512], sc[3][:, 0:512],
                               OP.add)
                    em.barrier()

        fin = em.dma("sy", outT, x_res[:, :, 4:TH])
        em.waits("gp", [fin])
        em.flush()
    return nc


def _prep_inputs(inp):
    C = np.ascontiguousarray
    d = {}
    W_in = inp["L0_attn_W_in"]
    d["win_t"] = C(W_in.reshape(73, 128, 16, 128).transpose(0, 2, 3, 1))
    d["wdt_t"] = C(inp["L0_attn_W_dt"].reshape(32, 128, 128).transpose(0, 2, 1))
    d["alog"] = C(inp["L0_attn_A_log"].reshape(32, 128, 16).transpose(1, 0, 2))
    d["bdt"] = C(inp["L0_attn_b_dt"].reshape(32, 128).T)
    d["convw"] = C(inp["L0_attn_conv_w"][:, 0, :].reshape(32, 128, 4)
                   .transpose(1, 0, 2))
    d["convb"] = C(inp["L0_attn_conv_b"].reshape(32, 128).T)
    d["dpar"] = C(inp["L0_attn_D_param"].reshape(32, 128).T)
    d["wout_t"] = C(inp["L0_attn_W_out"].reshape(16, 128, 32, 128)
                    .transpose(0, 2, 3, 1))
    wg = inp["L0_ffn_w_gate"].reshape(16, 128, 64, 128).transpose(2, 0, 1, 3)
    wu = inp["L0_ffn_w_up"].reshape(16, 128, 64, 128).transpose(2, 0, 1, 3)
    d["wgu_t"] = C(np.stack([wg, wu], axis=1))
    d["wdn_t"] = C(inp["L0_ffn_w_down"].reshape(64, 128, 16, 128)
                   .transpose(0, 2, 1, 3))
    d["sgate"] = C(inp["L0_ffn_s_gate"].reshape(64, 128).T)
    d["sup"] = C(inp["L0_ffn_s_up"].reshape(64, 128).T)
    d["sdown"] = C(inp["L0_ffn_s_down"].reshape(16, 128).T)
    vecs = [inp["L1_attn_" + n] for n in
            ["ln1_w", "ln1_b", "ln2_w", "ln2_b", "tm_u", "tm_w", "tm_mix_k",
             "tm_mix_v", "tm_mix_r", "cm_mix_k", "cm_mix_r", "tm_bo"]]
    d["l1v"] = C(np.stack(vecs, 0).reshape(12, 16, 128).transpose(2, 1, 0))
    tile2k = lambda w: w.reshape(16, 128, 16, 128).transpose(0, 2, 3, 1)
    d["wkvr_t"] = C(np.stack([tile2k(inp["L1_attn_tm_Wk"]),
                              tile2k(inp["L1_attn_tm_Wv"]),
                              tile2k(inp["L1_attn_tm_Wr"])], axis=1))
    d["wo_t"] = C(tile2k(inp["L1_attn_tm_Wo"]))
    d["wcm_t"] = C(np.stack([tile2k(inp["L1_attn_cm_Wk"]),
                             tile2k(inp["L1_attn_cm_Wv"]),
                             tile2k(inp["L1_attn_cm_Wr"])], axis=1))
    w1 = inp["L1_ffn_w1"].reshape(64, 128, 16, 128).transpose(0, 2, 3, 1)
    w2 = inp["L1_ffn_w2"].reshape(64, 128, 16, 128).transpose(0, 2, 3, 1)
    d["w12_t"] = C(np.stack([w1, w2], axis=1))
    d["w3_t"] = C(inp["L1_ffn_w3"].reshape(16, 128, 64, 128).transpose(2, 0, 3, 1))
    d["g1i"] = C(inp["L1_ffn_g1"].reshape(16, 128).T)
    d["g2i"] = C(inp["L1_ffn_g2"].reshape(16, 128).T)
    d["g3i"] = C(inp["L1_ffn_g3"].reshape(64, 128).T)
    d["ident"] = np.eye(128, dtype=BF)
    s16 = np.zeros((16, 128), dtype=BF)
    for p in range(128):
        s16[p % 16, p] = 1
    d["sel16"] = s16
    o16 = np.zeros((16, 16, 128), dtype=BF)
    for n in range(16):
        o16[n, n, :] = 1
    d["oh16"] = C(o16)
    slc = np.zeros((128, 16, 128), dtype=BF)
    for n in range(16):
        for tgt in range(128):
            slc[(tgt // 16) * 16 + n, n, tgt] = 1
    d["selc"] = slc
    d["onesc_b"] = np.ones((128, 1), dtype=BF)
    d["onesc_f"] = np.ones((128, 1), dtype=np.float32)
    d["ones1_b"] = np.ones((1, 128), dtype=np.float32)
    return d


def kernel(**inp):
    upto = int(inp.pop("_upto", 99))
    dbg = bool(inp.pop("_dbg", False))
    x = np.asarray(inp["L0_attn_hidden_states"], dtype=np.float32)
    shared = _prep_inputs({k: np.asarray(v) for k, v in inp.items()})
    nc = build_nc(upto=upto, dbg=dbg)
    in_maps = []
    for core in range(NCORE):
        b, q = core // 4, core % 4
        m = dict(shared)
        t0 = q * T
        xp_ = np.zeros((TH, D), np.float32)
        lo = max(t0 - 4, 0)
        xp_[4 - (t0 - lo):] = x[b, lo:t0 + T]
        m["x0c"] = np.ascontiguousarray(
            xp_.T.reshape(16, 128, TH).transpose(1, 0, 2))
        sp = np.zeros((128, 3, 8), np.float32)
        ep = np.zeros((128, 3), np.float32)
        for j in range(3):
            if j < q:
                sp[:, j, b * 4 + j] = 1.0
            else:
                ep[:, j] = 1.0
        m["selp"] = sp
        m["empt"] = ep
        sb_ = np.zeros((128, 8), np.float32)
        if q > 0:
            sb_[:, b * 4 + q - 1] = 1.0
        m["selb"] = sb_
        m["mhalo"] = np.full((128, 1), 1.0 if q > 0 else 0.0, np.float32)
        in_maps.append(m)
    res = run_bass_kernel_spmd(nc, in_maps, core_ids=list(range(NCORE)))
    out = np.empty((2, 4096, D), np.float32)
    for core in range(NCORE):
        b, q = core // 4, core % 4
        o = res.results[core]["outT"]
        out[b, q * T:(q + 1) * T] = o.transpose(1, 0, 2).reshape(D, T).T
    return out
    print("saved", out.shape)


# version 20 (final)
# speedup vs baseline: 1.2064x; 1.0826x over previous
import contextlib
import numpy as np
import ml_dtypes
import concourse.bass as bass
import concourse.mybir as mybir
from concourse.bass_utils import run_bass_kernel_spmd

F32 = mybir.dt.float32
BF16 = mybir.dt.bfloat16
I8 = mybir.dt.int8
AF = mybir.ActivationFunctionType
OP = mybir.AluOpType
AX = mybir.AxisListType
BF = ml_dtypes.bfloat16

D = 2048
DI = 4096
NST = 16
T = 1024
TH = 1028
NCORE = 8
DC = 16
DIC = 32
HC = 64
H = 8192
EPS_LN = 1e-5
EPS_RMS = 1e-8

ENGINES = ("te", "v", "s", "gp", "sy")
ENG_ATTR = {"te": "tensor", "v": "vector", "s": "scalar", "gp": "gpsimd", "sy": "sync"}
L1V = ["ln1w", "ln1b", "ln2w", "ln2b", "tmu", "tmw", "mk", "mv", "mr", "cmk", "cmr", "bo"]


NCH = 8


class Emit:
    def __init__(self, nc, ctx):
        self.nc = nc
        self.ops = {e: [] for e in ENGINES}
        self.sems = {e: ctx.enter_context(nc.semaphore("sem_" + e)) for e in ENGINES}
        for i in range(NCH):
            self.sems["ch%d" % i] = ctx.enter_context(
                nc.semaphore("sem_ch%d" % i))
        self.cnt = {e: 0 for e in self.sems}
        self.lastw = {e: {o: 0 for o in self.sems} for e in ENGINES}

    def op(self, eng, fn, dma=False):
        n = 16 if dma else 1
        self.cnt[eng] += n
        tok = (eng, self.cnt[eng])
        sem = self.sems[eng]

        def wrapped(e, fn=fn, sem=sem, n=n):
            fn(e).then_inc(sem, n)
        self.ops[eng].append(wrapped)
        return tok

    def dma(self, eng, out, in_, slow=False, ch=None):
        kw = {"allow_slow_non_contiguous": True} if slow else {}
        if ch is None:
            tok = self.op(eng, lambda e, o=out, i=in_, kw=kw: e.dma_start(
                out=o, in_=i, **kw), dma=True)
            self.wait(eng, tok)
            return tok
        name = "ch%d" % ch
        self.cnt[name] += 16
        tok = (name, self.cnt[name])
        sem = self.sems[name]

        def wrapped(e, o=out, i=in_, kw=kw, sem=sem):
            e.dma_start(out=o, in_=i, **kw).then_inc(sem, 16)
        self.ops[eng].append(wrapped)
        return tok

    def wait(self, eng, tok):
        if tok is None:
            return
        s, v = tok
        if self.lastw[eng][s] >= v:
            return
        self.lastw[eng][s] = v
        sem = self.sems[s]
        self.ops[eng].append(lambda e, sem=sem, v=v: e.wait_ge(sem, v))

    def waits(self, eng, toks):
        best = {}
        for t in toks:
            if t is None:
                continue
            s, v = t
            best[s] = max(best.get(s, 0), v)
        for s, v in best.items():
            self.wait(eng, (s, v))

    def barrier(self):
        snap = dict(self.cnt)
        for e in ENGINES:
            for o in self.sems:
                if o != e and snap[o] > 0:
                    self.wait(e, (o, snap[o]))

    def flush(self):
        with self.nc.Block() as block:
            for eng in ENGINES:
                ops = self.ops[eng]
                if not ops:
                    continue

                def run(e, ops=tuple(ops)):
                    for f in ops:
                        f(e)
                getattr(block, ENG_ATTR[eng])(run)
            self.ops = {e: [] for e in ENGINES}


def build_nc(upto=99, dbg=False):
    nc = bass.Bass("TRN2", target_bir_lowering=False, debug=False)
    dtr = nc.dram_tensor

    def inp(name, shape, dtype=F32):
        return dtr(name, shape, dtype, kind="ExternalInput").ap()

    x0c = inp("x0c", [128, DC, TH])
    win_t = inp("win_t", [73, 16, 128, 128])
    wdt_t = inp("wdt_t", [DIC, 128, 128])
    alog = inp("alog", [128, DIC, NST])
    bdt = inp("bdt", [128, DIC])
    convw = inp("convw", [128, DIC, 4])
    convb = inp("convb", [128, DIC])
    dpar = inp("dpar", [128, DIC])
    wout_t = inp("wout_t", [DC, DIC, 128, 128])
    wgu_t = inp("wgu_t", [HC, 2, DC, 128, 128], I8)
    wdn_t = inp("wdn_t", [HC, DC, 128, 128], I8)
    sgate = inp("sgate", [128, HC])
    sup = inp("sup", [128, HC])
    sdown = inp("sdown", [128, DC])
    l1v = inp("l1v", [128, DC, 12])
    wkvr_t = inp("wkvr_t", [DC, 3, DC, 128, 128])
    wo_t = inp("wo_t", [DC, DC, 128, 128])
    wcm_t = inp("wcm_t", [DC, 3, DC, 128, 128])
    w12_t = inp("w12_t", [HC, 2, DC, 128, 128])
    w3_t = inp("w3_t", [HC, DC, 128, 128])
    g1i = inp("g1i", [128, DC])
    g2i = inp("g2i", [128, DC])
    g3i = inp("g3i", [128, HC])
    ident = inp("ident", [128, 128], BF16)
    sel16 = inp("sel16", [16, 128], BF16)
    oh16 = inp("oh16", [16, NST, 128], BF16)
    selc = inp("selc", [128, NST, 128], BF16)
    onesc_b = inp("onesc_b", [128, 1], BF16)
    onesc_f = inp("onesc_f", [128, 1])
    ones1_b = inp("ones1_b", [1, 128])
    selp = inp("selp", [128, 3, 8])
    empt = inp("empt", [128, 3])
    selb = inp("selb", [128, 8])
    mhalo = inp("mhalo", [128, 1])

    outT = dtr("outT", [128, DC, T], F32, kind="ExternalOutput").ap()
    kd = {"kind": "ExternalOutput"} if dbg else {}
    dbg_sini = dtr("dbg_sini", [128, 512], F32, **kd).ap() if dbg else None
    dbg_cc1 = dtr("dbg_cc1", [NCORE * 128, 1024], F32, **kd).ap() if dbg else None
    dbg_cc1i = dtr("dbg_cc1i", [128, 1024], F32, **kd).ap() if dbg else None
    dbg_wl = dtr("dbg_wl", [128, DIC, NST], F32, **kd).ap() if dbg else None
    dbg_wg = dtr("dbg_wg", [128, DIC, NST], F32, **kd).ap() if dbg else None

    sp_z = dtr("sp_z", [DIC, 128, T], BF16, **kd).ap()
    sp_u = dtr("sp_u", [DIC, 128, T], BF16, **kd).ap()
    sp_cm = dtr("sp_cm", [DIC, 128, T], BF16, **kd).ap()
    sp_y = dtr("sp_y", [DIC, 128, T], BF16, **kd).ap()
    sp_k = dtr("sp_k", [DC, 128, T], BF16).ap()
    sp_v = dtr("sp_v", [DC, 128, T], BF16).ap()
    sp_r = dtr("sp_r", [DC, 128, T], BF16).ap()
    sp_mr = dtr("sp_mr", [DC, 128, T], BF16).ap()
    sp_x2 = dtr("sp_x2", [DC, 128, T], F32).ap()
    sp_h = dtr("sp_h", [HC, 128, T], BF16).ap()
    sp_s1 = dtr("sp_s1", [HC, 128, T], BF16).ap()
    sp_tm = dtr("sp_tm", [128, 8], F32).ap()
    cc1i = dtr("cc1i", [128, 1024], F32).ap()
    cc1o = dtr("cc1o", [NCORE * 128, 1024], F32).ap()
    cc2i = dtr("cc2i", [128, DC], F32).ap()
    cc2o = dtr("cc2o", [NCORE * 128, DC], F32).ap()
    cc3i = dtr("cc3i", [128, 5 * DC], F32).ap()
    cc3o = dtr("cc3o", [NCORE * 128, 5 * DC], F32).ap()

    octx = contextlib.ExitStack()
    with octx:
        def sbuf(name, shape, dtp=F32, ctx=octx):
            return ctx.enter_context(nc.sbuf_tensor(name, shape, dtp))
        em = Emit(nc, octx)

        x_res = sbuf("x_res", [128, DC, TH])
        c_l1 = sbuf("c_l1", [128, DC, 12])
        c_nw = sbuf("c_nw", [128, DC])
        c_nwT = sbuf("c_nwT", [128, DC])
        c_sg = sbuf("c_sg", [128, HC]); c_su = sbuf("c_su", [128, HC])
        c_sd = sbuf("c_sd", [128, DC])
        c_g1 = sbuf("c_g1", [128, DC]); c_g2 = sbuf("c_g2", [128, DC])
        c_g3 = sbuf("c_g3", [128, HC])
        c_id = sbuf("c_id", [128, 128], BF16)
        c_ocb = sbuf("c_ocb", [128, 1], BF16)
        c_ocf = sbuf("c_ocf", [128, 1])
        c_o1b = sbuf("c_o1b", [1, 128])
        c_selp = sbuf("c_selp", [128, 3, 8])
        c_empt = sbuf("c_empt", [128, 3])
        c_selb = sbuf("c_selb", [128, 8])
        c_mhalo = sbuf("c_mhalo", [128, 1])
        sc = [sbuf(f"sc{i}", [128, T]) for i in range(8)]
        scb = [sbuf(f"scb{i}", [128, T], BF16) for i in range(4)]
        lnr = sbuf("lnr", [1, 512, 5])
        tiny = sbuf("tiny", [128, 64])
        trow = sbuf("trow", [1, 64])
        abuf = sbuf("abuf", [128, DC, 4])

        pb = [octx.enter_context(nc.psum_tensor(f"pb{i}", [128, 512], F32))
              for i in range(7)]
        pbt = octx.enter_context(nc.psum_tensor("pbt", [128, 128], BF16))

        def ts(out, in0, s1, s2, o0, o1=None, eng="v"):
            kw = {}
            if o1 is not None:
                kw["op1"] = o1
            return em.op(eng, lambda e, out=out, in0=in0, s1=s1, s2=s2, o0=o0, kw=kw:
                         e.tensor_scalar(out=out, in0=in0, scalar1=s1, scalar2=s2,
                                         op0=o0, **kw))

        def tt(out, in0, in1, op, eng="v"):
            return em.op(eng, lambda e, out=out, in0=in0, in1=in1, op=op:
                         e.tensor_tensor(out=out, in0=in0, in1=in1, op=op))

        def stt(out, in0, scalar, in1, op0, op1, eng="v"):
            return em.op(eng, lambda e, out=out, in0=in0, scalar=scalar, in1=in1,
                         op0=op0, op1=op1: e.scalar_tensor_tensor(
                             out=out, in0=in0, scalar=scalar, in1=in1, op0=op0,
                             op1=op1))

        def act(out, in_, func, bias=0.0, scale=1.0, accum=None, eng="s",
                vwait=None):
            kw = {}
            if accum is not None:
                kw["accum_out"] = accum
            if eng == "s":
                if vwait is None:
                    em.wait("s", ("v", em.cnt["v"]))
                else:
                    em.waits("s", vwait)
            tok = em.op(eng, lambda e, out=out, in_=in_, func=func, bias=bias,
                        scale=scale, kw=kw: e.activation(
                            out=out, in_=in_, func=func, bias=bias, scale=scale,
                            **kw))
            if accum is not None:
                em.wait(eng, tok)
            return tok

        def cp(out, in_, eng="v"):
            if eng == "s":
                em.wait("s", ("v", em.cnt["v"]))
                return em.op(eng, lambda e, out=out, in_=in_: e.copy(out=out,
                                                                     in_=in_))
            return em.op(eng, lambda e, out=out, in_=in_: e.tensor_copy(out=out,
                                                                        in_=in_))

        def mm(out, lhsT, rhs, start, stop):
            return em.op("te", lambda e, out=out, l=lhsT, r=rhs, st=start, sp=stop:
                         e.matmul(out=out, lhsT=l, rhs=r, start=st, stop=sp))

        def tpose(out, in_):
            return em.op("te", lambda e, out=out, in_=in_:
                         e.transpose(out, in_, c_id[:]))

        def scan(out, d0, d1, initial):
            tok = em.op("v", lambda e, out=out, d0=d0, d1=d1, init=initial:
                        e.tensor_tensor_scan(out=out, data0=d0, data1=d1,
                                             initial=init, op0=OP.mult,
                                             op1=OP.add))
            em.wait("v", tok)
            return tok

        def red(out, in_, op, absval=False):
            tok = em.op("v", lambda e, out=out, in_=in_, op=op, av=absval:
                        e.tensor_reduce(out=out, in_=in_, axis=AX.X, op=op,
                                        apply_absolute_value=av))
            em.wait("v", tok)
            return tok

        def memset(ap_, val):
            tok = em.op("v", lambda e, ap_=ap_, val=val: e.memset(ap_, val))
            em.wait("v", tok)
            return tok

        def recip(out, in_):
            tok = em.op("v", lambda e, out=out, in_=in_: e.reciprocal(out=out,
                                                                      in_=in_))
            em.wait("v", tok)
            return tok

        pbusy = {}

        def pwait(i):
            em.waits("te", [pbusy.get(i)])

        def collective(in_ap, out_ap):
            def fn(e, i=in_ap, o=out_ap):
                return e.collective_compute(
                    "AllGather", OP.bypass, replica_groups=[list(range(NCORE))],
                    ins=[i.opt()], outs=[o.opt()])
            return em.op("gp", fn)

        def l1c(name):
            return L1V.index(name)

        def lw(name, ch):
            return c_l1[:, ch, l1c(name):l1c(name) + 1]

        def emit_ln(src, n, out_fn, affine=None):
            for th0 in range(0, n, 512):
                tn = min(512, n - th0)
                memset(lnr[0:1, 0:tn, 0], 0.0)
                memset(lnr[0:1, 0:tn, 1], 0.0)
                mmprev = [None, None]
                for ch in range(DC):
                    em.wait("s", mmprev[ch % 2])
                    ctok = cp(scb[ch % 2][:, 0:tn], src(ch)[:, th0:th0 + tn], eng="s")
                    sqtok = act(scb[2 + ch % 2][:, 0:tn], scb[ch % 2][:, 0:tn],
                                AF.Square)
                    em.wait("te", ctok)
                    pwait(4)
                    t1 = mm(pb[4][0:1, 0:tn], c_ocb[:, 0:1], scb[ch % 2][:, 0:tn],
                            True, True)
                    em.wait("te", sqtok)
                    pwait(5)
                    t2 = mm(pb[5][0:1, 0:tn], c_ocb[:, 0:1], scb[2 + ch % 2][:, 0:tn],
                            True, True)
                    em.waits("v", [t1, t2])
                    mmprev[ch % 2] = t2
                    pbusy[4] = tt(lnr[0:1, 0:tn, 0], lnr[0:1, 0:tn, 0],
                                  pb[4][0:1, 0:tn], OP.add)
                    pbusy[5] = tt(lnr[0:1, 0:tn, 1], lnr[0:1, 0:tn, 1],
                                  pb[5][0:1, 0:tn], OP.add)
                ts(lnr[0:1, 0:tn, 0], lnr[0:1, 0:tn, 0], 1.0 / D, None,
                   OP.mult)
                ts(lnr[0:1, 0:tn, 1], lnr[0:1, 0:tn, 1], 1.0 / D, None,
                   OP.mult)
                tt(lnr[0:1, 0:tn, 2], lnr[0:1, 0:tn, 0],
                   lnr[0:1, 0:tn, 0], OP.mult)
                tt(lnr[0:1, 0:tn, 2], lnr[0:1, 0:tn, 1],
                   lnr[0:1, 0:tn, 2], OP.subtract)
                vtok = ts(lnr[0:1, 0:tn, 3], lnr[0:1, 0:tn, 2],
                          float(EPS_LN), None, OP.add)
                em.wait("s", vtok)
                sq = act(lnr[0:1, 0:tn, 3], lnr[0:1, 0:tn, 3],
                         AF.Sqrt)
                em.wait("v", sq)
                rtok = recip(lnr[0:1, 0:tn, 4], lnr[0:1, 0:tn, 3])
                em.waits("te", [rtok])
                bmu = mm(pb[4][:, 0:tn], c_o1b[0:1, :], lnr[0:1, 0:tn, 0],
                         True, True)
                brv = mm(pb[5][:, 0:tn], c_o1b[0:1, :], lnr[0:1, 0:tn, 4],
                         True, True)
                em.waits("v", [bmu, brv])
                for ch in range(DC):
                    tt(sc[7][:, 0:tn], src(ch)[:, th0:th0 + tn], pb[4][:, 0:tn],
                       OP.subtract)
                    tt(sc[7][:, 0:tn], sc[7][:, 0:tn], pb[5][:, 0:tn], OP.mult)
                    if affine is not None:
                        wcol, bcol = affine
                        ts(sc[7][:, 0:tn], sc[7][:, 0:tn], wcol(ch), bcol(ch),
                           OP.mult, OP.add)
                    out_fn(ch, th0, tn)
                em.wait("te", ("v", em.cnt["v"]))

        def emit_tok_ln(src_col, dst_col, wname, bname, slot=0):
            t1 = mm(pb[4][0:1, 0:DC], c_ocf[:, 0:1], src_col, True, True)
            sqt = act(tiny[:, 32:32 + DC], src_col, AF.Square)
            em.wait("te", sqt)
            t2 = mm(pb[5][0:1, 0:DC], c_ocf[:, 0:1], tiny[:, 32:32 + DC], True, True)
            em.waits("v", [t1, t2])
            rd1 = red(trow[0:1, slot + 8:slot + 9],
                      pb[4][0:1, 0:DC].rearrange("x (a c) -> x a c", a=1), OP.add)
            rd2 = red(trow[0:1, slot + 9:slot + 10],
                      pb[5][0:1, 0:DC].rearrange("x (a c) -> x a c", a=1), OP.add)
            em.waits("gp", [rd2])
            hop = cp(trow[0:1, slot:slot + 2], trow[0:1, slot + 8:slot + 10],
                     eng="v")
            em.wait("v", hop)
            ts(trow[0:1, slot:slot + 1], trow[0:1, slot:slot + 1], 1.0 / D, None,
               OP.mult)
            ts(trow[0:1, slot + 1:slot + 2], trow[0:1, slot + 1:slot + 2], 1.0 / D,
               None, OP.mult)
            tt(trow[0:1, slot + 2:slot + 3], trow[0:1, slot:slot + 1],
               trow[0:1, slot:slot + 1], OP.mult)
            tt(trow[0:1, slot + 2:slot + 3], trow[0:1, slot + 1:slot + 2],
               trow[0:1, slot + 2:slot + 3], OP.subtract)
            vt = ts(trow[0:1, slot + 3:slot + 4], trow[0:1, slot + 2:slot + 3],
                    float(EPS_LN), None, OP.add)
            em.wait("s", vt)
            sq2 = act(trow[0:1, slot + 3:slot + 4], trow[0:1, slot + 3:slot + 4],
                      AF.Sqrt)
            em.wait("v", sq2)
            rct = recip(trow[0:1, slot + 4:slot + 5], trow[0:1, slot + 3:slot + 4])
            em.waits("te", [rct])
            bmu = mm(pb[4][:, 0:1], c_o1b[0:1, :], trow[0:1, slot:slot + 1], True,
                     True)
            brv = mm(pb[5][:, 0:1], c_o1b[0:1, :], trow[0:1, slot + 4:slot + 5],
                     True, True)
            em.waits("v", [bmu, brv])
            for ch in range(DC):
                ts(dst_col[:, ch:ch + 1], src_col[:, ch:ch + 1], pb[4][:, 0:1], None,
                   OP.subtract)
                tt(dst_col[:, ch:ch + 1], dst_col[:, ch:ch + 1], pb[5][:, 0:1],
                   OP.mult)
                tt(dst_col[:, ch:ch + 1], dst_col[:, ch:ch + 1], lw(wname, ch),
                   OP.mult)
                tt(dst_col[:, ch:ch + 1], dst_col[:, ch:ch + 1], lw(bname, ch),
                   OP.add)
                ts(dst_col[:, ch:ch + 1], dst_col[:, ch:ch + 1], c_mhalo[:, 0:1],
                   None, OP.mult)
            em.wait("te", ("v", em.cnt["v"]))

        lt = []
        for dst, src_ in ((c_l1, l1v), (c_sg, sgate), (c_su, sup),
                          (c_sd, sdown), (c_g1, g1i), (c_g2, g2i), (c_g3, g3i),
                          (c_ocf, onesc_f), (c_selp, selp), (c_empt, empt),
                          (c_selb, selb), (c_mhalo, mhalo)):
            lt.append(em.dma("sy", dst[:], src_))
        for dst, src_ in ((c_id, ident), (c_ocb, onesc_b)):
            lt.append(em.dma("gp", dst[:], src_))
        lt.append(em.dma("sy", c_o1b[:], ones1_b))
        t_x0 = em.dma("sy", x_res[:], x0c)
        em.waits("v", lt)
        ts(c_nw[:], c_l1[:, :, 5], -1.0, None, OP.mult)
        ts(c_nwT[:], c_l1[:, :, 5], -float(T), None, OP.mult)
        em.barrier()

        with contextlib.ExitStack() as ctx1:
            xp = sbuf("xp", [128, 4, TH], BF16, ctx=ctx1)
            dtlb = sbuf("dtlb", [128, T], BF16, ctx=ctx1)
            bmb = sbuf("bmb", [128, 4, T], BF16, ctx=ctx1)
            c_wdt = sbuf("c_wdt", [128, DIC, 128], BF16, ctx=ctx1)
            wgg = sbuf("wgg", [128, DIC, NST], F32, ctx=ctx1)
            wloc = sbuf("wloc", [128, DIC, NST], F32, ctx=ctx1)
            sini = sbuf("sini", [128, DIC, NST], F32, ctx=ctx1)
            c_a = sbuf("c_a", [128, DIC, NST], ctx=ctx1)
            c_bdt2 = sbuf("c_bdt2", [128, DIC], ctx=ctx1)
            c_convw = sbuf("c_convw", [128, DIC, 4], ctx=ctx1)
            c_convb = sbuf("c_convb", [128, DIC], ctx=ctx1)
            c_dpar = sbuf("c_dpar", [128, DIC], ctx=ctx1)
            c_s16 = sbuf("c_s16", [16, 128], BF16, ctx=ctx1)
            c_oh16 = sbuf("c_oh16", [16, NST, 128], BF16, ctx=ctx1)
            c_selc = sbuf("c_selc", [128, NST, 128], BF16, ctx=ctx1)
            st16b = sbuf("st16b", [16, TH], BF16, ctx=ctx1)
            st16c = sbuf("st16c", [16, T], BF16, ctx=ctx1)
            xrep = sbuf("xrep", [128, TH], BF16, ctx=ctx1)

            lt1 = []
            for dst, src_ in ((c_bdt2, bdt), (c_convw, convw),
                              (c_convb, convb), (c_dpar, dpar)):
                lt1.append(em.dma("sy", dst[:], src_))
            for dst, src_ in ((c_s16, sel16), (c_oh16, oh16), (c_selc, selc)):
                lt1.append(em.dma("gp", dst[:], src_))
            t_a = em.dma("sy", c_a[:], alog)
            em.wait("s", t_a)
            ae = act(c_a[:], c_a[:], AF.Exp)
            em.waits("v", lt1 + [ae])
            ts(c_a[:], c_a[:], -1.0, None, OP.mult)
            ts(c_bdt2[:], c_bdt2[:], 2.0, None, OP.mult)
            em.barrier()

            t_wdt = em.dma("gp", c_wdt[:], wdt_t.rearrange("c p m -> p c m"))

            with contextlib.ExitStack() as ctx1a:
                xln0 = sbuf("xln0", [128, DC, TH], BF16, ctx=ctx1a)
                wbuf = [sbuf(f"wbuf{i}", [128, 16, 128], BF16, ctx=ctx1a)
                        for i in range(2)]

                def out_ln0(ch, th0, tn):
                    cp(xln0[:, ch, th0:th0 + tn], sc[7][:, 0:tn])
                em.wait("s", t_x0)
                em.wait("v", t_x0)
                emit_ln(lambda ch: x_res[:, ch, :], TH, out_ln0)
                em.barrier()

                wpar = [None, None]
                for m in range(73):
                    par = m % 2
                    em.waits("gp", [wpar[par]])
                    twl = em.dma("gp", wbuf[par][:],
                                 win_t[m].rearrange("k p c -> p k c"))
                    if 32 <= m < 36:
                        tts_ = [(0, 512), (512, 512), (1024, 4)]
                    else:
                        tts_ = [(4, 512), (516, 512)]
                    for fi, (t0, tn) in enumerate(tts_):
                        bank = pb[fi % 2]
                        em.waits("te", [twl])
                        pwait(fi % 2)
                        for kc in range(DC):
                            last = mm(bank[:, 0:tn], wbuf[par][:, kc, :],
                                      xln0[:, kc, t0:t0 + tn], kc == 0, kc == DC - 1)
                        rt = t0 - 4
                        if m < 32:
                            em.waits("s", [last, wpar[par]])
                            etok = act(scb[m % 2][:, 0:tn], bank[:, 0:tn], AF.Silu)
                            em.wait("sy", etok)
                            wpar[par] = em.dma("sy", sp_z[m, :, rt:rt + tn],
                                               scb[m % 2][:, 0:tn])
                            pbusy[fi % 2] = etok
                        elif m < 36:
                            em.waits("v", [last])
                            wpar[par] = cp(xp[:, m - 32, t0:t0 + tn], bank[:, 0:tn])
                            pbusy[fi % 2] = wpar[par]
                        elif m < 40:
                            em.waits("v", [last])
                            wpar[par] = cp(bmb[:, m - 36, rt:rt + tn], bank[:, 0:tn])
                            pbusy[fi % 2] = wpar[par]
                        elif m < 72:
                            em.waits("v", [last, wpar[par]])
                            etok = cp(scb[2 + m % 2][:, 0:tn], bank[:, 0:tn])
                            em.wait("sy", etok)
                            wpar[par] = em.dma("sy", sp_cm[m - 40, :, rt:rt + tn],
                                               scb[2 + m % 2][:, 0:tn])
                            pbusy[fi % 2] = etok
                        else:
                            em.waits("v", [last])
                            wpar[par] = cp(dtlb[:, rt:rt + tn], bank[:, 0:tn])
                            pbusy[fi % 2] = wpar[par]
                em.barrier()

            scx = sbuf("scx", [128, T], ctx=ctx1)

            def emit_delta(c, accum=None):
                em.wait("te", t_wdt)
                for t0 in range(0, T, 512):
                    pwait(5)
                    tk = mm(pb[5][:, 0:512], c_wdt[:, c, :], dtlb[:, t0:t0 + 512],
                            True, True)
                    em.wait("s", tk)
                    e1 = act(sc[0][:, t0:t0 + 512], pb[5][:, 0:512], AF.Exp,
                             bias=c_bdt2[:, c:c + 1])
                    pbusy[5] = e1
                return act(sc[0][:], sc[0][:], AF.Ln, bias=1.0, accum=accum)

            def emit_u(c, prev_te):
                em.waits("gp", [prev_te])
                tg = em.dma("gp", st16b[:, 0:TH],
                            xp[(c % 8) * 16:(c % 8) * 16 + 16, c // 8, :])
                xtoks = []
                for fi, (t0, tn) in enumerate(((0, 512), (512, 512), (1024, 4))):
                    em.waits("te", [tg])
                    pwait(4)
                    tk = mm(pb[4][:, 0:tn], c_s16[:], st16b[0:16, t0:t0 + tn],
                            True, True)
                    em.waits("v", [tk])
                    xtoks.append(cp(xrep[:, t0:t0 + tn], pb[4][:, 0:tn]))
                    pbusy[4] = xtoks[-1]
                ts(sc[1][:], xrep[:, 1:1025], c_convw[:, c, 0:1], None, OP.mult)
                for k in range(1, 4):
                    stt(sc[1][:], xrep[:, 1 + k:1025 + k], c_convw[:, c, k:k + 1],
                        sc[1][:], OP.mult, OP.add)
                em.wait("s", ("v", em.cnt["v"]))
                return act(sc[1][:], sc[1][:], AF.Silu, bias=c_convb[:, c:c + 1])

            def bcast_B(n, stok):
                out = []
                for fi, t0 in enumerate((0, 512)):
                    em.waits("te", [stok])
                    pwait(2 + fi)
                    out.append(mm(pb[2 + fi][:, 0:512], c_oh16[:, n, :],
                                  st16c[0:16, t0:t0 + 512], True, True))
                return out

            prev_te = None
            uspill = None
            for c in range(DIC):
                dtok = emit_delta(c, accum=tiny[:, 0:1])
                utok = emit_u(c, prev_te)
                em.waits("v", [utok, uspill])
                ub = cp(scb[1][:], sc[1][:])
                em.wait("sy", ub)
                uspill = em.dma("sy", sp_u[c], scb[1][:], ch=4)
                em.waits("v", [dtok])
                dut = tt(sc[2][:], sc[0][:], sc[1][:], OP.mult)
                gt = tt(wgg[:, c, :], c_a[:, c, :],
                        tiny[:, 0:1].broadcast_to([128, NST]), OP.mult)
                em.wait("s", gt)
                act(wgg[:, c, :], wgg[:, c, :], AF.Exp)
                em.waits("gp", [prev_te])
                bstage = em.dma("gp", st16c[:, 0:T],
                                bmb[(c % 8) * 16:(c % 8) * 16 + 16, c // 8, :],
                                ch=3)
                dbuf = [sc[3], sc[6]]
                sctoks = [None, None]
                for n in range(NST):
                    em.wait("s", dut)
                    datok = act(dbuf[n % 2][:], sc[0][:], AF.Exp,
                                scale=c_a[:, c, n:n + 1],
                                vwait=[sctoks[n % 2], dut])
                    btoks = bcast_B(n, bstage)
                    em.waits("v", btoks)
                    pbusy[2] = tt(sc[4][:, 0:512], sc[2][:, 0:512], pb[2][:, 0:512],
                                  OP.mult)
                    pbusy[3] = tt(sc[4][:, 512:1024], sc[2][:, 512:1024],
                                  pb[3][:, 0:512], OP.mult)
                    em.waits("v", [datok])
                    sctoks[n % 2] = scan(sc[5][:], dbuf[n % 2][:], sc[4][:], 0.0)
                    cp(wloc[:, c, n:n + 1], sc[5][:, T - 1:T])
                prev_te = ("te", em.cnt["te"])
            em.barrier()

            if dbg:
                em.dma("sy", dbg_wl, wloc[:])
                em.dma("sy", dbg_wg, wgg[:])
            pw = em.dma("gp", cc1i[:, 0:512],
                        wloc[:].rearrange("p a b -> p (a b)"))
            pg = em.dma("gp", cc1i[:, 512:1024],
                        wgg[:].rearrange("p a b -> p (a b)"))
            em.waits("gp", [pw, pg])
            if dbg:
                em.waits("gp", [pw, pg])
                em.dma("gp", dbg_cc1i, cc1i)
            cct = collective(cc1i, cc1o)
            if dbg:
                em.waits("gp", [cct])
                em.dma("gp", dbg_cc1, cc1o)
            with contextlib.ExitStack() as ctx1b:
                accW = [sbuf(f"accW{j}", [128, 512], F32, ctx=ctx1b)
                        for j in range(3)]
                accG = [sbuf(f"accG{j}", [128, 512], F32, ctx=ctx1b)
                        for j in range(3)]
                rb1 = [sbuf(f"rb1{i}", [128, 1024], F32, ctx=ctx1b)
                       for i in range(2)]
                for j in range(3):
                    memset(accW[j][:], 0.0)
                    memset(accG[j][:], 0.0)
                em.wait("gp", ("v", em.cnt["v"]))
                rtp = [None, None]
                for cp_ in range(NCORE):
                    em.waits("gp", [cct, rtp[cp_ % 2]])
                    rt = em.dma("gp", rb1[cp_ % 2][:],
                                cc1o[cp_ * 128:(cp_ + 1) * 128, :])
                    em.wait("v", rt)
                    for j in range(3):
                        stt(accW[j][:], rb1[cp_ % 2][:, 0:512],
                            c_selp[:, j, cp_:cp_ + 1], accW[j][:], OP.mult, OP.add)
                        rtp[cp_ % 2] = stt(accG[j][:], rb1[cp_ % 2][:, 512:1024],
                                           c_selp[:, j, cp_:cp_ + 1], accG[j][:],
                                           OP.mult, OP.add)
                sv = sini[:].rearrange("p a b -> p (a b)")
                memset(sv, 0.0)
                for j in range(3):
                    ts(accG[j][:], accG[j][:], c_empt[:, j:j + 1], None, OP.add)
                    tt(sv, sv, accG[j][:], OP.mult)
                    tt(sv, sv, accW[j][:], OP.add)
                if dbg:
                    em.wait("sy", ("v", em.cnt["v"]))
                    em.dma("sy", dbg_sini, sv)
                em.barrier()

            prev_te = None
            yspill = None
            for c in range(DIC):
                dtok = emit_delta(c)
                em.wait("sy", ("v", em.cnt["v"]))
                ur = em.dma("sy", scb[1][:], sp_u[c])
                em.wait("v", ur)
                cp(sc[1][:], scb[1][:])
                em.waits("v", [dtok])
                dut = tt(sc[2][:], sc[0][:], sc[1][:], OP.mult)
                em.waits("sy", [prev_te])
                cmr = em.dma("sy", scb[2][:], sp_cm[c])
                em.waits("gp", [prev_te])
                bstage = em.dma("gp", st16c[:, 0:T],
                                bmb[(c % 8) * 16:(c % 8) * 16 + 16, c // 8, :],
                                ch=3)
                memset(sc[6][:], 0.0)
                dbuf = [sc[3], scx]
                sctoks = [None, None]
                for n in range(NST):
                    em.wait("s", dut)
                    datok = act(dbuf[n % 2][:], sc[0][:], AF.Exp,
                                scale=c_a[:, c, n:n + 1],
                                vwait=[sctoks[n % 2], dut])
                    btoks = bcast_B(n, bstage)
                    em.waits("v", btoks)
                    pbusy[2] = tt(sc[4][:, 0:512], sc[2][:, 0:512], pb[2][:, 0:512],
                                  OP.mult)
                    pbusy[3] = tt(sc[4][:, 512:1024], sc[2][:, 512:1024],
                                  pb[3][:, 0:512], OP.mult)
                    em.waits("v", [datok])
                    sctoks[n % 2] = scan(sc[5][:], dbuf[n % 2][:], sc[4][:],
                                         sini[:, c, n:n + 1])
                    em.wait("te", cmr)
                    pwait(0)
                    pwait(1)
                    ctoks = [mm(pb[0][:, 0:512], c_selc[:, n, :], scb[2][:, 0:512],
                                True, True),
                             mm(pb[1][:, 0:512], c_selc[:, n, :], scb[2][:, 512:1024],
                                True, True)]
                    em.waits("v", ctoks)
                    pbusy[0] = tt(sc[7][:, 0:512], sc[5][:, 0:512], pb[0][:, 0:512],
                                  OP.mult)
                    pbusy[1] = tt(sc[7][:, 512:1024], sc[5][:, 512:1024],
                                  pb[1][:, 0:512], OP.mult)
                    tt(sc[6][:], sc[6][:], sc[7][:], OP.add)
                zr = em.dma("sy", scb[3][:], sp_z[c])
                stt(sc[6][:], sc[1][:], c_dpar[:, c:c + 1], sc[6][:], OP.mult,
                    OP.add)
                em.waits("v", [zr, yspill])
                yb = tt(scb[1][:], sc[6][:], scb[3][:], OP.mult)
                em.wait("sy", yb)
                yspill = em.dma("sy", sp_y[c], scb[1][:])
                prev_te = ("te", em.cnt["te"])
            em.barrier()

            with contextlib.ExitStack() as ctx1c:
                ygrp = sbuf("ygrp", [128, 8, T], BF16, ctx=ctx1c)
                wob = [sbuf(f"wob{i}", [128, 8, 128], BF16, ctx=ctx1c)
                       for i in range(2)]
                for g in range(4):
                    gr = []
                    em.wait("sy", ("te", em.cnt["te"]))
                    for ki in range(8):
                        gr.append(em.dma("sy", ygrp[:, ki, :], sp_y[g * 8 + ki]))
                    wpar = [None, None]
                    for m in range(DC):
                        par = m % 2
                        em.waits("gp", [wpar[par]])
                        twl = em.dma("gp", wob[par][:],
                                     wout_t[m, g * 8:(g + 1) * 8].rearrange(
                                         "k p c -> p k c"), ch=par)
                        for fi, t0 in enumerate((0, 512)):
                            bank = pb[fi]
                            em.waits("te", [twl] + gr)
                            pwait(fi)
                            for ki in range(8):
                                last = mm(bank[:, 0:512], wob[par][:, ki, :],
                                          ygrp[:, ki, t0:t0 + 512], ki == 0,
                                          ki == 7)
                            em.waits("v", [last])
                            wpar[par] = tt(x_res[:, m, 4 + t0:4 + t0 + 512],
                                           x_res[:, m, 4 + t0:4 + t0 + 512],
                                           bank[:, 0:512], OP.add)
                            pbusy[fi] = wpar[par]
                    em.barrier()

        if upto <= 1:
            em.barrier()
            fin = em.dma("sy", outT, x_res[:, :, 4:TH])
            em.waits("gp", [fin])
            em.flush()
            return nc

        with contextlib.ExitStack() as ctx2:
            xlnb = sbuf("xlnb", [128, DC, T], BF16, ctx=ctx2)
            fuse = [sbuf(f"fuse{i}", [128, T], BF16, ctx=ctx2) for i in range(8)]
            wg8 = [sbuf(f"wg8{i}", [128, 2, DC, 128], BF16, ctx=ctx2)
                   for i in range(2)]
            wd8 = [sbuf(f"wd8{i}", [128, 8, 128], BF16, ctx=ctx2)
                   for i in range(2)]

            def out_lnf(ch, th0, tn):
                cp(xlnb[:, ch, th0:th0 + tn], sc[7][:, 0:tn])
            emit_ln(lambda ch: x_res[:, ch, 4:TH], T, out_lnf)
            em.barrier()

            wpar = [None, None]
            dpar2 = [None, None]
            for g in range(8):
                ftoks = []
                for ki in range(8):
                    hm = g * 8 + ki
                    par = hm % 2
                    em.waits("gp", [wpar[par]])
                    twl = em.dma("gp", wg8[par][:],
                                 wgu_t[hm].rearrange("two k p c -> p two k c"),
                                 ch=par)
                    for fi, t0 in enumerate((0, 512)):
                        em.waits("te", [twl])
                        pwait(fi)
                        for kc in range(DC):
                            gm = mm(pb[fi][:, 0:512], wg8[par][:, 0, kc, :],
                                    xlnb[:, kc, t0:t0 + 512], kc == 0, kc == DC - 1)
                        pwait(2 + fi)
                        for kc in range(DC):
                            um = mm(pb[2 + fi][:, 0:512], wg8[par][:, 1, kc, :],
                                    xlnb[:, kc, t0:t0 + 512], kc == 0, kc == DC - 1)
                        em.waits("s", [gm])
                        gt = act(sc[0][:, t0:t0 + 512], pb[fi][:, 0:512], AF.Silu,
                                 scale=c_sg[:, hm:hm + 1])
                        pbusy[fi] = gt
                        em.waits("v", [um, gt])
                        ft = stt(fuse[ki][:, t0:t0 + 512], pb[2 + fi][:, 0:512],
                                 c_su[:, hm:hm + 1], sc[0][:, t0:t0 + 512],
                                 OP.mult, OP.mult)
                        pbusy[2 + fi] = ft
                        ftoks.append(ft)
                    wpar[par] = ftoks[-1]
                for mp in range(DC):
                    par = mp % 2
                    em.waits("gp", [dpar2[par]])
                    twl = em.dma("gp", wd8[par][:],
                                 wdn_t[g * 8:(g + 1) * 8, mp].rearrange(
                                     "k p c -> p k c"), ch=2 + par)
                    for fi, t0 in enumerate((0, 512)):
                        bank = pb[4 + fi]
                        em.waits("te", [twl] + ftoks)
                        pwait(4 + fi)
                        for ki in range(8):
                            dm = mm(bank[:, 0:512], wd8[par][:, ki, :],
                                    fuse[ki][:, t0:t0 + 512], ki == 0, ki == 7)
                        em.waits("v", [dm])
                        dpar2[par] = stt(x_res[:, mp, 4 + t0:4 + t0 + 512],
                                         bank[:, 0:512], c_sd[:, mp:mp + 1],
                                         x_res[:, mp, 4 + t0:4 + t0 + 512],
                                         OP.mult, OP.add)
                        pbusy[4 + fi] = dpar2[par]
                em.barrier()
            em.barrier()

        if upto <= 2:
            fin = em.dma("sy", outT, x_res[:, :, 4:TH])
            em.waits("gp", [fin])
            em.flush()
            return nc

        pbnd = em.dma("sy", cc2i, x_res[:, :, TH - 1:TH].rearrange(
            "p c t -> p (c t)"), slow=True)
        em.waits("gp", [pbnd])
        cc2t = collective(cc2i, cc2o)
        memset(abuf[:, :, 0], 0.0)
        em.wait("gp", ("v", em.cnt["v"]))
        hs = None
        for cp_ in range(NCORE):
            em.waits("gp", [cc2t, hs])
            rt = em.dma("gp", abuf[:, :, 1], cc2o[cp_ * 128:(cp_ + 1) * 128, :],
                        slow=True)
            em.wait("v", rt)
            hs = stt(abuf[:, :, 0], abuf[:, :, 1], c_selb[:, cp_:cp_ + 1],
                     abuf[:, :, 0], OP.mult, OP.add)
        em.barrier()

        with contextlib.ExitStack() as ctx3:
            slA = sbuf("slA", [128, DC, T + 1], BF16, ctx=ctx3)
            slB = sbuf("slB", [128, DC, T], BF16, ctx=ctx3)
            wv3 = [sbuf(f"wv3{i}", [128, DC, 128], BF16, ctx=ctx3)
                   for i in range(2)]
            aag = sbuf("aag", [128, 5, DC], F32, ctx=ctx3)

            sx = []
            for ch in range(DC):
                sx.append(em.dma("sy", sp_x2[ch], x_res[:, ch, 4:TH]))
            em.waits("v", sx)
            em.waits("s", sx)

            def out_xb(ch, th0, tn):
                cp(x_res[:, ch, 4 + th0:4 + th0 + tn], sc[7][:, 0:tn])
            emit_ln(lambda ch: x_res[:, ch, 4:TH], T, out_xb)
            em.barrier()

            t1 = mm(pb[4][0:1, 0:DC], c_ocf[:, 0:1], abuf[:, :, 0], True, True)
            sqt = act(abuf[:, :, 1], abuf[:, :, 0], AF.Square)
            em.wait("te", sqt)
            t2 = mm(pb[5][0:1, 0:DC], c_ocf[:, 0:1], abuf[:, :, 1], True, True)
            em.waits("v", [t1, t2])
            rd1 = red(trow[0:1, 30:31],
                      pb[4][0:1, 0:DC].rearrange("x (a c) -> x a c", a=1), OP.add)
            rd2 = red(trow[0:1, 31:32],
                      pb[5][0:1, 0:DC].rearrange("x (a c) -> x a c", a=1), OP.add)
            em.waits("gp", [rd2])
            hop = cp(trow[0:1, 0:2], trow[0:1, 30:32])
            em.wait("v", hop)
            ts(trow[0:1, 0:1], trow[0:1, 0:1], 1.0 / D, None, OP.mult)
            ts(trow[0:1, 1:2], trow[0:1, 1:2], 1.0 / D, None, OP.mult)
            tt(trow[0:1, 2:3], trow[0:1, 0:1], trow[0:1, 0:1], OP.mult)
            tt(trow[0:1, 2:3], trow[0:1, 1:2], trow[0:1, 2:3], OP.subtract)
            vt = ts(trow[0:1, 3:4], trow[0:1, 2:3], float(EPS_LN), None, OP.add)
            em.wait("s", vt)
            sq2 = act(trow[0:1, 3:4], trow[0:1, 3:4], AF.Sqrt)
            em.wait("v", sq2)
            rct = recip(trow[0:1, 4:5], trow[0:1, 3:4])
            em.waits("te", [rct])
            bmu = mm(pb[4][:, 0:1], c_o1b[0:1, :], trow[0:1, 0:1], True, True)
            brv = mm(pb[5][:, 0:1], c_o1b[0:1, :], trow[0:1, 4:5], True, True)
            em.waits("v", [bmu, brv])
            for ch in range(DC):
                ts(abuf[:, ch, 2:3], abuf[:, ch, 0:1], pb[4][:, 0:1], None,
                   OP.subtract)
                tt(abuf[:, ch, 2:3], abuf[:, ch, 2:3], pb[5][:, 0:1], OP.mult)
                tt(abuf[:, ch, 3:4], abuf[:, ch, 2:3], lw("ln1w", ch), OP.mult)
                tt(abuf[:, ch, 3:4], abuf[:, ch, 3:4], lw("ln1b", ch), OP.add)
                ts(abuf[:, ch, 3:4], abuf[:, ch, 3:4], c_mhalo[:, 0:1], None,
                   OP.mult)
            em.wait("te", ("v", em.cnt["v"]))

            def out_ln1(ch, th0, tn):
                cp(slA[:, ch, 1 + th0:1 + th0 + tn], sc[7][:, 0:tn])
            emit_ln(lambda ch: x_res[:, ch, 4:TH], T, out_ln1,
                    affine=(lambda ch: lw("ln1w", ch), lambda ch: lw("ln1b", ch)))
            for ch in range(DC):
                cp(slA[:, ch, 0:1], abuf[:, ch, 3:4])
            em.barrier()

            def emit_mix(src, mname, ch, out_ap):
                tt(sc[0][:], src[:, ch, 1:T + 1], src[:, ch, 0:T], OP.subtract)
                return stt(out_ap, sc[0][:], lw(mname, ch), src[:, ch, 0:T],
                           OP.mult, OP.add)

            def kvr_mm(widx, dst_sp, sigmoid_first):
                wpar = [None, None]
                prev_s = None
                for ch in range(DC):
                    if sigmoid_first:
                        em.waits("v", [prev_s])
                        mt = emit_mix(slA, "mr", ch, scb[0][:])
                        em.wait("s", mt)
                        prev_s = act(slB[:, ch, :], scb[0][:], AF.Sigmoid)
                    else:
                        emit_mix(slA, "mk" if widx == 0 else "mv", ch, slB[:, ch, :])
                em.barrier()
                for m in range(DC):
                    par = m % 2
                    em.waits("gp", [wpar[par]])
                    twl = em.dma("gp", wv3[par][:],
                                 wkvr_t[m, widx].rearrange("k p c -> p k c"),
                                 ch=par)
                    for fi, t0 in enumerate((0, 512)):
                        bank = pb[fi]
                        em.waits("te", [twl])
                        pwait(fi)
                        for kc in range(DC):
                            km = mm(bank[:, 0:512], wv3[par][:, kc, :],
                                    slB[:, kc, t0:t0 + 512], kc == 0, kc == DC - 1)
                        em.waits("v", [km, wpar[par]])
                        et = cp(scb[fi][:, t0:t0 + 512], bank[:, 0:512])
                        pbusy[fi] = et
                        em.wait("sy", et)
                        wpar[par] = em.dma("sy", dst_sp[m, :, t0:t0 + 512],
                                           scb[fi][:, t0:t0 + 512], ch=4 + fi)
                em.barrier()

            kvr_mm(0, sp_k, False)
            kvr_mm(1, sp_v, False)
            kvr_mm(2, sp_r, True)

            def emit_e12(ch):
                em.wait("sy", ("v", em.cnt["v"]))
                em.wait("v", ("s", em.cnt["s"]))
                kr = em.dma("sy", scb[0][:], sp_k[ch], ch=4)
                vr = em.dma("sy", scb[1][:], sp_v[ch], ch=5)
                em.wait("v", kr)
                ts(sc[0][:], scb[0][:], lw("tmu", ch), None, OP.add)
                qt = ts(sc[1][:], sc[0][:], lw("tmw", ch), None, OP.max)
                em.wait("s", qt)
                e1t = act(sc[2][:], sc[1][:], AF.Exp, bias=c_nw[:, ch:ch + 1],
                          scale=-1.0)
                em.wait("v", e1t)
                sbt = tt(sc[3][:], sc[0][:], sc[1][:], OP.subtract)
                em.wait("s", sbt)
                e2t = act(sc[3][:], sc[3][:], AF.Exp)
                em.waits("v", [e2t, vr])
                tt(sc[4][:], sc[3][:], scb[1][:], OP.mult)

            for ch in range(DC):
                emit_e12(ch)
                scan(sc[5][:], sc[2][:], sc[4][:], 0.0)
                cp(aag[:, 0, ch:ch + 1], sc[5][:, T - 1:T])
                scan(sc[5][:], sc[2][:], sc[3][:], 0.0)
                cp(aag[:, 1, ch:ch + 1], sc[5][:, T - 1:T])
                act(sc[6][:], sc[1][:], AF.Copy, accum=tiny[:, 0:1])
                act(aag[:, 2, ch:ch + 1], tiny[:, 0:1], AF.Exp, scale=-1.0,
                    bias=c_nwT[:, ch:ch + 1])
                cp(aag[:, 3, ch:ch + 1], x_res[:, ch, TH - 1:TH])
            rlr = em.dma("gp", tiny[:, 32:32 + DC],
                         sp_r[:, :, T - 1:T].rearrange("c p t -> p (c t)"),
                         slow=True)
            em.wait("v", rlr)
            cp(aag[:, 4, :], tiny[:, 32:32 + DC])
            em.barrier()

            pub = em.dma("sy", cc3i, aag[:].rearrange("p f c -> p (f c)"))
            em.waits("gp", [pub])
            cc3t = collective(cc3i, cc3o)
            a_in = abuf[:, :, 0]
            b_in = abuf[:, :, 1]
            ypv = abuf[:, :, 2]
            rpv = abuf[:, :, 3]
            accA = [sc[0][:, 0:DC], sc[0][:, 64:64 + DC], sc[0][:, 128:128 + DC]]
            accB = [sc[1][:, 0:DC], sc[1][:, 64:64 + DC], sc[1][:, 128:128 + DC]]
            accD = [sc[2][:, 0:DC], sc[2][:, 64:64 + DC], sc[2][:, 128:128 + DC]]
            for j in range(3):
                memset(accA[j], 0.0)
                memset(accB[j], 0.0)
                memset(accD[j], 0.0)
            memset(ypv, 0.0)
            memset(rpv, 0.0)
            em.wait("gp", ("v", em.cnt["v"]))
            rtp = None
            for cp_ in range(NCORE):
                em.waits("gp", [cc3t, rtp])
                rt = em.dma("gp", sc[3][:, 0:5 * DC],
                            cc3o[cp_ * 128:(cp_ + 1) * 128, :])
                em.wait("v", rt)
                for j in range(3):
                    stt(accA[j], sc[3][:, 0:DC], c_selp[:, j, cp_:cp_ + 1], accA[j],
                        OP.mult, OP.add)
                    stt(accB[j], sc[3][:, DC:2 * DC], c_selp[:, j, cp_:cp_ + 1],
                        accB[j], OP.mult, OP.add)
                    stt(accD[j], sc[3][:, 2 * DC:3 * DC], c_selp[:, j, cp_:cp_ + 1],
                        accD[j], OP.mult, OP.add)
                stt(ypv, sc[3][:, 3 * DC:4 * DC], c_selb[:, cp_:cp_ + 1], ypv,
                    OP.mult, OP.add)
                rtp = stt(rpv, sc[3][:, 4 * DC:5 * DC], c_selb[:, cp_:cp_ + 1], rpv,
                          OP.mult, OP.add)
            memset(a_in, 0.0)
            memset(b_in, 0.0)
            for j in range(3):
                ts(accD[j], accD[j], c_empt[:, j:j + 1], None, OP.add)
                tt(a_in, a_in, accD[j], OP.mult)
                tt(a_in, a_in, accA[j], OP.add)
                tt(b_in, b_in, accD[j], OP.mult)
                tt(b_in, b_in, accB[j], OP.add)
            em.barrier()

            wkvr = slA
            ts(tiny[:, 30:31], c_mhalo[:, 0:1], -1.0, 1.0, OP.mult, OP.add)
            ts(sc[0][:, 0:DC], b_in, tiny[:, 30:31], None, OP.add)
            recip(sc[0][:, 0:DC], sc[0][:, 0:DC])
            tt(sc[1][:, 0:DC], a_in, sc[0][:, 0:DC], OP.mult)
            tt(sc[1][:, 0:DC], sc[1][:, 0:DC], rpv, OP.mult)
            cp(wkvr[:, :, 0], sc[1][:, 0:DC])
            for ch in range(DC):
                emit_e12(ch)
                rr = em.dma("sy", scb[2][:], sp_r[ch], ch=6)
                scan(sc[5][:], sc[2][:], sc[4][:], a_in[:, ch:ch + 1])
                scan(sc[6][:], sc[2][:], sc[3][:], b_in[:, ch:ch + 1])
                recip(sc[7][:], sc[6][:])
                tt(sc[5][:], sc[5][:], sc[7][:], OP.mult)
                em.wait("v", rr)
                tt(wkvr[:, ch, 1:T + 1], sc[5][:], scb[2][:], OP.mult)
            em.barrier()

            wpar = [None, None]
            for m in range(DC):
                par = m % 2
                em.waits("gp", [wpar[par]])
                twl = em.dma("gp", wv3[par][:],
                             wo_t[m].rearrange("k p c -> p k c"), ch=par)
                for fi, (t0, tn) in enumerate(((1, 512), (513, 512), (0, 1))):
                    bi_ = fi % 2 if tn > 1 else 2
                    bank = pb[bi_]
                    em.waits("te", [twl])
                    pwait(bi_)
                    for kc in range(DC):
                        am = mm(bank[:, 0:tn], wv3[par][:, kc, :],
                                wkvr[:, kc, t0:t0 + tn], kc == 0, kc == DC - 1)
                    em.waits("v", [am])
                    if tn > 1:
                        wpar[par] = stt(x_res[:, m, 3 + t0:3 + t0 + tn],
                                        bank[:, 0:tn], lw("bo", m),
                                        x_res[:, m, 3 + t0:3 + t0 + tn],
                                        OP.add, OP.add)
                    else:
                        ts(abuf[:, m, 1:2], bank[:, 0:1], lw("bo", m), None, OP.add)
                        wpar[par] = tt(abuf[:, m, 1:2], abuf[:, m, 1:2],
                                       ypv[:, m:m + 1], OP.add)
                    pbusy[bi_] = wpar[par]
            em.barrier()

            emit_tok_ln(abuf[:, :, 1], abuf[:, :, 2], "ln2w", "ln2b", slot=8)

            ln2x = slA

            def out_ln2(ch, th0, tn):
                cp(ln2x[:, ch, 1 + th0:1 + th0 + tn], sc[7][:, 0:tn])
            emit_ln(lambda ch: x_res[:, ch, 4:TH], T, out_ln2,
                    affine=(lambda ch: lw("ln2w", ch), lambda ch: lw("ln2b", ch)))
            for ch in range(DC):
                cp(ln2x[:, ch, 0:1], abuf[:, ch, 2:3])
            em.barrier()

            for ch in range(DC):
                emit_mix(ln2x, "cmk", ch, slB[:, ch, :])
            prev = None
            for ch in range(DC):
                em.waits("v", [prev])
                mt = emit_mix(ln2x, "cmr", ch, scb[ch % 2][:])
                em.waits("sy", [mt])
                prev = em.dma("sy", sp_mr[ch], scb[ch % 2][:])
            em.barrier()

            k2b = slA
            wpar = [None, None]
            for m in range(DC):
                par = m % 2
                em.waits("gp", [wpar[par]])
                twl = em.dma("gp", wv3[par][:],
                             wcm_t[m, 0].rearrange("k p c -> p k c"), ch=par)
                for fi, t0 in enumerate((0, 512)):
                    bank = pb[fi]
                    em.waits("te", [twl])
                    pwait(fi)
                    for kc in range(DC):
                        km = mm(bank[:, 0:512], wv3[par][:, kc, :],
                                slB[:, kc, t0:t0 + 512], kc == 0, kc == DC - 1)
                    em.waits("v", [km])
                    em.wait("v", ("s", em.cnt["s"]))
                    rl = ts(sc[0][:, 0:512], bank[:, 0:512], 0.0, None, OP.max)
                    pbusy[fi] = rl
                    em.wait("s", rl)
                    wpar[par] = act(k2b[:, m, t0:t0 + 512], sc[0][:, 0:512],
                                    AF.Square)
            em.barrier()
            rt = None
            for ch in range(DC):
                rt = em.dma("sy", slB[:, ch, :], sp_mr[ch])
            em.wait("v", rt)
            em.wait("te", rt)
            wpar = [None, None]
            for m in range(DC):
                par = m % 2
                em.waits("gp", [wpar[par]])
                twl = em.dma("gp", wv3[par][:],
                             wcm_t[m, 2].rearrange("k p c -> p k c"), ch=par)
                crt = None
                for fi, t0 in enumerate((0, 512)):
                    em.waits("te", [twl])
                    pwait(2 + fi)
                    for kc in range(DC):
                        rm = mm(pb[2 + fi][:, 0:512], wv3[par][:, kc, :],
                                slB[:, kc, t0:t0 + 512], kc == 0, kc == DC - 1)
                    em.waits("s", [rm])
                    crt = act(sc[2][:, t0:t0 + 512], pb[2 + fi][:, 0:512],
                              AF.Sigmoid)
                    pbusy[2 + fi] = crt
                em.waits("gp", [wpar[par], crt])
                twl2 = em.dma("gp", wv3[par][:],
                              wcm_t[m, 1].rearrange("k p c -> p k c"), ch=par)
                for fi, t0 in enumerate((0, 512)):
                    em.waits("te", [twl2, crt])
                    pwait(fi)
                    for kc in range(DC):
                        vm = mm(pb[fi][:, 0:512], wv3[par][:, kc, :],
                                k2b[:, kc, t0:t0 + 512], kc == 0, kc == DC - 1)
                    em.waits("v", [vm, crt])
                    pbusy[fi] = tt(sc[1][:, 0:512], sc[2][:, t0:t0 + 512],
                                   pb[fi][:, 0:512], OP.mult)
                    wpar[par] = tt(x_res[:, m, 4 + t0:4 + t0 + 512],
                                   x_res[:, m, 4 + t0:4 + t0 + 512],
                                   sc[1][:, 0:512], OP.add)
            em.barrier()
            prevs = [None, None]
            for ch in range(DC):
                em.waits("sy", [prevs[ch % 2]])
                rt = em.dma("sy", sc[ch % 2][:], sp_x2[ch])
                em.wait("v", rt)
                prevs[ch % 2] = tt(x_res[:, ch, 4:TH], x_res[:, ch, 4:TH],
                                   sc[ch % 2][:], OP.add)
            em.barrier()

        if upto <= 3:
            fin = em.dma("sy", outT, x_res[:, :, 4:TH])
            em.waits("gp", [fin])
            em.flush()
            return nc

        with contextlib.ExitStack() as ctx4:
            rows = sbuf("rows", [1, T, 6], F32, ctx=ctx4)
            qb = sbuf("q1b", [128, DC, T], BF16, ctx=ctx4)
            tmax = sbuf("tmax", [128, 8, 2], F32, ctx=ctx4)
            qi8 = sbuf("qi8", [128, T], I8, ctx=ctx4)

            with contextlib.ExitStack() as ctx4s:
                wab = [sbuf("wab%d" % i, [128, DC, 128], F32, ctx=ctx4s)
                       for i in range(2)]
                memset(tiny[:, 8:11], 0.0)
                srcs = ([(w12_t[hm, half], 8 + half)
                         for hm in range(HC) for half in range(2)] +
                        [(w3_t[kc], 10) for kc in range(HC)])
                tls = [None, None]
                tls[0] = em.dma("s", wab[0][:],
                                srcs[0][0].rearrange("k p c -> p k c"), ch=4)
                for j, (sap, col) in enumerate(srcs):
                    par = j % 2
                    if j + 1 < len(srcs):
                        tls[(j + 1) % 2] = em.dma(
                            "s", wab[(j + 1) % 2][:],
                            srcs[j + 1][0].rearrange("k p c -> p k c"),
                            ch=4 + (j + 1) % 2)
                    em.wait("s", tls[par])
                    for seg in range(2):
                        a1 = act(sc[0][:],
                                 wab[par][:, seg * 8:(seg + 1) * 8]
                                 .rearrange("p k c -> p (k c)"),
                                 AF.Abs, accum=tiny[:, 12:13])
                        em.wait("v", a1)
                        tt(tiny[:, col:col + 1], tiny[:, col:col + 1],
                           tiny[:, 12:13], OP.add)
                em.barrier()
            t1 = mm(pb[4][0:1, 0:3], c_ocf[:, 0:1], tiny[:, 8:11], True, True)
            em.wait("v", t1)
            ts(trow[0:1, 8:11], pb[4][0:1, 0:3], 1.0 / (D * H), None, OP.mult)
            ts(trow[0:1, 11:14], trow[0:1, 8:11], float(EPS_RMS), None, OP.add)
            ivt = recip(trow[0:1, 14:17], trow[0:1, 11:14])
            em.wait("s", ivt)
            act(trow[0:1, 17:20], trow[0:1, 8:11], AF.Ln, bias=1.0)
            nt = ts(trow[0:1, 20:23], trow[0:1, 8:11], -1.0, 1.0, OP.mult, OP.add)
            em.wait("s", nt)
            lt2 = act(trow[0:1, 23:26], trow[0:1, 20:23], AF.Ln)
            em.wait("v", lt2)
            tt(trow[0:1, 26:29], trow[0:1, 17:20], trow[0:1, 23:26], OP.subtract)
            al = ts(trow[0:1, 26:29], trow[0:1, 26:29], 0.5, None, OP.mult)
            em.waits("te", [ivt, al])
            bi_ = mm(pb[4][:, 0:9], c_o1b[0:1, :], trow[0:1, 8:17], True, True)
            em.wait("v", bi_)
            cp(tiny[:, 16:25], pb[4][:, 0:9])
            em.barrier()

            for th0 in range(0, T, 512):
                memset(rows[0:1, th0:th0 + 512, 0], 0.0)
                memset(lnr[0:1, 0:512, 1], 0.0)
                mmprev = [None, None]
                for ch in range(DC):
                    em.wait("s", mmprev[ch % 2])
                    ctok = cp(scb[ch % 2][:, 0:512],
                              x_res[:, ch, 4 + th0:4 + th0 + 512], eng="s")
                    sqtok = act(scb[2 + ch % 2][:, 0:512], scb[ch % 2][:, 0:512],
                                AF.Square)
                    em.wait("te", ctok)
                    pwait(4)
                    t1 = mm(pb[4][0:1, 0:512], c_ocb[:, 0:1], scb[ch % 2][:, 0:512],
                            True, True)
                    em.wait("te", sqtok)
                    pwait(5)
                    t2 = mm(pb[5][0:1, 0:512], c_ocb[:, 0:1],
                            scb[2 + ch % 2][:, 0:512], True, True)
                    em.waits("v", [t1, t2])
                    mmprev[ch % 2] = t2
                    pbusy[4] = tt(rows[0:1, th0:th0 + 512, 0],
                                  rows[0:1, th0:th0 + 512, 0], pb[4][0:1, 0:512],
                                  OP.add)
                    pbusy[5] = tt(lnr[0:1, 0:512, 1], lnr[0:1, 0:512, 1],
                                  pb[5][0:1, 0:512], OP.add)
                ts(rows[0:1, th0:th0 + 512, 0], rows[0:1, th0:th0 + 512, 0],
                   1.0 / D, None, OP.mult)
                ts(lnr[0:1, 0:512, 1], lnr[0:1, 0:512, 1], 1.0 / D, None,
                   OP.mult)
                tt(lnr[0:1, 0:512, 2], rows[0:1, th0:th0 + 512, 0],
                   rows[0:1, th0:th0 + 512, 0], OP.mult)
                tt(lnr[0:1, 0:512, 2], lnr[0:1, 0:512, 1],
                   lnr[0:1, 0:512, 2], OP.subtract)
                vtok = ts(lnr[0:1, 0:512, 3], lnr[0:1, 0:512, 2],
                          float(EPS_LN), None, OP.add)
                em.wait("s", vtok)
                sq = act(lnr[0:1, 0:512, 3], lnr[0:1, 0:512, 3],
                         AF.Sqrt)
                em.wait("v", sq)
                recip(rows[0:1, th0:th0 + 512, 1], lnr[0:1, 0:512, 3])
                tt(rows[0:1, th0:th0 + 512, 2], rows[0:1, th0:th0 + 512, 1],
                   rows[0:1, th0:th0 + 512, 1], OP.mult)
                tt(rows[0:1, th0:th0 + 512, 2], rows[0:1, th0:th0 + 512, 2],
                   lnr[0:1, 0:512, 2], OP.mult)
                vt2 = ts(lnr[0:1, 0:512, 4], rows[0:1, th0:th0 + 512, 2],
                         float(EPS_RMS), None, OP.add)
                em.wait("s", vt2)
                sq3 = act(lnr[0:1, 0:512, 4], lnr[0:1, 0:512, 4],
                          AF.Sqrt)
                em.wait("v", sq3)
                recip(rows[0:1, th0:th0 + 512, 2], lnr[0:1, 0:512, 4])
            em.barrier()

            memset(tmax[:], 0.0)
            rtk = mxk = None
            em.wait("gp", ("v", em.cnt["v"]))
            for th0 in range(0, T, 512):
                pwait(4)
                bmu = mm(pb[4][:, 0:512], c_o1b[0:1, :], rows[0:1, th0:th0 + 512, 0],
                         True, True)
                pwait(5)
                brv = mm(pb[5][:, 0:512], c_o1b[0:1, :], rows[0:1, th0:th0 + 512, 1],
                         True, True)
                em.waits("v", [bmu, brv])
                for ch in range(DC):
                    tt(sc[0][:, 0:512], x_res[:, ch, 4 + th0:4 + th0 + 512],
                       pb[4][:, 0:512], OP.subtract)
                    xh = tt(sc[0][:, 0:512], sc[0][:, 0:512], pb[5][:, 0:512],
                            OP.mult)
                    pbusy[4] = xh
                    pbusy[5] = xh
                    for br in range(2):
                        gcol = c_g1[:, ch:ch + 1] if br == 0 else c_g2[:, ch:ch + 1]
                        gx = ts(scb[br][:, 0:512], sc[0][:, 0:512], gcol, None,
                                OP.mult)
                        for tb in range(4):
                            tj = th0 // 128 + tb
                            em.waits("te", [gx, rtk, mxk])
                            trt = tpose(pbt[:], scb[br][:, tb * 128:(tb + 1) * 128])
                            em.wait("v", trt)
                            em.wait("v", mxk)
                            rtk = red(sc[5][:, 0:1], pbt[:], OP.max, absval=True)
                            em.wait("gp", rtk)
                            mxk = tt(tmax[:, tj, br:br + 1],
                                     tmax[:, tj, br:br + 1],
                                     sc[5][:, 0:1], OP.max)
            em.barrier()
            mt = None
            for br in range(2):
                for tb in range(8):
                    mt = em.dma("gp", rows[0:1, tb * 128:(tb + 1) * 128, 4 + br],
                                tmax[:, tb, br:br + 1], slow=True)
            em.wait("v", mt)
            for br in range(2):
                tt(rows[0:1, 0:T, 4 + br], rows[0:1, 0:T, 4 + br],
                   rows[0:1, 0:T, 2], OP.mult)
                ts(rows[0:1, 0:T, 4 + br], rows[0:1, 0:T, 4 + br], 1e-4, None,
                   OP.max)
            em.barrier()

            def emit_qbuild(br):
                gc = c_g1 if br == 0 else c_g2
                for th0 in range(0, T, 512):
                    pwait(4)
                    bmu = mm(pb[4][:, 0:512], c_o1b[0:1, :],
                             rows[0:1, th0:th0 + 512, 0], True, True)
                    pwait(5)
                    brv = mm(pb[5][:, 0:512], c_o1b[0:1, :],
                             rows[0:1, th0:th0 + 512, 1], True, True)
                    recip(lnr[0:1, 0:512, 0], rows[0:1, th0:th0 + 512, 4 + br])
                    tt(lnr[0:1, 0:512, 0], lnr[0:1, 0:512, 0],
                       rows[0:1, th0:th0 + 512, 2], OP.mult)
                    sf = ts(lnr[0:1, 0:512, 0], lnr[0:1, 0:512, 0],
                            127.0, None, OP.mult)
                    em.waits("te", [sf])
                    pwait(0)
                    bsf = mm(pb[0][:, 0:512], c_o1b[0:1, :],
                             lnr[0:1, 0:512, 0], True, True)
                    em.waits("v", [bmu, brv, bsf])
                    lastq = None
                    for ch in range(DC):
                        tt(sc[0][:, 0:512], x_res[:, ch, 4 + th0:4 + th0 + 512],
                           pb[4][:, 0:512], OP.subtract)
                        tt(sc[0][:, 0:512], sc[0][:, 0:512], pb[5][:, 0:512],
                           OP.mult)
                        ts(sc[0][:, 0:512], sc[0][:, 0:512], gc[:, ch:ch + 1],
                           None, OP.mult)
                        lastq = tt(sc[0][:, 0:512], sc[0][:, 0:512],
                                   pb[0][:, 0:512], OP.mult)
                        cp(qi8[:, 0:512], sc[0][:, 0:512])
                        cp(qb[:, ch, th0:th0 + 512], qi8[:, 0:512])
                    pbusy[0] = lastq
                    pbusy[4] = lastq
                    pbusy[5] = lastq
                em.barrier()
                ts(rows[0:1, 0:T, 4 + br], rows[0:1, 0:T, 4 + br], 1.0 / 127.0,
                   None, OP.mult)
                ts(rows[0:1, 0:T, 4 + br], rows[0:1, 0:T, 4 + br],
                   trow[0:1, 26 + br:27 + br], None, OP.mult)
                em.barrier()

            def emit_w12pass(br):
                with contextlib.ExitStack() as ctxw:
                    w12s = [sbuf("w12sh%d_%d" % (br, i), [128, DC, 128], F32,
                                 ctx=ctxw) for i in range(2)]
                    w12b = sbuf("w12bh%d" % br, [128, DC, 128], BF16, ctx=ctxw)
                    dsr = sbuf("dsr%d" % br, [128, T], F32, ctx=ctxw)
                    for fi, t0 in enumerate((0, 512)):
                        pwait(4)
                        dd = mm(pb[4][:, 0:512], c_o1b[0:1, :],
                                rows[0:1, t0:t0 + 512, 4 + br], True, True)
                        em.waits("v", [dd])
                        pbusy[4] = cp(dsr[:, t0:t0 + 512], pb[4][:, 0:512])
                    hprev = None
                    mm_prev = None
                    tls = [None, None]
                    tls[0] = em.dma("s", w12s[0][:],
                                    w12_t[0, br].rearrange("k p c -> p k c"),
                                    ch=4)
                    for hm in range(HC):
                        par = hm % 2
                        if hm + 1 < HC:
                            tls[(hm + 1) % 2] = em.dma(
                                "s", w12s[(hm + 1) % 2][:],
                                w12_t[hm + 1, br].rearrange("k p c -> p k c"),
                                ch=4 + (hm + 1) % 2)
                        em.wait("s", tls[par])
                        em.waits("v", [mm_prev])
                        bt = None
                        for seg in range(2):
                            th_ = act(sc[0][:],
                                      w12s[par][:, seg * 8:(seg + 1) * 8]
                                      .rearrange("p k c -> p (k c)"), AF.Tanh,
                                      scale=tiny[:, 22 + br:23 + br])
                            em.wait("v", th_)
                            cp(qi8[:], sc[0][:])
                            bt = cp(w12b[:, seg * 8:(seg + 1) * 8]
                                    .rearrange("p k c -> p (k c)"), qi8[:])
                        em.waits("te", [bt])
                        ht = None
                        for fi, t0 in enumerate((0, 512)):
                            pwait(fi)
                            m1 = None
                            for kc in range(DC):
                                m1 = mm(pb[fi][:, 0:512], w12b[:, kc, :],
                                        qb[:, kc, t0:t0 + 512], kc == 0,
                                        kc == DC - 1)
                            em.waits("v", [m1])
                            lb = tt(sc[0][:, 0:512], pb[fi][:, 0:512],
                                    dsr[:, t0:t0 + 512], OP.mult)
                            pbusy[fi] = lb
                            if br == 0:
                                em.wait("s", lb)
                                sw = act(sc[1][:, 0:512], sc[0][:, 0:512],
                                         AF.Silu)
                                em.waits("v", [sw])
                                ht = cp(scb[2][:, t0:t0 + 512], sc[1][:, 0:512])
                            else:
                                rr = em.dma("sy", scb[3][:, t0:t0 + 512],
                                            sp_s1[hm, :, t0:t0 + 512], ch=7)
                                em.wait("v", rr)
                                ht = tt(scb[2][:, t0:t0 + 512],
                                        sc[0][:, 0:512],
                                        scb[3][:, t0:t0 + 512], OP.mult)
                        mm_prev = ("te", em.cnt["te"])
                        dst = sp_s1 if br == 0 else sp_h
                        em.waits("sy", [hprev, ht])
                        hprev = em.dma("sy", dst[hm], scb[2][:], ch=6)
                        em.wait("v", hprev)
                    em.barrier()

            emit_qbuild(0)
            emit_w12pass(0)
            emit_qbuild(1)
            emit_w12pass(1)

        with contextlib.ExitStack() as ctx4B:
            rowsb = sbuf("rowsb", [1, T, 3], F32, ctx=ctx4B)
            tmaxb = sbuf("tmaxb", [128, 8, 1], F32, ctx=ctx4B)
            qi8 = sbuf("qi8b", [128, T], I8, ctx=ctx4B)

            with contextlib.ExitStack() as ctx4h:
                hrow = sbuf("hrow2", [1, T, 2], F32, ctx=ctx4h)
                memset(tmaxb[:, :, 0], 0.0)
                memset(hrow[0:1, :, 0], 0.0)
                rtk = mxk = None
                em.wait("gp", ("v", em.cnt["v"]))
                for hm in range(HC):
                    em.wait("sy", ("v", em.cnt["v"]))
                    em.wait("sy", ("s", em.cnt["s"]))
                    hr = em.dma("sy", scb[2][:], sp_h[hm])
                    em.wait("s", hr)
                    for fi, t0 in enumerate((0, 512)):
                        em.wait("s", ("te", em.cnt["te"]))
                        hsq = act(scb[3][:, 0:512], scb[2][:, t0:t0 + 512],
                                  AF.Square)
                        em.wait("te", hsq)
                        pwait(4)
                        s1 = mm(pb[4][0:1, 0:512], c_ocb[:, 0:1], scb[3][:, 0:512],
                                True, True)
                        em.waits("v", [s1])
                        pbusy[4] = tt(hrow[0:1, t0:t0 + 512, 0],
                                      hrow[0:1, t0:t0 + 512, 0], pb[4][0:1, 0:512],
                                      OP.add)
                        gx = ts(scb[3][:, 0:512], scb[2][:, t0:t0 + 512],
                                c_g3[:, hm:hm + 1], None, OP.mult)
                        for tb in range(4):
                            tj = t0 // 128 + tb
                            em.waits("te", [gx, rtk, mxk])
                            trt = tpose(pbt[:], scb[3][:, tb * 128:(tb + 1) * 128])
                            em.wait("v", trt)
                            em.wait("v", mxk)
                            rtk = red(sc[5][:, 0:1], pbt[:], OP.max, absval=True)
                            em.wait("gp", rtk)
                            mxk = tt(tmaxb[:, tj, 0:1], tmaxb[:, tj, 0:1],
                                    sc[5][:, 0:1], OP.max)
                ts(hrow[0:1, :, 0], hrow[0:1, :, 0], 1.0 / H, None, OP.mult)
                vh = ts(hrow[0:1, :, 1], hrow[0:1, :, 0], float(EPS_RMS), None,
                        OP.add)
                em.wait("s", vh)
                sqh = act(hrow[0:1, :, 1], hrow[0:1, :, 1], AF.Sqrt)
                em.wait("v", sqh)
                recip(rowsb[0:1, 0:T, 0], hrow[0:1, :, 1])
                em.waits("gp", [mxk])
                for tb in range(8):
                    mt = em.dma("gp", rowsb[0:1, tb * 128:(tb + 1) * 128, 1],
                                tmaxb[:, tb, 0:1], slow=True)
                em.wait("v", mt)
                tt(rowsb[0:1, 0:T, 1], rowsb[0:1, 0:T, 1], rowsb[0:1, 0:T, 0], OP.mult)
                ts(rowsb[0:1, 0:T, 1], rowsb[0:1, 0:T, 1], 1e-4, None, OP.max)
                recip(rowsb[0:1, 0:T, 2], rowsb[0:1, 0:T, 1])
                tt(rowsb[0:1, 0:T, 2], rowsb[0:1, 0:T, 2], rowsb[0:1, 0:T, 0], OP.mult)
                ts(rowsb[0:1, 0:T, 2], rowsb[0:1, 0:T, 2], 127.0, None, OP.mult)
                ts(rowsb[0:1, 0:T, 1], rowsb[0:1, 0:T, 1], 1.0 / 127.0, None, OP.mult)
                ts(rowsb[0:1, 0:T, 1], rowsb[0:1, 0:T, 1],
                   trow[0:1, 28:29], None, OP.mult)
                em.barrier()

            with contextlib.ExitStack() as ctx4x:
                w3s = [sbuf("w3sl%d" % i, [128, DC, 128], F32, ctx=ctx4x)
                       for i in range(2)]
                w3g = [sbuf(f"w3g{i}", [128, DC, 128], BF16, ctx=ctx4x)
                       for i in range(4)]
                q3g = [sbuf(f"q3g{i}", [128, T], BF16, ctx=ctx4x) for i in range(4)]
                d3s = sbuf("d3s", [128, T], F32, ctx=ctx4x)
                pwait(2)
                pwait(3)
                pwait(4)
                pwait(5)
                b1 = mm(pb[2][:, 0:512], c_o1b[0:1, :], rowsb[0:1, 0:512, 2], True,
                        True)
                b2 = mm(pb[3][:, 0:512], c_o1b[0:1, :], rowsb[0:1, 512:1024, 2],
                        True, True)
                d1 = mm(pb[4][:, 0:512], c_o1b[0:1, :], rowsb[0:1, 0:512, 1], True,
                        True)
                d2 = mm(pb[5][:, 0:512], c_o1b[0:1, :], rowsb[0:1, 512:1024, 1],
                        True, True)
                em.waits("v", [b1, b2, d1, d2])
                cp(d3s[:, 0:512], pb[4][:, 0:512])
                pbusy[4] = pbusy[5] = cp(d3s[:, 512:1024], pb[5][:, 0:512])
                tls = [None, None]
                tls[0] = em.dma("s", w3s[0][:],
                                w3_t[0].rearrange("k p c -> p k c"), ch=4)
                for g in range(16):
                    for ki in range(4):
                        kc = g * 4 + ki
                        par = kc % 2
                        if kc + 1 < HC:
                            tls[(kc + 1) % 2] = em.dma(
                                "s", w3s[(kc + 1) % 2][:],
                                w3_t[kc + 1].rearrange("k p c -> p k c"),
                                ch=4 + (kc + 1) % 2)
                        em.wait("s", tls[par])
                        for seg in range(2):
                            th_ = act(sc[0][:],
                                      w3s[par][:, seg * 8:(seg + 1) * 8]
                                      .rearrange("p k c -> p (k c)"), AF.Tanh,
                                      scale=tiny[:, 24:25])
                            em.wait("v", th_)
                            cp(qi8[:], sc[0][:])
                            cp(w3g[ki][:, seg * 8:(seg + 1) * 8]
                                           .rearrange("p k c -> p (k c)"), qi8[:])
                        em.wait("sy", ("v", em.cnt["v"]))
                        hr = em.dma("sy", scb[2][:], sp_h[kc], ch=7)
                        em.wait("v", hr)
                        for fi, t0 in enumerate((0, 512)):
                            ts(sc[1][:, 0:512], scb[2][:, t0:t0 + 512],
                               c_g3[:, kc:kc + 1], None, OP.mult)
                            tt(sc[1][:, 0:512], sc[1][:, 0:512],
                               pb[2 + fi][:, 0:512], OP.mult)
                            cp(qi8[:, 0:512], sc[1][:, 0:512])
                            cp(q3g[ki][:, t0:t0 + 512], qi8[:, 0:512])
                    qtok = ("v", em.cnt["v"])
                    for mp in range(DC):
                        for fi, t0 in enumerate((0, 512)):
                            bidx = (mp * 2 + fi) % 2
                            bank = pb[bidx]
                            em.waits("te", [qtok])
                            pwait(bidx)
                            for ki in range(4):
                                wm = mm(bank[:, 0:512], w3g[ki][:, mp, :],
                                        q3g[ki][:, t0:t0 + 512], ki == 0, ki == 3)
                            em.waits("v", [wm])
                            pbusy[bidx] = tt(sc[3][:, 0:512], bank[:, 0:512],
                                             d3s[:, t0:t0 + 512], OP.mult)
                            tt(x_res[:, mp, 4 + t0:4 + t0 + 512],
                               x_res[:, mp, 4 + t0:4 + t0 + 512], sc[3][:, 0:512],
                               OP.add)
                    em.barrier()

        fin = em.dma("sy", outT, x_res[:, :, 4:TH])
        em.waits("gp", [fin])
        em.flush()
    return nc


def _prep_inputs(inp):
    C = np.ascontiguousarray
    d = {}
    W_in = inp["L0_attn_W_in"]
    d["win_t"] = C(W_in.reshape(73, 128, 16, 128).transpose(0, 2, 3, 1))
    d["wdt_t"] = C(inp["L0_attn_W_dt"].reshape(32, 128, 128).transpose(0, 2, 1))
    d["alog"] = C(inp["L0_attn_A_log"].reshape(32, 128, 16).transpose(1, 0, 2))
    d["bdt"] = C(inp["L0_attn_b_dt"].reshape(32, 128).T)
    d["convw"] = C(inp["L0_attn_conv_w"][:, 0, :].reshape(32, 128, 4)
                   .transpose(1, 0, 2))
    d["convb"] = C(inp["L0_attn_conv_b"].reshape(32, 128).T)
    d["dpar"] = C(inp["L0_attn_D_param"].reshape(32, 128).T)
    d["wout_t"] = C(inp["L0_attn_W_out"].reshape(16, 128, 32, 128)
                    .transpose(0, 2, 3, 1))
    wg = inp["L0_ffn_w_gate"].reshape(16, 128, 64, 128).transpose(2, 0, 1, 3)
    wu = inp["L0_ffn_w_up"].reshape(16, 128, 64, 128).transpose(2, 0, 1, 3)
    d["wgu_t"] = C(np.stack([wg, wu], axis=1))
    d["wdn_t"] = C(inp["L0_ffn_w_down"].reshape(64, 128, 16, 128)
                   .transpose(0, 2, 1, 3))
    d["sgate"] = C(inp["L0_ffn_s_gate"].reshape(64, 128).T)
    d["sup"] = C(inp["L0_ffn_s_up"].reshape(64, 128).T)
    d["sdown"] = C(inp["L0_ffn_s_down"].reshape(16, 128).T)
    vecs = [inp["L1_attn_" + n] for n in
            ["ln1_w", "ln1_b", "ln2_w", "ln2_b", "tm_u", "tm_w", "tm_mix_k",
             "tm_mix_v", "tm_mix_r", "cm_mix_k", "cm_mix_r", "tm_bo"]]
    d["l1v"] = C(np.stack(vecs, 0).reshape(12, 16, 128).transpose(2, 1, 0))
    tile2k = lambda w: w.reshape(16, 128, 16, 128).transpose(0, 2, 3, 1)
    d["wkvr_t"] = C(np.stack([tile2k(inp["L1_attn_tm_Wk"]),
                              tile2k(inp["L1_attn_tm_Wv"]),
                              tile2k(inp["L1_attn_tm_Wr"])], axis=1))
    d["wo_t"] = C(tile2k(inp["L1_attn_tm_Wo"]))
    d["wcm_t"] = C(np.stack([tile2k(inp["L1_attn_cm_Wk"]),
                             tile2k(inp["L1_attn_cm_Wv"]),
                             tile2k(inp["L1_attn_cm_Wr"])], axis=1))
    w1 = inp["L1_ffn_w1"].reshape(64, 128, 16, 128).transpose(0, 2, 3, 1)
    w2 = inp["L1_ffn_w2"].reshape(64, 128, 16, 128).transpose(0, 2, 3, 1)
    d["w12_t"] = C(np.stack([w1, w2], axis=1))
    d["w3_t"] = C(inp["L1_ffn_w3"].reshape(16, 128, 64, 128).transpose(2, 0, 3, 1))
    d["g1i"] = C(inp["L1_ffn_g1"].reshape(16, 128).T)
    d["g2i"] = C(inp["L1_ffn_g2"].reshape(16, 128).T)
    d["g3i"] = C(inp["L1_ffn_g3"].reshape(64, 128).T)
    d["ident"] = np.eye(128, dtype=BF)
    s16 = np.zeros((16, 128), dtype=BF)
    for p in range(128):
        s16[p % 16, p] = 1
    d["sel16"] = s16
    o16 = np.zeros((16, 16, 128), dtype=BF)
    for n in range(16):
        o16[n, n, :] = 1
    d["oh16"] = C(o16)
    slc = np.zeros((128, 16, 128), dtype=BF)
    for n in range(16):
        for tgt in range(128):
            slc[(tgt // 16) * 16 + n, n, tgt] = 1
    d["selc"] = slc
    d["onesc_b"] = np.ones((128, 1), dtype=BF)
    d["onesc_f"] = np.ones((128, 1), dtype=np.float32)
    d["ones1_b"] = np.ones((1, 128), dtype=np.float32)
    return d


def kernel(**inp):
    upto = int(inp.pop("_upto", 99))
    dbg = bool(inp.pop("_dbg", False))
    x = np.asarray(inp["L0_attn_hidden_states"], dtype=np.float32)
    shared = _prep_inputs({k: np.asarray(v) for k, v in inp.items()})
    nc = build_nc(upto=upto, dbg=dbg)
    in_maps = []
    for core in range(NCORE):
        b, q = core // 4, core % 4
        m = dict(shared)
        t0 = q * T
        xp_ = np.zeros((TH, D), np.float32)
        lo = max(t0 - 4, 0)
        xp_[4 - (t0 - lo):] = x[b, lo:t0 + T]
        m["x0c"] = np.ascontiguousarray(
            xp_.T.reshape(16, 128, TH).transpose(1, 0, 2))
        sp = np.zeros((128, 3, 8), np.float32)
        ep = np.zeros((128, 3), np.float32)
        for j in range(3):
            if j < q:
                sp[:, j, b * 4 + j] = 1.0
            else:
                ep[:, j] = 1.0
        m["selp"] = sp
        m["empt"] = ep
        sb_ = np.zeros((128, 8), np.float32)
        if q > 0:
            sb_[:, b * 4 + q - 1] = 1.0
        m["selb"] = sb_
        m["mhalo"] = np.full((128, 1), 1.0 if q > 0 else 0.0, np.float32)
        in_maps.append(m)
    res = run_bass_kernel_spmd(nc, in_maps, core_ids=list(range(NCORE)))
    out = np.empty((2, 4096, D), np.float32)
    for core in range(NCORE):
        b, q = core // 4, core % 4
        o = res.results[core]["outT"]
        out[b, q * T:(q + 1) * T] = o.transpose(1, 0, 2).reshape(D, T).T
    return out
    print("saved", out.shape)
